# version 11
# speedup vs baseline: 1.1100x; 1.0973x over previous
import numpy as np
import ml_dtypes
import concourse.bass as bass
import concourse.mybir as mybir
from concourse.bass import IndirectOffsetOnAxis
from concourse.bass_utils import run_bass_kernel_spmd

F32 = mybir.dt.float32
BF16 = mybir.dt.bfloat16
AF = mybir.ActivationFunctionType
ALU = mybir.AluOpType
AX = mybir.AxisListType
BF = ml_dtypes.bfloat16

P = 128
D = 1024
DC = D // P
NH = 16
DH = 64
NHP = NH // 2
L = 512
LC = L // P
E = 8
FF = 4096
FFC = FF // P
QSB = 512
NEG_A = -10000.0
NEG_B = -9e15
EPS = 1e-5
SCALE = 0.125
I32 = mybir.dt.int32



def _sparse_mask_bool(T):
    pos = np.arange(T)
    dist = pos[None, :] - pos[:, None]
    window = np.abs(dist) <= 64
    num_global = max(1, int(T * 0.1))
    isg = pos < num_global
    return window | isg[:, None] | isg[None, :]


def _causal_mask_bool(T):
    return np.tril(np.ones((T, T), dtype=bool))


def _classify(mask_qk, T, neg):
    mb = mask_qk.T
    nq, nk = T // QSB, T // P
    tiles, index, cls, kb_lists = [], {}, {}, []
    for qsb in range(nq):
        lst = []
        for kb in range(nk):
            blk = mb[kb * P:(kb + 1) * P, qsb * QSB:(qsb + 1) * QSB]
            if not blk.any():
                continue
            lst.append(kb)
            if blk.all():
                cls[(qsb, kb)] = 'clean'
            else:
                key = blk.tobytes()
                if key not in index:
                    index[key] = len(tiles)
                    tiles.append(np.where(blk, 0.0, neg).astype(np.float32))
                cls[(qsb, kb)] = ('tile', index[key])
        kb_lists.append(lst)
    return tiles, cls, kb_lists



class Sem:
    def __init__(self, nc, name):
        self.h = nc.alloc_semaphore(name)
        self.n = 0
        self.name = name

    def bump(self, k):
        self.n += k
        return self.n


class Phase:
    ENGS = ("tensor", "scalar", "vector", "sync", "gpsimd")

    def __init__(self, name):
        self.name = name
        self.ops = {e: [] for e in self.ENGS}

    def op(self, eng, fn, waits=(), inc=None, inc_n=1):
        tgt = inc.bump(inc_n) if inc is not None else None
        waits = tuple((s, v) for (s, v) in waits)

        def _f(e, fn=fn, waits=waits, inc=inc, inc_n=inc_n):
            for s, v in waits:
                e.wait_ge(s.h, v)
            ins = fn(e)
            if inc is not None:
                ins.then_inc(inc.h, inc_n)
        self.ops[eng].append(_f)
        return tgt

    def wait(self, eng, sem, val):
        self.ops[eng].append(lambda e, s=sem, v=val: e.wait_ge(s.h, v))

    def realize(self, nc):
        with nc.Block(self.name) as blk:
            for eng in self.ENGS:
                lst = self.ops[eng]
                if not lst:
                    continue
                getattr(blk, eng)(lambda e, lst=lst: [f(e) for f in lst])


def dma(ph, eng, sem, dst, src, waits=(), **kw):
    sem.eng = eng
    return ph.op(eng, lambda e: e.dma_start(dst, src, **kw), waits=waits,
                 inc=sem, inc_n=16)



class Builder:
    def __init__(self, T=4096, dbg=False):
        self.dbg = dbg
        self.T = T
        self.NT = T // P
        self.NQ = T // QSB
        nc = bass.Bass("TRN2", target_bir_lowering=False)
        self.nc = nc
        self.sem = {k: Sem(nc, k) for k in ("sp", "gp", "pe", "ac", "dv")}
        self._semcache = {}

        ext = {}
        def inp(name, shape):
            ext[name] = nc.dram_tensor(name, list(shape), F32, kind="ExternalInput")
            return ext[name]
        self.x = inp("x", (T, D))
        self.noise = inp("noise", (T, E))
        for nm, shp in [
            ("aWq", (D, D)), ("abq", (D,)), ("aWk", (D, D)), ("abk", (D,)),
            ("aWv", (D, D)), ("abv", (D,)), ("aWo", (D, D)), ("abo", (D,)),
            ("fWdkv", (D, L)), ("fbdkv", (L,)), ("fWuk", (L, D)), ("fbuk", (D,)),
            ("fWuv", (L, D)), ("fbuv", (D,)), ("fWdq", (D, L)), ("fbdq", (L,)),
            ("fWuq", (L, D)), ("fbuq", (D,)), ("fWo", (D, D)), ("fbo", (D,)),
            ("g1", (D,)), ("be1", (D,)), ("g2", (D,)), ("be2", (D,)),
            ("Wg", (D, E)), ("bg", (E,)), ("Wn", (D, E)), ("bn", (E,)),
            ("W1t", (E, P, DC * FF)), ("b1p", (E, P, FFC)), ("W2t", (E, P, FFC * D)), ("b2", (E, D)),
        ]:
            inp(nm, shp)
        self.ext = ext
        self.y = nc.dram_tensor("y", [T, D], F32, kind="ExternalOutput")

        T_ = self.T
        dt_ = lambda n, s, d: nc.dram_tensor(
            n, list(s), d, kind="ExternalOutput" if self.dbg else "Internal")
        self.ln0bf = dt_("ln0bf", (T_, D), BF16)
        self.ln1bf = dt_("ln1bf", (T_, D), BF16)
        self.x1f32 = dt_("x1f32", (T_, D), F32)
        self.x1ff32 = dt_("x1ff32", (T_, D), F32)
        self.x1fbf = dt_("x1fbf", (T_, D), BF16)
        self.x1fT_hbm = dt_("x1fT_hbm", (DC, P, T_), BF16)
        self.qt_hbm = dt_("qt_hbm", (NHP, P, T_), BF16)
        self.kt_hbm = dt_("kt_hbm", (NHP, P, T_), BF16)
        self.v_hbm = dt_("v_hbm", (NHP, P, 2 * 65 * self.NT), BF16)
        self.kt2_hbm = dt_("kt2_hbm", (NHP, P, T_), BF16)
        self.qt2_hbm = dt_("qt2_hbm", (NHP, P, T_), BF16)
        self.v2_hbm = dt_("v2_hbm", (NHP, P, 2 * 65 * self.NT), BF16)
        self.moe_hbm = dt_("moe_hbm", (T_, D), F32)
        self.NCHK = (2 * T_ + 8 * 511 + 511) // 512
        self.moe2_hbm = dt_("moe2_hbm", (2 * T_ + 2048, D), F32)
        self.aux2_hbm = dt_("aux2_hbm", (T_, 2), F32)
        self.gb_hbm = dt_("gb_hbm", (4 * P, D), BF16)
        self.idxg_hbm = dt_("idxg_hbm", (self.NCHK * 512,), F32)
        self.idxs_hbm = dt_("idxs_hbm", (self.NCHK * 512,), F32)
        self.w1bf = dt_("w1bf", (E, P, DC * FF), BF16)
        self.w2bf = dt_("w2bf", (E, P, FFC * D), BF16)

        self.ident_bf = nc.inline_tensor(np.eye(P, dtype=BF), "ident_bf")
        self.ident_f32 = nc.inline_tensor(np.eye(P, dtype=np.float32), "ident_f32")
        self.ones_bf = nc.inline_tensor(np.ones((1, P), dtype=BF), "ones_bf")
        self.ones_f32 = nc.inline_tensor(np.ones((1, P), dtype=np.float32), "ones_f32")
        self.ltu_bf = nc.inline_tensor(np.triu(np.ones((P, P), dtype=BF), 1), "ltu_bf")
        self.iota_p = nc.inline_tensor(np.arange(P, dtype=np.float32).reshape(P, 1),
                                       "iota_p")
        self.iota8 = nc.inline_tensor(
            np.tile(np.arange(E, dtype=np.float32), (P, 1)), "iota8")
        self.cvals = nc.inline_tensor(
            (512.0 * np.arange(self.NCHK, dtype=np.float32)).reshape(1, self.NCHK),
            "cvals")
        self.padg_c = nc.inline_tensor(np.tile(
            np.arange(P, dtype=np.float32).reshape(P, 1),
            (1, self.NCHK * 4)), "padg_c")
        self.pads_c = nc.inline_tensor(
            (2.0 * T_ + (np.arange(P * self.NCHK * 4) % 2048))
            .astype(np.float32).reshape(P, self.NCHK * 4), "pads_c")
        self.iota_pr = nc.inline_tensor(
            np.tile(np.arange(P, dtype=np.float32).reshape(P, 1),
                    (1, self.NCHK)), "iota_pr")
        tA, cA, kA = _classify(_sparse_mask_bool(T_), T_, NEG_A)
        tB, cB, kB = _classify(_causal_mask_bool(T_), T_, NEG_B)
        self.maskA = (nc.inline_tensor(np.stack(tA).reshape(len(tA) * P, QSB), "maskA")
                      if tA else None)
        self.maskB = (nc.inline_tensor(np.stack(tB).reshape(len(tB) * P, QSB), "maskB")
                      if tB else None)
        self.nmA, self.clsA, self.kbA = len(tA), cA, kA
        self.nmB, self.clsB, self.kbB = len(tB), cB, kB

        A = nc.alloc_sbuf_tensor
        self.sb_ident = A("sb_ident", [P, P], BF16)
        self.sb_ident32 = A("sb_ident32", [P, P], F32)
        self.sb_ones = A("sb_ones", [1, P], BF16)
        self.sb_ones32 = A("sb_ones32", [1, P], F32)
        self.b_bqT = A("b_bqT", [P, DC], F32)
        self.b_bkT = A("b_bkT", [P, DC], F32)
        self.b_bdkvT = A("b_bdkvT", [P, LC], F32)
        self.b_bdqT = A("b_bdqT", [P, LC], F32)
        self.b_bukT = A("b_bukT", [P, DC], F32)
        self.b_buqT = A("b_buqT", [P, DC], F32)
        self.r_bv = A("r_bv", [1, D], BF16)
        self.r_bo = A("r_bo", [1, D], BF16)
        self.r_buv = A("r_buv", [1, D], BF16)
        self.r_fbo = A("r_fbo", [1, D], BF16)
        self.r_bg = A("r_bg", [1, E], BF16)
        self.r_bn = A("r_bn", [1, E], BF16)
        self.recipZ = A("recipZ", [P, self.NT], F32)
        self.idxW_i = A("idxW_i", [P, (2 * self.T + 8 * 511 + 511) // 512], I32)
        self._idxWf_sb = A("idxWf_p", [P, (2 * self.T + 8 * 511 + 511) // 512], F32)
        self.idxE_i = A("idxE_i", [1, (2 * self.T + 8 * 511 + 511) // 512], I32)
        self.cst = A("cst", [P, 2], F32)
        self._pc_jobs = None
        self._alive = []

    def _preconvert_jobs(self):
        for e in range(E):
            for g in range(DC):
                src = self.ext["W1t"].ap()[e, :, g * FF:(g + 1) * FF]
                dst = self.w1bf.ap()[e, :, g * FF:(g + 1) * FF]
                yield (src, dst, [P, FF])
            for g in range(E):
                src = self.ext["W2t"].ap()[e, :, g * FF:(g + 1) * FF]
                dst = self.w2bf.ap()[e, :, g * FF:(g + 1) * FF]
                yield (src, dst, [P, FF])

    def emit_preconvert(self, ph, count):
        if self._pc_jobs is None:
            self._pc_jobs = self._preconvert_jobs()
            self._pc_store_tgt = [0, 0]
            self._pc_i = 0
        gp = self.sem["gp"]
        for _ in range(count):
            job = next(self._pc_jobs, None)
            if job is None:
                return
            src, dst, shp = job
            i = self._pc_i
            st = self.pcs[i % 2][:, 0:shp[1]]
            if len(src.shape) == 3:
                st = st.rearrange("p (a n) -> p a n", a=src.shape[1])
            ld = dma(ph, "gpsimd", gp, st, src, waits=[(gp, gp.n)])
            dma(ph, "gpsimd", gp, dst, st, waits=[(gp, ld)])
            self._pc_i = i + 1

    def build(self):
        from contextlib import ExitStack
        self.phase_setup()
        self.scP = ExitStack()
        self.pcs = [self.scP.enter_context(
            self.nc.sbuf_tensor(f"pcs{i}", [P, FF], BF16)) for i in range(2)]
        with ExitStack() as scA:
            self.scA = scA
            self.phase_ln0()
            self.phase_qkvA()
        with ExitStack() as scB1:
            self.ctxT = scB1.enter_context(
                self.nc.sbuf_tensor("ctxT", [P, NHP * self.T], BF16))
            with ExitStack() as sc:
                self.phase_attn(sc, "A")
            with ExitStack() as sc:
                self.phase_woA(sc)
        with ExitStack() as scC:
            self.ckvT = scC.enter_context(
                self.nc.sbuf_tensor("ckvT", [P, LC * self.T], BF16))
            self.cqT = scC.enter_context(
                self.nc.sbuf_tensor("cqT", [P, LC * self.T], BF16))
            with ExitStack() as sc:
                self.phase_dkv(sc)
            with ExitStack() as sc:
                self.phase_uproj(sc)
        with ExitStack() as scB2:
            self.ctxT = scB2.enter_context(
                self.nc.sbuf_tensor("ctxT2", [P, NHP * self.T], BF16))
            with ExitStack() as sc:
                self.phase_attn(sc, "B")
            with ExitStack() as sc:
                self.phase_woB(sc)
        with ExitStack() as sc:
            self.phase_x1fT(sc)
        self.scP.close()
        with ExitStack() as sc:
            self.phase_gate(sc)
        with ExitStack() as sc:
            self.phase_moe_sparse(sc)
        with ExitStack() as sc:
            self.phase_final(sc)
        self.finish()
        return self.nc

    def salloc(self, stack, name, shape, dtype):
        return stack.enter_context(self.nc.sbuf_tensor(name, shape, dtype))

    def finish(self):
        ph = Phase("finish")
        for sem in list(self.sem.values()) + list(self._semcache.values()):
            if sem.n > 0 and getattr(sem, "eng", None):
                ph.wait(sem.eng, sem, sem.n)
        ph.realize(self.nc)

    def S(self, name):
        if name not in self._semcache:
            self._semcache[name] = Sem(self.nc, name)
        return self._semcache[name]

    def alloc(self, name, shape, dtype):
        t = self.nc.alloc_sbuf_tensor(name, shape, dtype)
        self._alive.append(t)
        return t

    def phase_setup(self):
        nc, sem = self.nc, self.sem
        sp, gp, pe, ac, dv = (sem[k] for k in ("sp", "gp", "pe", "ac", "dv"))
        ph = Phase("setup")
        ph.op("vector", lambda v: v.memset(self.cst[:, 0:1], EPS), inc=dv)
        ph.op("vector", lambda v: v.memset(self.cst[:, 1:2], -1.0), inc=dv)
        dma(ph, "sync", sp, self.sb_ident[:], self.ident_bf.ap())
        dma(ph, "sync", sp, self.sb_ident32[:], self.ident_f32.ap())
        dma(ph, "sync", sp, self.sb_ones[:], self.ones_bf.ap())
        dma(ph, "sync", sp, self.sb_ones32[:], self.ones_f32.ap())
        for dst, src in [(self.r_bv, "abv"), (self.r_bo, "abo"), (self.r_buv, "fbuv"),
                         (self.r_fbo, "fbo"), (self.r_bg, "bg"), (self.r_bn, "bn")]:
            dma(ph, "gpsimd", gp, dst[:], self.ext[src].ap().rearrange("(o d) -> o d", o=1))
        from contextlib import ExitStack
        self._scS = ExitStack()
        stage = self.salloc(self._scS, "bias_stage", [P, P], F32)
        rows = self.salloc(self._scS, "grows", [1, 4 * D], F32)
        gbc = self.salloc(self._scS, "gbc_tmp", [P, D], BF16)
        for i, nm in enumerate(("g1", "be1", "g2", "be2")):
            dma(ph, "sync", sp, rows[:, i * D:(i + 1) * D],
                self.ext[nm].ap().rearrange("(o d) -> o d", o=1))
        jobs = [(self.b_bqT, "abq", DC), (self.b_bkT, "abk", DC),
                (self.b_bdkvT, "fbdkv", LC), (self.b_bdqT, "fbdq", LC),
                (self.b_bukT, "fbuk", DC), (self.b_buqT, "fbuq", DC)]

        with nc.psum_tensor("ps_setup", [P, P], F32) as pst, \
             nc.psum_tensor("ps_bc", [P, 512], F32) as psb:
            tp_tgt = 0
            for dst, src, n in jobs:
                if isinstance(src, tuple):
                    srcap = self.ext[src[0]].ap()[src[1]].rearrange("(c p) -> c p", p=P)
                else:
                    srcap = self.ext[src].ap().rearrange("(c p) -> c p", p=P)
                tgt = dma(ph, "sync", sp, stage[0:n, :], srcap)
                ph.ops["sync"].insert(len(ph.ops["sync"]) - 1,
                    (lambda e, s_=pe, v=tp_tgt: e.wait_ge(s_.h, v)))
                tp_tgt = ph.op("tensor", lambda t, n=n: t.transpose(
                    pst[:, 0:n], stage[0:n, :], self.sb_ident32[0:n, 0:n]),
                    waits=[(sp, tgt), (ac, ac.n)], inc=pe)
                if dst is None:
                    e_ = src[1]
                    out = self.b_b1T[:, e_ * FFC:(e_ + 1) * FFC]
                else:
                    out = dst[:, 0:n]
                ph.op("scalar", lambda s, out=out, n=n: s.copy(out, pst[:, 0:n]),
                      waits=[(pe, pe.n)], inc=ac)
            grow_tgt = sp.n
            for i in range(4):
                for h2 in range(2):
                    ph.op("tensor", lambda t, i=i, h2=h2: t.matmul(
                        psb[:, 0:512], self.sb_ones32[:],
                        rows[:, i * D + h2 * 512: i * D + (h2 + 1) * 512],
                        start=True, stop=True),
                        waits=[(sp, grow_tgt), (dv, dv.n), (sp, sp.n)], inc=pe)
                    ph.op("vector", lambda v, h2=h2: v.tensor_copy(
                        gbc[:, h2 * 512:(h2 + 1) * 512], psb[:, 0:512]),
                        waits=[(pe, pe.n)], inc=dv)
                dma(ph, "sync", sp, self.gb_hbm.ap()[i * P:(i + 1) * P, :],
                    gbc[:], waits=[(dv, dv.n)])
            ph.wait("sync", sp, sp.n)
            ph.wait("gpsimd", gp, gp.n)
            ph.realize(nc)
        self._scS.close()

    def phase_ln0(self):
        nc, sem = self.nc, self.sem
        sp, gp, pe, ac, dv = (sem[k] for k in ("sp", "gp", "pe", "ac", "dv"))
        self.ln0T = self.salloc(self.scA, "ln0T", [P, DC * self.T], BF16)
        from contextlib import ExitStack
        self.sc0 = self.scA.enter_context(ExitStack())
        sc0 = self.sc0
        xb = [self.salloc(sc0, f"xb{i}", [P, D], F32) for i in range(3)]
        sq = self.salloc(sc0, "sq0", [P, D], F32)
        lnb = [self.salloc(sc0, f"lnb{i}", [P, D], BF16) for i in range(2)]
        stat = [self.salloc(sc0, f"stat{i}", [P, 8], F32) for i in range(2)]
        ph = Phase("ln0")
        xsem = [self.S(f"xb{j}") for j in range(3)]
        lnsem = [self.S(f"lnst{j}") for j in range(2)]
        chain_end, ln_store = {}, {}
        for i in range(self.NT):
            w = [(dv, chain_end[i - 3])] if i >= 3 else []
            x_tgt = dma(ph, "sync", xsem[i % 3], xb[i % 3][:],
                        self.x.ap()[i * P:(i + 1) * P, :], waits=w)
            self._ln_tile(ph, src=xb[i % 3][:], pre_wait=[(xsem[i % 3], x_tgt)],
                          sq=sq, stat=stat[i % 2], out=lnb[i % 2][:],
                          out_guard=(lnsem[i % 2], ln_store.get(i - 2, 0)))
            chain_end[i] = dv.n
            ln_store[i] = dma(ph, "sync", lnsem[i % 2],
                              self.ln0bf.ap()[i * P:(i + 1) * P, :], lnb[i % 2][:],
                              waits=[(dv, chain_end[i])])
        ph.wait("sync", lnsem[0], lnsem[0].n)
        ph.wait("sync", lnsem[1], lnsem[1].n)
        for c in range(DC):
            dma(ph, "sync", sp, self.ln0T[:, c * self.T:(c + 1) * self.T],
                self.ln0bf.ap()[:, c * P:(c + 1) * P], transpose=True)
        self.ln0T_ready = sp.n
        self.emit_preconvert(ph, 20)
        ph.realize(nc)
        self.sc0.close()

    def _ln_tile(self, ph, src, pre_wait, sq, stat, out, out_guard,
                 add=None, gamma=None, beta=None, out2=None, res=None):
        sem = self.sem
        pe, ac, dv = sem["pe"], sem["ac"], sem["dv"]
        ph.op("scalar", lambda s: s.activation(
            sq[:], src, AF.Square, accum_out=stat[:, 1:2]),
            waits=pre_wait + [(dv, dv.n)], inc=ac)
        ph.op("vector", lambda v: v.reduce_sum(stat[:, 0:1], src, axis=AX.X),
              waits=pre_wait + [(ac, ac.n)], inc=dv)
        ph.op("vector", lambda v: v.tensor_scalar(
            stat[:, 2:3], stat[:, 0:1], 1.0 / D, None, ALU.mult),
            waits=[(dv, dv.n), (ac, ac.n)], inc=dv)
        ph.op("vector", lambda v: v.tensor_tensor(
            stat[:, 3:4], stat[:, 2:3], stat[:, 2:3], ALU.mult),
            waits=[(dv, dv.n)], inc=dv)
        ph.op("vector", lambda v: v.tensor_scalar(
            stat[:, 4:5], stat[:, 1:2], 1.0 / D, None, ALU.mult),
            waits=[(dv, dv.n)], inc=dv)
        ph.op("vector", lambda v: v.tensor_sub(
            stat[:, 4:5], stat[:, 4:5], stat[:, 3:4]), waits=[(dv, dv.n)], inc=dv)
        ph.op("scalar", lambda s: s.activation(
            stat[:, 5:6], stat[:, 4:5], AF.Sqrt, bias=self.cst[:, 0:1]),
            waits=[(dv, dv.n)], inc=ac)
        ph.op("vector", lambda v: v.reciprocal(stat[:, 6:7], stat[:, 5:6]),
              waits=[(ac, ac.n)], inc=dv)
        ph.op("vector", lambda v: v.tensor_scalar(
            stat[:, 7:8], stat[:, 2:3], stat[:, 6:7], -1.0, ALU.mult, ALU.mult),
            waits=[(dv, dv.n)], inc=dv)
        ph.op("vector", lambda v: v.tensor_scalar(
            out, src, stat[:, 6:7], stat[:, 7:8], ALU.mult, ALU.add),
            waits=[(dv, dv.n), out_guard], inc=dv)



    def _emit_wT(self, ph, sc, wname, wt, pst, stage, stgsem):
        pe, ac = self.sem["pe"], self.sem["ac"]
        tp_tgt = {}
        for r in range(DC):
            w = [(pe, tp_tgt[r - 2])] if r >= 2 else []
            ld = dma(ph, "gpsimd", stgsem[r % 2], stage[r % 2][:],
                     self.ext[wname].ap()[r * P:(r + 1) * P, :], waits=w)
            for c in range(DC):
                ph.op("tensor", lambda t, r=r, c=c: t.transpose(
                    pst[c % 2][0:P, 0:P], stage[r % 2][:, c * P:(c + 1) * P],
                    self.sb_ident[:]),
                    waits=[(stgsem[r % 2], ld), (ac, max(0, ac.n - 1))], inc=pe)
                ph.op("scalar", lambda s, r=r, c=c: s.copy(
                    wt[:, D * c + P * r: D * c + P * (r + 1)], pst[c % 2][0:P, 0:P]),
                    waits=[(pe, pe.n)], inc=ac)
            tp_tgt[r] = pe.n

    def phase_qkvA(self):
        from contextlib import ExitStack
        nc = self.nc
        sp, gp, pe, ac, dv = (self.sem[k] for k in ("sp", "gp", "pe", "ac", "dv"))
        NT, T_ = self.NT, self.T
        for wname, kind, bias, outh in [
            ("aWq", "feat", self.b_bqT, self.qt_hbm),
            ("aWk", "feat", self.b_bkT, self.kt_hbm),
            ("aWv", "tok", None, self.v_hbm),
        ]:
            with ExitStack() as sc:
                wt = self.salloc(sc, "wt_" + wname, [P, DC * D], BF16)
                stage = [self.salloc(sc, f"stg{i}_" + wname, [P, D], BF16)
                         for i in range(2)]
                stgsem = [self.S(f"stg{i}_{wname}") for i in range(2)]
                ob = [self.salloc(sc, f"ob{i}_" + wname,
                                  [P, T_ if kind == "feat" else 2 * 65 * NT], BF16)
                      for i in range(2)]
                obsem = [self.S(f"ob{i}_{wname}") for i in range(2)]
                ph = Phase("qkvA_" + wname)
                with nc.psum_tensor("pst0" + wname, [P, P], BF16) as p0, \
                     nc.psum_tensor("pst1" + wname, [P, P], BF16) as p1, \
                     nc.psum_tensor("psq0" + wname, [P, 512], F32) as q0, \
                     nc.psum_tensor("psq1" + wname, [P, 512], F32) as q1:
                    self._emit_wT(ph, sc, wname, wt, [p0, p1], stage, stgsem)
                    wT_done = self.sem["ac"].n
                    psq = [q0, q1]
                    ev_tgt = {}
                    st_tgt = {}
                    for hp in range(NHP):
                        if kind == "feat":
                            for nb in range(T_ // 512):
                                gi = hp * (T_ // 512) + nb
                                for ic in range(DC):
                                    wl = [(ac, wT_done), (sp, self.ln0T_ready)] \
                                        if gi == 0 and ic == 0 else []
                                    if ic == 0 and gi >= 2:
                                        wl = wl + [(ac, ev_tgt[gi - 2])]
                                    ph.op("tensor", lambda t, hp=hp, nb=nb, ic=ic,
                                          g=gi: t.matmul(
                                        psq[g % 2][:, 0:512],
                                        wt[:, D * ic + P * hp: D * ic + P * (hp + 1)],
                                        self.ln0T[:, T_ * ic + 512 * nb:
                                                  T_ * ic + 512 * (nb + 1)],
                                        start=(ic == 0), stop=(ic == DC - 1)),
                                        waits=wl, inc=(pe if ic == DC - 1 else None))
                                mmt = pe.n
                                wl = [(pe, mmt)]
                                if hp >= 2 and nb == 0:
                                    wl.append((obsem[hp % 2], st_tgt[hp - 2]))
                                ev_tgt[gi] = ph.op("scalar", lambda s, hp=hp, nb=nb,
                                                   g=gi: s.activation(
                                    ob[hp % 2][:, 512 * nb:512 * (nb + 1)],
                                    psq[g % 2][:, 0:512], AF.Identity,
                                    bias=bias[:, hp:hp + 1]),
                                    waits=wl, inc=ac)
                            st_tgt[hp] = dma(ph, "sync", obsem[hp % 2],
                                             outh.ap()[hp], ob[hp % 2][:],
                                             waits=[(ac, ev_tgt[hp * (T_ // 512)
                                                                + T_ // 512 - 1])])
                        else:
                            for tb in range(NT):
                                gi = hp * NT + tb
                                for ic in range(DC):
                                    wl = [(ac, wT_done), (sp, self.ln0T_ready)] \
                                        if gi == 0 and ic == 0 else []
                                    if ic == 0 and gi >= 2:
                                        wl = wl + [(ac, ev_tgt[gi - 2])]
                                    ph.op("tensor", lambda t, hp=hp, tb=tb, ic=ic,
                                          g=gi: t.matmul(
                                        psq[g % 2][:, 0:P],
                                        self.ln0T[:, T_ * ic + P * tb:
                                                  T_ * ic + P * (tb + 1)],
                                        wt[:, D * ic + P * hp: D * ic + P * (hp + 1)],
                                        start=(ic == 0), stop=False),
                                        waits=wl, inc=None)
                                ph.op("tensor", lambda t, hp=hp, g=gi: t.matmul(
                                    psq[g % 2][:, 0:P], self.sb_ones[:],
                                    self.r_bv[0:1, P * hp:P * (hp + 1)],
                                    start=False, stop=True), inc=pe)
                                mmt = pe.n
                                wl = [(pe, mmt)]
                                if hp >= 2 and tb == 0:
                                    wl.append((obsem[hp % 2], st_tgt[hp - 2]))
                                for h2 in range(2):
                                    col = 65 * NT * h2 + 65 * tb
                                    ev_tgt[gi] = ph.op(
                                        "scalar", lambda s, col=col, h2=h2, g=gi:
                                        s.copy(ob[hp % 2][:, col:col + 64]
                                               if False else
                                               ob[(g // NT) % 2][:, col:col + 64],
                                               psq[g % 2][:, 64 * h2:64 * (h2 + 1)]),
                                        waits=wl if h2 == 0 else [], inc=ac)
                            onescol = ph.op("vector", lambda v, hp=hp: v.memset(
                                ob[hp % 2][:, 0:2 * 65 * NT].rearrange(
                                    "p (t c) -> p t c", c=65)[:, :, 64:65], 1.0),
                                waits=[(ac, ev_tgt[hp * NT + NT - 1])], inc=dv)
                            st_tgt[hp] = dma(ph, "sync", obsem[hp % 2],
                                             outh.ap()[hp], ob[hp % 2][:],
                                             waits=[(dv, onescol)])
                    for j in range(2):
                        ph.wait("sync", obsem[j], obsem[j].n)
                    self.emit_preconvert(ph, 10)
                    ph.realize(nc)

    def phase_attn(self, sc, kind):
        nc = self.nc
        sp, gp, pe, ac, dv = (self.sem[k] for k in ("sp", "gp", "pe", "ac", "dv"))
        NT, T_, NQ = self.NT, self.T, self.NQ
        if kind == "A":
            qth, kth, vh = self.qt_hbm, self.kt_hbm, self.v_hbm
            nm, cls, kbl, maskh = self.nmA, self.clsA, self.kbA, self.maskA
        else:
            qth, kth, vh = self.qt2_hbm, self.kt2_hbm, self.v2_hbm
            nm, cls, kbl, maskh = self.nmB, self.clsB, self.kbB, self.maskB
        mask_sb = self.salloc(sc, f"masksb{kind}", [P, max(nm, 1) * QSB], F32)
        qb = [self.salloc(sc, f"at_q{i}{kind}", [P, T_], BF16) for i in range(2)]
        kb = [self.salloc(sc, f"at_k{i}{kind}", [P, T_], BF16) for i in range(2)]
        vb = [self.salloc(sc, f"at_v{i}{kind}", [P, 2 * 65 * NT], BF16) for i in range(2)]
        qkvsem = [self.S(f"at{kind}{i}") for i in range(2)]
        probs = [self.salloc(sc, f"at_p{i}{kind}", [P, QSB], BF16) for i in range(3)]
        mtmp = [self.salloc(sc, f"at_m{i}{kind}", [P, QSB], F32) for i in range(2)]
        recip2 = [self.salloc(sc, f"at_recip{i}" + kind, [1, QSB], F32)
                  for i in range(2)]
        bcast = self.salloc(sc, "at_bc" + kind, [64, QSB], F32)
        ph = Phase("attn" + kind)
        for mi in range(nm):
            dma(ph, "sync", sp, mask_sb[:, mi * QSB:(mi + 1) * QSB],
                maskh.ap()[mi * P:(mi + 1) * P, :])
        mask_rdy = sp.n
        ph.wait("sync", sp, mask_rdy)
        with nc.psum_tensor("ps_s0" + kind, [P, 512], F32) as s0, \
             nc.psum_tensor("ps_s1" + kind, [P, 512], F32) as s1, \
             nc.psum_tensor("ps_s2" + kind, [P, 512], F32) as s2, \
             nc.psum_tensor("ps_c0" + kind, [P, 512], F32) as c0, \
             nc.psum_tensor("ps_c1" + kind, [P, 512], F32) as c1, \
             nc.psum_tensor("ps_b" + kind, [P, 512], F32) as pb:
            ps_s, ps_c = [s0, s1, s2], [c0, c1]
            ld_tgt, hp_last_use = {}, {}

            def plan_load(hp):
                w = [(pe, hp_last_use[hp - 2])] if hp >= 2 else []
                dma(ph, "sync", qkvsem[hp % 2], qb[hp % 2][:], qth.ap()[hp], waits=w)
                dma(ph, "sync", qkvsem[hp % 2], kb[hp % 2][:], kth.ap()[hp])
                ld_tgt[hp] = dma(ph, "sync", qkvsem[hp % 2], vb[hp % 2][:],
                                 vh.ap()[hp])
            plan_load(0)
            plan_load(1)
            ji, cseq = 0, 0
            bank_free, mtmp_free, ctx_free = {}, {}, {}
            LAG = 2
            pending_norm = []

            norm_state = {"bt": {}, "g": 0}

            def plan_norm(cps, hp, hrow, qsb, cmt):
                def go():
                    g = norm_state["g"]; norm_state["g"] = g + 1
                    rb = recip2[g % 2]
                    wl1 = [(pe, cmt)]
                    if g - 2 in norm_state["bt"]:
                        wl1.append((pe, norm_state["bt"][g - 2]))
                    l1 = ph.op("scalar", lambda s, cps=cps, rb=rb: s.activation(
                        rb[:], cps[64:65, 0:512], AF.Ln), waits=wl1, inc=ac)
                    l2 = ph.op("scalar", lambda s, rb=rb: s.activation(
                        rb[:], rb[:], AF.Exp, scale=self.cst[0:1, 1:2]),
                        waits=[(ac, l1)], inc=ac)
                    wlb = [(ac, l2)]
                    if g - 1 in norm_state["bt"]:
                        pass
                    bt = ph.op("tensor", lambda t, rb=rb: t.matmul(
                        pb[0:64, 0:512], self.sb_ones32[0:1, 0:64], rb[:],
                        start=True, stop=True),
                        waits=wlb + [(dv, norm_state.get("ct_prev", 0))], inc=pe)
                    norm_state["bt"][g] = bt
                    ct = ph.op("vector", lambda v: v.tensor_copy(
                        bcast[:], pb[0:64, 0:512]),
                        waits=[(pe, bt), (dv, dv.n)], inc=dv)
                    norm_state["ct_prev"] = ct
                    ev = ph.op("vector", lambda v, cps=cps, hp=hp, hrow=hrow,
                               qsb=qsb: v.tensor_mul(
                        self.ctxT[hrow:hrow + 64,
                                  T_ * hp + QSB * qsb:T_ * hp + QSB * (qsb + 1)],
                        cps[0:64, 0:512], bcast[:]),
                        waits=[(dv, ct)], inc=dv)
                    return ev
                return go

            for hp in range(NHP):
                for h2 in range(2):
                    hrow = 64 * h2
                    for qsb in range(NQ):
                        lst = kbl[qsb]
                        cps = ps_c[cseq % 2]
                        nk = len(lst)
                        pws = {}

                        def plan_S(j):
                            nonlocal ji
                            kbi = lst[j]
                            sps = ps_s[ji % 3]
                            wl = [(qkvsem[hp % 2], ld_tgt[hp])]
                            if ji % 3 in bank_free:
                                wl.append(bank_free[ji % 3])
                            ph.op("tensor", lambda t, hp=hp, hrow=hrow, qsb=qsb,
                                  kbi=kbi, sps=sps: t.matmul(
                                sps[:, 0:512],
                                kb[hp % 2][hrow:hrow + 64, P * kbi:P * (kbi + 1)],
                                qb[hp % 2][hrow:hrow + 64,
                                           QSB * qsb:QSB * (qsb + 1)],
                                start=True, stop=True), waits=wl, inc=pe)
                            smt = pe.n
                            c = cls[(qsb, kbi)]
                            if c == 'clean':
                                et = ph.op("scalar", lambda s, sps=sps, ji=ji:
                                           s.activation(probs[ji % 3][:],
                                                        sps[:, 0:512], AF.Exp,
                                                        scale=SCALE),
                                           waits=[(pe, smt)], inc=ac)
                                bank_free[ji % 3] = (ac, et)
                                pws[j] = (ac, et)
                            else:
                                mi = c[1]
                                wv = [(pe, smt), (dv, dv.n)]
                                if ji % 2 in mtmp_free:
                                    wv.append(mtmp_free[ji % 2])
                                vt = ph.op("vector", lambda v, sps=sps, mi=mi,
                                           ji=ji: v.scalar_tensor_tensor(
                                    mtmp[ji % 2][:], sps[:, 0:512], SCALE,
                                    mask_sb[:, mi * QSB:(mi + 1) * QSB],
                                    ALU.mult, ALU.add),
                                    waits=wv, inc=dv)
                                bank_free[ji % 3] = (dv, vt)
                                et = ph.op("scalar", lambda s, ji=ji: s.activation(
                                    probs[ji % 3][:], mtmp[ji % 2][:], AF.Exp),
                                    waits=[(dv, vt)], inc=ac)
                                mtmp_free[ji % 2] = (ac, et)
                                pws[j] = (ac, et)
                            pws[j, "pi"] = ji
                            ji += 1

                        def plan_C(j):
                            kbi = lst[j]
                            pi = pws[j, "pi"]
                            vcol = 65 * NT * h2 + 65 * kbi
                            wl2 = [pws[j]]
                            if j == 0 and cseq % 2 in ctx_free:
                                wl2.append(ctx_free[cseq % 2])
                            ph.op("tensor", lambda t, hp=hp, vcol=vcol, cps=cps,
                                  pi=pi, j=j, last=(j == nk - 1): t.matmul(
                                cps[0:65, 0:512],
                                vb[hp % 2][:, vcol:vcol + 65], probs[pi % 3][:],
                                start=(j == 0), stop=last),
                                waits=wl2, inc=(pe if j == nk - 1 else None))

                        for j in range(nk):
                            plan_S(j)
                            if j == LAG and pending_norm:
                                ctx_free[(cseq - 1) % 2] = (dv, pending_norm.pop(0)())
                            if j >= LAG:
                                plan_C(j - LAG)
                        if nk <= LAG and pending_norm:
                            ctx_free[(cseq - 1) % 2] = (dv, pending_norm.pop(0)())
                        for j in range(max(0, nk - LAG), nk):
                            plan_C(j)
                        cmt = pe.n
                        pending_norm.append(plan_norm(cps, hp, hrow, qsb, cmt))
                        cseq += 1
                hp_last_use[hp] = pe.n
                if hp + 2 < NHP:
                    plan_load(hp + 2)
            while pending_norm:
                pending_norm.pop(0)()
            for j2 in range(2):
                ph.wait("sync", qkvsem[j2], qkvsem[j2].n)
            self.emit_preconvert(ph, 20)
            ph.realize(nc)
        self.attn_done = dv.n


    def phase_woA(self, sc):
        nc = self.nc
        sp, gp, pe, ac, dv = (self.sem[k] for k in ("sp", "gp", "pe", "ac", "dv"))
        NT, T_ = self.NT, self.T
        wt = self.salloc(sc, "wt_aWo", [P, DC * D], BF16)
        stage = [self.salloc(sc, f"stgo{i}", [P, D], BF16) for i in range(2)]
        stgsem = [self.S(f"stgo{i}") for i in range(2)]
        xb = [self.salloc(sc, f"wx{i}", [P, D], F32) for i in range(2)]
        xsem = [self.S(f"wx{i}") for i in range(2)]
        ao = [self.salloc(sc, f"ao{i}", [P, D], F32) for i in range(2)]
        lnat = self.salloc(sc, "lnat", [P, D], F32)
        x1b = [self.salloc(sc, f"x1b{i}", [P, D], F32) for i in range(2)]
        ln1b = [self.salloc(sc, f"ln1b{i}", [P, D], BF16) for i in range(2)]
        sq = self.salloc(sc, "wsq", [P, D], F32)
        stat = [self.salloc(sc, f"wst{i}", [P, 8], F32) for i in range(4)]
        x1sem = [self.S(f"x1st{i}") for i in range(2)]
        ln1sem = [self.S(f"ln1st{i}") for i in range(2)]
        ph = Phase("woA")
        with nc.psum_tensor("pwt0", [P, P], BF16) as p0, \
             nc.psum_tensor("pwt1", [P, P], BF16) as p1, \
             nc.psum_tensor("po0", [P, 512], F32) as o0, \
             nc.psum_tensor("po1", [P, 512], F32) as o1, \
             nc.psum_tensor("po2", [P, 512], F32) as o2, \
             nc.psum_tensor("po3", [P, 512], F32) as o3:
            self._emit_wT(ph, sc, "aWo", wt, [p0, p1], stage, stgsem)
            wT_done = ac.n
            pos = [[o0, o1], [o2, o3]]
            ev_tgt, x1_tgt, ln1_tgt = {}, {}, {}
            for tb in range(NT):
                wx = [(dv, ev_tgt[tb - 2])] if tb >= 2 else []
                xt = dma(ph, "sync", xsem[tb % 2], xb[tb % 2][:],
                         self.x.ap()[tb * P:(tb + 1) * P, :], waits=wx)
                for dh in range(2):
                    for hp in range(NHP):
                        wl = []
                        if tb == 0 and dh == 0 and hp == 0:
                            wl.append((ac, wT_done))
                        if hp == 0 and tb >= 2:
                            wl.append((ac, ev_tgt[tb - 2, dh] if isinstance(
                                ev_tgt.get((tb - 2, dh)), int) else
                                ev_tgt[(tb - 2, dh)]))
                        ph.op("tensor", lambda t, tb=tb, dh=dh, hp=hp: t.matmul(
                            pos[tb % 2][dh][:, 0:512],
                            self.ctxT[:, T_ * hp + P * tb:T_ * hp + P * (tb + 1)],
                            wt[:, D * hp + 512 * dh:D * hp + 512 * (dh + 1)],
                            start=(hp == 0), stop=False), waits=wl, inc=None)
                    ph.op("tensor", lambda t, tb=tb, dh=dh: t.matmul(
                        pos[tb % 2][dh][:, 0:512], self.sb_ones[:],
                        self.r_bo[0:1, 512 * dh:512 * (dh + 1)],
                        start=False, stop=True), inc=pe)
                    mmt = pe.n
                    ev_tgt[(tb, dh)] = ph.op(
                        "scalar", lambda s, tb=tb, dh=dh: s.copy(
                            ao[tb % 2][:, 512 * dh:512 * (dh + 1)],
                            pos[tb % 2][dh][:, 0:512]),
                        waits=[(pe, mmt)], inc=ac)
                self._ln_tile(ph, src=ao[tb % 2][:],
                              pre_wait=[(ac, ev_tgt[(tb, 1)])],
                              sq=sq, stat=stat[tb % 2], out=lnat[:],
                              out_guard=(dv, dv.n))
                x1d = ph.op("vector", lambda v, tb=tb: v.tensor_add(
                    x1b[tb % 2][:], xb[tb % 2][:], lnat[:]),
                    waits=[(dv, dv.n), (xsem[tb % 2], xt),
                           (x1sem[tb % 2], x1_tgt.get(tb - 2, 0))], inc=dv)
                x1_tgt[tb] = dma(ph, "sync", x1sem[tb % 2],
                                 self.x1f32.ap()[tb * P:(tb + 1) * P, :],
                                 x1b[tb % 2][:], waits=[(dv, x1d)])
                self._ln_tile(ph, src=x1b[tb % 2][:], pre_wait=[(dv, x1d)],
                              sq=sq, stat=stat[2 + tb % 2], out=ln1b[tb % 2][:],
                              out_guard=(ln1sem[tb % 2], ln1_tgt.get(tb - 2, 0)))
                ev_tgt[tb] = dv.n
                ln1_tgt[tb] = dma(ph, "sync", ln1sem[tb % 2],
                                  self.ln1bf.ap()[tb * P:(tb + 1) * P, :],
                                  ln1b[tb % 2][:], waits=[(dv, dv.n)])
            for j in range(2):
                ph.wait("sync", x1sem[j], x1sem[j].n)
                ph.wait("sync", ln1sem[j], ln1sem[j].n)
            self.emit_preconvert(ph, 15)
            ph.realize(nc)

    def phase_dkv(self, sc):
        nc = self.nc
        sp, gp, pe, ac, dv = (self.sem[k] for k in ("sp", "gp", "pe", "ac", "dv"))
        NT, T_ = self.NT, self.T
        ln1T = self.salloc(sc, "ln1T", [P, DC * T_], BF16)
        wdkv = self.salloc(sc, "wdkv", [P, DC * L], BF16)
        wdq = self.salloc(sc, "wdq", [P, DC * L], BF16)
        ph = Phase("dkv")
        for c in range(DC):
            dma(ph, "sync", sp, ln1T[:, c * T_:(c + 1) * T_],
                self.ln1bf.ap()[:, c * P:(c + 1) * P], transpose=True)
        dma(ph, "gpsimd", gp, wdkv[:].rearrange("p (ic l) -> p ic l", ic=DC),
            self.ext["fWdkv"].ap().rearrange("(ic p) l -> p ic l", p=P))
        dma(ph, "gpsimd", gp, wdq[:].rearrange("p (ic l) -> p ic l", ic=DC),
            self.ext["fWdq"].ap().rearrange("(ic p) l -> p ic l", p=P))
        ph.wait("sync", sp, sp.n)
        ph.wait("gpsimd", gp, gp.n)
        ld_rdy_sp, ld_rdy_gp = sp.n, gp.n
        with nc.psum_tensor("pd0", [P, 512], F32) as d0, \
             nc.psum_tensor("pd1", [P, 512], F32) as d1:
            psd = [d0, d1]
            gi = 0
            ev_tgt = {}
            for (wsb, bias, out) in [(wdkv, self.b_bdkvT, self.ckvT),
                                     (wdq, self.b_bdqT, self.cqT)]:
                for lc in range(LC):
                    for nb in range(T_ // 512):
                        wl = []
                        if gi == 0:
                            wl = [(sp, ld_rdy_sp), (gp, ld_rdy_gp)]
                        if gi >= 2:
                            wl.append((ac, ev_tgt[gi - 2]))
                        for ic in range(DC):
                            ph.op("tensor", lambda t, wsb=wsb, lc=lc, nb=nb, ic=ic,
                                  gi=gi: t.matmul(
                                psd[gi % 2][:, 0:512],
                                wsb[:, L * ic + P * lc:L * ic + P * (lc + 1)],
                                ln1T[:, T_ * ic + 512 * nb:T_ * ic + 512 * (nb + 1)],
                                start=(ic == 0), stop=(ic == DC - 1)),
                                waits=(wl if ic == 0 else []),
                                inc=(pe if ic == DC - 1 else None))
                        ev_tgt[gi] = ph.op(
                            "scalar", lambda s, out=out, bias=bias, lc=lc, nb=nb,
                            gi=gi: s.activation(
                                out[:, T_ * lc + 512 * nb:T_ * lc + 512 * (nb + 1)],
                                psd[gi % 2][:, 0:512], AF.Identity,
                                bias=bias[:, lc:lc + 1]),
                            waits=[(pe, pe.n)], inc=ac)
                        gi += 1
            self.emit_preconvert(ph, 10)
            ph.realize(nc)

    def phase_uproj(self, sc):
        nc = self.nc
        sp, gp, pe, ac, dv = (self.sem[k] for k in ("sp", "gp", "pe", "ac", "dv"))
        NT, T_ = self.NT, self.T
        for wname, kind, bias, outh, src in [
            ("fWuk", "feat", self.b_bukT, self.kt2_hbm, "ckvT"),
            ("fWuq", "feat", self.b_buqT, self.qt2_hbm, "cqT"),
            ("fWuv", "tok", None, self.v2_hbm, "ckvT"),
        ]:
            from contextlib import ExitStack
            with ExitStack() as sc2:
                wu = self.salloc(sc2, "wu_" + wname, [P, LC * D], BF16)
                ob = [self.salloc(sc2, f"ub{i}_" + wname,
                                  [P, T_ if kind == "feat" else 2 * 65 * NT], BF16)
                      for i in range(2)]
                obsem = [self.S(f"ub{i}_{wname}") for i in range(2)]
                srcT = getattr(self, src)
                ph = Phase("uproj_" + wname)
                dma(ph, "gpsimd", gp, wu[:].rearrange("p (lc d) -> p lc d", lc=LC),
                    self.ext[wname].ap().rearrange("(lc p) d -> p lc d", p=P))
                ph.wait("gpsimd", gp, gp.n)
                w_rdy = gp.n
                with nc.psum_tensor("pu0" + wname, [P, 512], F32) as u0, \
                     nc.psum_tensor("pu1" + wname, [P, 512], F32) as u1:
                    psu = [u0, u1]
                    ev_tgt, st_tgt = {}, {}
                    for hp in range(NHP):
                        if kind == "feat":
                            for nb in range(T_ // 512):
                                gi = hp * (T_ // 512) + nb
                                wl = [(gp, w_rdy)] if gi == 0 else []
                                if gi >= 2:
                                    wl = wl + [(ac, ev_tgt[gi - 2])]
                                for lc in range(LC):
                                    ph.op("tensor", lambda t, hp=hp, nb=nb, lc=lc,
                                          gi=gi: t.matmul(
                                        psu[gi % 2][:, 0:512],
                                        wu[:, D * lc + P * hp:D * lc + P * (hp + 1)],
                                        srcT[:, T_ * lc + 512 * nb:
                                             T_ * lc + 512 * (nb + 1)],
                                        start=(lc == 0), stop=(lc == LC - 1)),
                                        waits=(wl if lc == 0 else []),
                                        inc=(pe if lc == LC - 1 else None))
                                mmt = pe.n
                                wl2 = [(pe, mmt)]
                                if hp >= 2 and nb == 0:
                                    wl2.append((obsem[hp % 2], st_tgt[hp - 2]))
                                ev_tgt[gi] = ph.op("scalar", lambda s, hp=hp, nb=nb,
                                                   gi=gi: s.activation(
                                    ob[hp % 2][:, 512 * nb:512 * (nb + 1)],
                                    psu[gi % 2][:, 0:512], AF.Identity,
                                    bias=bias[:, hp:hp + 1]),
                                    waits=wl2, inc=ac)
                            st_tgt[hp] = dma(ph, "sync", obsem[hp % 2],
                                             outh.ap()[hp], ob[hp % 2][:],
                                             waits=[(ac, ev_tgt[hp * (T_ // 512)
                                                                + T_ // 512 - 1])])
                        else:
                            for tb in range(NT):
                                gi = hp * NT + tb
                                wl = [(gp, w_rdy)] if gi == 0 else []
                                if gi >= 2:
                                    wl = wl + [(ac, ev_tgt[gi - 2])]
                                for lc in range(LC):
                                    ph.op("tensor", lambda t, hp=hp, tb=tb, lc=lc,
                                          gi=gi: t.matmul(
                                        psu[gi % 2][:, 0:P],
                                        srcT[:, T_ * lc + P * tb:T_ * lc + P * (tb + 1)],
                                        wu[:, D * lc + P * hp:D * lc + P * (hp + 1)],
                                        start=(lc == 0), stop=False),
                                        waits=(wl if lc == 0 else []), inc=None)
                                ph.op("tensor", lambda t, hp=hp, gi=gi: t.matmul(
                                    psu[gi % 2][:, 0:P], self.sb_ones[:],
                                    self.r_buv[0:1, P * hp:P * (hp + 1)],
                                    start=False, stop=True), inc=pe)
                                mmt = pe.n
                                wl2 = [(pe, mmt)]
                                if hp >= 2 and tb == 0:
                                    wl2.append((obsem[hp % 2], st_tgt[hp - 2]))
                                for h2 in range(2):
                                    col = 65 * NT * h2 + 65 * tb
                                    ev_tgt[gi] = ph.op(
                                        "scalar", lambda s, col=col, h2=h2, gi=gi,
                                        hp=hp: s.copy(
                                            ob[hp % 2][:, col:col + 64],
                                            psu[gi % 2][:, 64 * h2:64 * (h2 + 1)]),
                                        waits=(wl2 if h2 == 0 else []), inc=ac)
                            onescol = ph.op("vector", lambda v, hp=hp: v.memset(
                                ob[hp % 2][:, 0:2 * 65 * NT].rearrange(
                                    "p (t c) -> p t c", c=65)[:, :, 64:65], 1.0),
                                waits=[(ac, ev_tgt[hp * NT + NT - 1]), (dv, dv.n)],
                                inc=dv)
                            st_tgt[hp] = dma(ph, "sync", obsem[hp % 2],
                                             outh.ap()[hp], ob[hp % 2][:],
                                             waits=[(dv, onescol)])
                    for j in range(2):
                        ph.wait("sync", obsem[j], obsem[j].n)
                    self.emit_preconvert(ph, 8)
                    ph.realize(nc)

    def phase_woB(self, sc):
        nc = self.nc
        sp, gp, pe, ac, dv = (self.sem[k] for k in ("sp", "gp", "pe", "ac", "dv"))
        NT, T_ = self.NT, self.T
        wo2 = self.salloc(sc, "wo2", [P, DC * D], BF16)
        g1bc = self.salloc(sc, "g1bcl", [P, D], BF16)
        be1bc = self.salloc(sc, "be1bcl", [P, D], BF16)
        ln1b = [self.salloc(sc, f"bl{i}", [P, D], BF16) for i in range(2)]
        lnsem = [self.S(f"bln{i}") for i in range(2)]
        ai = [self.salloc(sc, f"ai{i}", [P, D], F32) for i in range(2)]
        ssum = self.salloc(sc, "ssum", [P, D], F32)
        x1fb = [self.salloc(sc, f"x1fb{i}", [P, D], F32) for i in range(2)]
        sq = self.salloc(sc, "bsq", [P, D], F32)
        stat = [self.salloc(sc, f"bst{i}", [P, 8], F32) for i in range(2)]
        x1fsem = [self.S(f"x1fst{i}") for i in range(2)]
        x1fbsem = [self.S(f"x1fbst{i}") for i in range(2)]
        ph = Phase("woB")
        dma(ph, "sync", sp, g1bc[:], self.gb_hbm.ap()[0:P, :])
        dma(ph, "sync", sp, be1bc[:], self.gb_hbm.ap()[P:2 * P, :])
        ph.wait("sync", sp, sp.n)
        dma(ph, "gpsimd", gp, wo2[:].rearrange("p (jc d) -> p jc d", jc=DC),
            self.ext["fWo"].ap().rearrange("(jc p) d -> p jc d", p=P))
        ph.wait("gpsimd", gp, gp.n)
        w_rdy = gp.n
        with nc.psum_tensor("bo0", [P, 512], F32) as o0, \
             nc.psum_tensor("bo1", [P, 512], F32) as o1, \
             nc.psum_tensor("bo2", [P, 512], F32) as o2, \
             nc.psum_tensor("bo3", [P, 512], F32) as o3:
            pos = [[o0, o1], [o2, o3]]
            ev_tgt, st1, st2 = {}, {}, {}
            for tb in range(NT):
                wlx = [(dv, ev_tgt[tb - 2])] if tb >= 2 else []
                lt = dma(ph, "sync", lnsem[tb % 2], ln1b[tb % 2][:],
                         self.ln1bf.ap()[tb * P:(tb + 1) * P, :], waits=wlx)
                for dh in range(2):
                    wl = [(gp, w_rdy)] if tb == 0 and dh == 0 else []
                    if tb >= 2:
                        wl.append((ac, ev_tgt[(tb - 2, dh)]))
                    for jc in range(DC):
                        ph.op("tensor", lambda t, tb=tb, dh=dh, jc=jc: t.matmul(
                            pos[tb % 2][dh][:, 0:512],
                            self.ctxT[:, T_ * jc + P * tb:T_ * jc + P * (tb + 1)],
                            wo2[:, D * jc + 512 * dh:D * jc + 512 * (dh + 1)],
                            start=(jc == 0), stop=False),
                            waits=(wl if jc == 0 else []), inc=None)
                    ph.op("tensor", lambda t, tb=tb, dh=dh: t.matmul(
                        pos[tb % 2][dh][:, 0:512], self.sb_ones[:],
                        self.r_fbo[0:1, 512 * dh:512 * (dh + 1)],
                        start=False, stop=True), inc=pe)
                    mmt = pe.n
                    ev_tgt[(tb, dh)] = ph.op(
                        "scalar", lambda s, tb=tb, dh=dh: s.copy(
                            ai[tb % 2][:, 512 * dh:512 * (dh + 1)],
                            pos[tb % 2][dh][:, 0:512]),
                        waits=[(pe, mmt)], inc=ac)
                sd = ph.op("vector", lambda v, tb=tb: v.tensor_add(
                    ssum[:], ln1b[tb % 2][:], ai[tb % 2][:]),
                    waits=[(ac, ev_tgt[(tb, 1)]), (lnsem[tb % 2], lt),
                           (dv, dv.n)], inc=dv)
                self._ln_tile(ph, src=ssum[:], pre_wait=[(dv, sd)],
                              sq=sq, stat=stat[tb % 2], out=ssum[:],
                              out_guard=(dv, dv.n))
                g1d = ph.op("vector", lambda v, tb=tb: v.tensor_mul(
                    ssum[:], ssum[:], g1bc[:]), waits=[(dv, dv.n)], inc=dv)
                b1d = ph.op("vector", lambda v, tb=tb: v.tensor_add(
                    x1fb[tb % 2][:], ssum[:], be1bc[:]),
                    waits=[(dv, g1d),
                           (x1fsem[tb % 2], st1.get(tb - 2, 0)),
                           (x1fbsem[tb % 2], st2.get(tb - 2, 0))], inc=dv)
                st1[tb] = dma(ph, "sync", x1fsem[tb % 2],
                              self.x1ff32.ap()[tb * P:(tb + 1) * P, :],
                              x1fb[tb % 2][:], waits=[(dv, b1d)])
                st2[tb] = dma(ph, "gpsimd", x1fbsem[tb % 2],
                              self.x1fbf.ap()[tb * P:(tb + 1) * P, :],
                              x1fb[tb % 2][:], waits=[(dv, b1d)])
                ev_tgt[tb] = dv.n
            for j in range(2):
                ph.wait("sync", x1fsem[j], x1fsem[j].n)
                ph.wait("gpsimd", x1fbsem[j], x1fbsem[j].n)
                ph.wait("sync", lnsem[j], lnsem[j].n)
            self.emit_preconvert(ph, 15)
            ph.realize(nc)

    def phase_x1fT(self, sc):
        nc = self.nc
        sp, gp, pe, ac, dv = (self.sem[k] for k in ("sp", "gp", "pe", "ac", "dv"))
        T_ = self.T
        stg = [self.salloc(sc, f"xft{i}", [P, T_], BF16) for i in range(2)]
        tsem = [self.S(f"xftt{i}") for i in range(2)]
        ssem = [self.S(f"xfts{i}") for i in range(2)]
        ph = Phase("x1fT")
        st_tgt = {}
        for c in range(DC):
            w = [(ssem[c % 2], st_tgt[c - 2])] if c >= 2 else []
            tt = dma(ph, "sync", tsem[c % 2], stg[c % 2][:],
                     self.x1fbf.ap()[:, c * P:(c + 1) * P], transpose=True,
                     waits=w)
            st_tgt[c] = dma(ph, "sync", ssem[c % 2], self.x1fT_hbm.ap()[c],
                            stg[c % 2][:], waits=[(tsem[c % 2], tt)])
        for j in range(2):
            ph.wait("sync", ssem[j], ssem[j].n)
        self.emit_preconvert(ph, 30)
        ph.realize(nc)

    def phase_moe(self, sc):
        nc = self.nc
        sp, gp, pe, ac, dv = (self.sem[k] for k in ("sp", "gp", "pe", "ac", "dv"))
        NT, T_ = self.NT, self.T
        NTC = T_ // 512
        w1sb = self.salloc(sc, "w1sb", [P, DC * FF], BF16)
        w2sb = self.salloc(sc, "w2sb", [P, FFC * D], BF16)
        h_sb = self.salloc(sc, "h_sb", [P, FFC * 512], BF16)
        moe = self.salloc(sc, "moe", [P, 4 * D], F32)
        xtc = self.salloc(sc, "xtc", [P, DC * 512], BF16)
        wg_sb = self.salloc(sc, "wg_sb", [P, DC * E], BF16)
        wn_sb = self.salloc(sc, "wn_sb", [P, DC * E], BF16)
        nz = self.salloc(sc, "nz", [P, 4 * E], F32)
        gt1 = self.salloc(sc, "gt1", [P, E], F32)
        gt2 = self.salloc(sc, "gt2", [P, E], F32)
        hx = self.salloc(sc, "hx", [P, E], F32)
        m1 = self.salloc(sc, "m1", [P, 1], F32)
        negm1 = self.salloc(sc, "negm1", [P, 1], F32)
        m1pad = self.salloc(sc, "m1pad", [P, E], F32)
        hrep = self.salloc(sc, "hrep", [P, E], F32)
        m2 = self.salloc(sc, "m2", [P, 1], F32)
        e1t = self.salloc(sc, "e1t", [P, E], F32)
        ge = self.salloc(sc, "ge", [P, E], F32)
        zt = self.salloc(sc, "zt", [P, 1], F32)
        w1sem, w2sem = self.S("w1s"), self.S("w2s")
        xtcsem, nzsem = self.S("xtc"), self.S("nz")
        self._moe_st = {}
        ph = Phase("moe")
        dma(ph, "gpsimd", gp, wg_sb[:].rearrange("p (ic e) -> p ic e", ic=DC),
            self.ext["Wg"].ap().rearrange("(ic p) e -> p ic e", p=P))
        dma(ph, "gpsimd", gp, wn_sb[:].rearrange("p (ic e) -> p ic e", ic=DC),
            self.ext["Wn"].ap().rearrange("(ic p) e -> p ic e", p=P))
        ph.wait("gpsimd", gp, gp.n)
        g_rdy = gp.n
        with nc.psum_tensor("mh0", [P, 512], F32) as h0, \
             nc.psum_tensor("mh1", [P, 512], F32) as h1, \
             nc.psum_tensor("mo0", [P, 512], F32) as po0, \
             nc.psum_tensor("mo1", [P, 512], F32) as po1, \
             nc.psum_tensor("mgG", [P, 512], F32) as pgG, \
             nc.psum_tensor("mgN", [P, 512], F32) as pgN, \
             nc.psum_tensor("mgT", [P, P], F32) as pgT:
            psH, psO = [h0, h1], [po0, po1]
            w1_use, w2_use = 0, 0
            xtc_use = 0
            ystore = {}
            for tc in range(NTC):
                xl = 0
                for c in range(DC):
                    xl = dma(ph, "sync", xtcsem, xtc[:, c * 512:(c + 1) * 512],
                             self.x1fT_hbm.ap()[c, :, tc * 512:(tc + 1) * 512],
                             waits=([(pe, xtc_use)] if c == 0 else []))
                nl = dma(ph, "sync", nzsem,
                         nz[:].rearrange("p (b e) -> p b e", e=E),
                         self.noise.ap()[tc * 512:(tc + 1) * 512, :]
                         .rearrange("(b p) e -> p b e", p=P),
                         waits=[(dv, dv.n)])
                w1l = dma(ph, "sync", w1sem,
                          w1sb[:].rearrange("p (ic f) -> p ic f", ic=DC),
                          self.w1bf.ap()[0].rearrange("(ic p) f -> p ic f", p=P),
                          waits=[(pe, w1_use), (gp, self._pc_done_tgt())])
                w2l = dma(ph, "sync", w2sem,
                          w2sb[:].rearrange("p (f d) -> p f d", f=FFC),
                          self.w2bf.ap()[0].rearrange("(f p) d -> p f d", p=P),
                          waits=[(pe, w2_use)])
                gmt = {}
                for tbi in range(4):
                    wlg = [(xtcsem, xl)]
                    if tc == 0 and tbi == 0:
                        wlg.append((gp, g_rdy))
                    if tbi >= 1:
                        wlg.append((dv, gmt[tbi - 1]))
                    for ic in range(DC):
                        ph.op("tensor", lambda t, tbi=tbi, ic=ic: t.matmul(
                            pgG[:, 0:E],
                            xtc[:, 512 * ic + P * tbi:512 * ic + P * (tbi + 1)],
                            wg_sb[:, E * ic:E * (ic + 1)],
                            start=(ic == 0), stop=False),
                            waits=(wlg if ic == 0 else []), inc=None)
                    ph.op("tensor", lambda t: t.matmul(
                        pgG[:, 0:E], self.sb_ones[:], self.r_bg[:],
                        start=False, stop=True), inc=pe)
                    for ic in range(DC):
                        ph.op("tensor", lambda t, tbi=tbi, ic=ic: t.matmul(
                            pgN[:, 0:E],
                            xtc[:, 512 * ic + P * tbi:512 * ic + P * (tbi + 1)],
                            wn_sb[:, E * ic:E * (ic + 1)],
                            start=(ic == 0), stop=False), inc=None)
                    ph.op("tensor", lambda t: t.matmul(
                        pgN[:, 0:E], self.sb_ones[:], self.r_bn[:],
                        start=False, stop=True), inc=pe)
                    gmm = pe.n
                    a1 = ph.op("scalar", lambda s: s.activation(
                        gt2[:], pgN[:, 0:E], AF.Abs), waits=[(pe, gmm)], inc=ac)
                    a2 = ph.op("scalar", lambda s: s.activation(
                        gt2[:], gt2[:], AF.Exp, scale=self.cst[:, 1:2]),
                        waits=[(ac, a1)], inc=ac)
                    a3 = ph.op("scalar", lambda s: s.activation(
                        gt2[:], gt2[:], AF.Ln, bias=1.0), waits=[(ac, a2)], inc=ac)
                    r1 = ph.op("vector", lambda v: v.tensor_scalar_max(
                        gt1[:], pgN[:, 0:E], 0.0),
                        waits=[(pe, gmm), (dv, dv.n)], inc=dv)
                    s1_ = ph.op("vector", lambda v: v.tensor_add(
                        gt1[:], gt1[:], gt2[:]), waits=[(dv, r1), (ac, a3)], inc=dv)
                    n1 = ph.op("vector", lambda v, tbi=tbi: v.tensor_mul(
                        gt1[:], gt1[:], nz[:, E * tbi:E * (tbi + 1)]),
                        waits=[(dv, s1_), (nzsem, nl)], inc=dv)
                    hx1 = ph.op("vector", lambda v: v.tensor_add(
                        hx[:], pgG[:, 0:E], gt1[:]), waits=[(dv, n1)], inc=dv)
                    mm1 = ph.op("vector", lambda v: v.reduce_max(
                        m1[:], hx[:], axis=AX.X), waits=[(dv, hx1)], inc=dv)
                    nm1 = ph.op("vector", lambda v: v.reduce_max(
                        negm1[:], hx[:], axis=AX.X, negate=True),
                        waits=[(dv, hx1)], inc=dv)
                    mp = ph.op("vector", lambda v: v.memset(m1pad[:], -1e38),
                               waits=[(dv, dv.n)], inc=dv)
                    mp2 = ph.op("vector", lambda v: v.tensor_copy(
                        m1pad[:, 0:1], m1[:]), waits=[(dv, mm1), (dv, mp)], inc=dv)
                    mr = ph.op("vector", lambda v: v.match_replace(
                        hrep[:], m1pad[:], hx[:], -1e30),
                        waits=[(dv, mp2)], inc=dv)
                    mm2 = ph.op("vector", lambda v: v.reduce_max(
                        m2[:], hrep[:], axis=AX.X), waits=[(dv, mr)], inc=dv)
                    ex1 = ph.op("scalar", lambda s: s.activation(
                        e1t[:], hx[:], AF.Exp, bias=negm1[:]),
                        waits=[(dv, nm1), (dv, hx1)], inc=ac)
                    g1_ = ph.op("vector", lambda v: v.tensor_scalar(
                        ge[:], hx[:], m2[:, 0:1], None, ALU.is_ge),
                        waits=[(dv, mm2)], inc=dv)
                    e1d = ph.op("vector", lambda v, tbi=tbi: v.tensor_mul(
                        self.e1m_sb[:, E * tbi:E * (tbi + 1)], e1t[:], ge[:]),
                        waits=[(dv, g1_), (ac, ex1)], inc=dv)
                    zd = ph.op("vector", lambda v, tbi=tbi: v.reduce_sum(
                        zt[:], self.e1m_sb[:, E * tbi:E * (tbi + 1)], axis=AX.X),
                        waits=[(dv, e1d)], inc=dv)
                    rz = ph.op("vector", lambda v, tb=tc * 4 + tbi: v.reciprocal(
                        self.recipZ[:, tb:tb + 1], zt[:]),
                        waits=[(dv, zd)], inc=dv)
                    et1 = ph.op("tensor", lambda t, tbi=tbi: t.transpose(
                        pgT[0:E, 0:P], self.e1m_sb[:, E * tbi:E * (tbi + 1)],
                        self.sb_ident32[:]), waits=[(dv, e1d), (ac, ac.n)], inc=pe)
                    ph.op("scalar", lambda s, tbi=tbi: s.copy(
                        self.e1mT_sb[0:E, P * tbi:P * (tbi + 1)], pgT[0:E, 0:P]),
                        waits=[(pe, et1)], inc=ac)
                    gmt[tbi] = dv.n
                gate_done_ac = ac.n
                b2t = {}
                for tbi in range(4):
                    for dh in range(2):
                        wbm = [(ac, gate_done_ac), (dv, dv.n)]
                        if tbi > 0:
                            wbm.append((ac, b2t[(tbi - 1, dh)]))
                        bt = ph.op("tensor", lambda t, tbi=tbi, dh=dh: t.matmul(
                            psO[dh][:, 0:512],
                            self.e1mT_sb[0:E, P * tbi:P * (tbi + 1)],
                            self.b2_sb[0:E, 512 * dh:512 * (dh + 1)],
                            start=True, stop=True),
                            waits=wbm, inc=pe)
                        wb2 = [(pe, bt)]
                        if tc > 0:
                            wb2.append((self.S(f"moest{tbi}"), self._moe_st[tbi]))
                        b2t[(tbi, dh)] = ph.op(
                            "scalar", lambda s, tbi=tbi, dh=dh: s.copy(
                                moe[:, D * tbi + 512 * dh:D * tbi + 512 * (dh + 1)],
                                psO[dh][:, 0:512]), waits=wb2, inc=ac)
                moe_init_ac = ac.n
                hev = {}
                for e in range(E):
                    if e > 0:
                        w1l = dma(ph, "sync", w1sem,
                                  w1sb[:].rearrange("p (ic f) -> p ic f", ic=DC),
                                  self.w1bf.ap()[e].rearrange(
                                      "(ic p) f -> p ic f", p=P),
                                  waits=[(pe, w1_use)])
                        w2l = dma(ph, "sync", w2sem,
                                  w2sb[:].rearrange("p (f d) -> p f d", f=FFC),
                                  self.w2bf.ap()[e].rearrange(
                                      "(f p) d -> p f d", p=P),
                                  waits=[(pe, w2_use)])
                    w2_prev_done = pe.n if e > 0 else 0
                    for f in range(FFC):
                        wl = [(w1sem, w1l)] if f == 0 else []
                        if f == 0:
                            wl.append((xtcsem, xl))
                        if f >= 2:
                            wl.append((ac, hev[f - 2]))
                        for ic in range(DC):
                            ph.op("tensor", lambda t, e=e, f=f, ic=ic: t.matmul(
                                psH[f % 2][:, 0:512],
                                w1sb[:, FF * ic + P * f:FF * ic + P * (f + 1)],
                                xtc[:, 512 * ic:512 * (ic + 1)],
                                start=(ic == 0), stop=(ic == DC - 1)),
                                waits=(wl if ic == 0 else []),
                                inc=(pe if ic == DC - 1 else None))
                        mmt = pe.n
                        wl2 = [(pe, mmt)]
                        if f == 0 and e > 0:
                            wl2.append((pe, w2_prev_done))
                        hev[f] = ph.op("scalar", lambda s, e=e, f=f: s.activation(
                            h_sb[:, 512 * f:512 * (f + 1)], psH[f % 2][:, 0:512],
                            AF.Relu, bias=self.b_b1T[:, FFC * e + f:FFC * e + f + 1]),
                            waits=wl2, inc=ac)
                    w1_use = pe.n
                    h_done = hev[FFC - 1]
                    sttv = {}
                    for tbi in range(4):
                        for dh in range(2):
                            gsel = (tbi * 2 + dh) % 2
                            wl = [(ac, h_done), (w2sem, w2l)] \
                                if tbi == 0 and dh == 0 else []
                            if e == 0 and tbi * 2 + dh < 2:
                                wl.append((ac, moe_init_ac))
                            if (tbi, dh) in (( 1, 0),) and False:
                                pass
                            prev = (tbi * 2 + dh) - 2
                            if prev >= 0:
                                wl.append((dv, sttv.get(prev, dv.n)))
                            elif e > 0 or tc > 0:
                                wl.append((dv, dv.n))
                            for f in range(FFC):
                                ph.op("tensor", lambda t, tbi=tbi, dh=dh, f=f,
                                      gsel=gsel: t.matmul(
                                    psO[gsel][:, 0:512],
                                    h_sb[:, 512 * f + P * tbi:512 * f + P * (tbi + 1)],
                                    w2sb[:, D * f + 512 * dh:D * f + 512 * (dh + 1)],
                                    start=(f == 0), stop=(f == FFC - 1)),
                                    waits=(wl if f == 0 else []),
                                    inc=(pe if f == FFC - 1 else None))
                            mmt = pe.n
                            wme = [(pe, mmt), (dv, dv.n)]
                            if e == 0:
                                wme.append((ac, b2t[(tbi, dh)]))
                            sttv[tbi * 2 + dh] = ph.op(
                                "vector", lambda v, tbi=tbi, dh=dh, e=e:
                                v.scalar_tensor_tensor(
                                    moe[:, D * tbi + 512 * dh:
                                        D * tbi + 512 * (dh + 1)],
                                    psO[gsel][:, 0:512] if False else
                                    psO[(tbi * 2 + dh) % 2][:, 0:512],
                                    self.e1m_sb[:, E * tbi + e:E * tbi + e + 1],
                                    moe[:, D * tbi + 512 * dh:
                                        D * tbi + 512 * (dh + 1)],
                                    ALU.mult, ALU.add),
                                waits=wme, inc=dv)
                    w2_use = pe.n
                xtc_use = pe.n
                for tbi in range(4):
                    tb = tc * 4 + tbi
                    self._moe_st[tbi] = dma(
                        ph, "sync", self.S(f"moest{tbi}"),
                        self.moe_hbm.ap()[tb * P:(tb + 1) * P, :],
                        moe[:, D * tbi:D * (tbi + 1)],
                        waits=[(dv, sttv[tbi * 2 + 1])])
            for tbi in range(4):
                sm = self.S(f"moest{tbi}")
                ph.wait("sync", sm, sm.n)
            ph.realize(nc)

    def phase_gate(self, sc):
        nc = self.nc
        sp, gp, pe, ac, dv = (self.sem[k] for k in ("sp", "gp", "pe", "ac", "dv"))
        NT, T_, NCHK = self.NT, self.T, self.NCHK
        xT = self.salloc(sc, "gxT", [P, DC * T_], BF16)
        wg_sb = self.salloc(sc, "gwg", [P, DC * E], BF16)
        wn_sb = self.salloc(sc, "gwn", [P, DC * E], BF16)
        selb = self.salloc(sc, "selb", [P, NT * E], BF16)
        i12 = self.salloc(sc, "i12", [P, 2 * NT], F32)
        nzb = [self.salloc(sc, f"gnz{i}", [P, E], F32) for i in range(2)]
        nzsem = [self.S(f"gnz{i}") for i in range(2)]
        gt1 = self.salloc(sc, "ggt1", [P, E], F32)
        gt2 = self.salloc(sc, "ggt2", [P, E], F32)
        hx = self.salloc(sc, "ghx", [P, E], F32)
        m1 = self.salloc(sc, "gm1", [P, 1], F32)
        negm1 = self.salloc(sc, "gnegm1", [P, 1], F32)
        m1pad = self.salloc(sc, "gm1pad", [P, E], F32)
        hrep = self.salloc(sc, "ghrep", [P, E], F32)
        m2 = self.salloc(sc, "gm2", [P, 1], F32)
        e1t = self.salloc(sc, "ge1t", [P, E], F32)
        m2pad = self.salloc(sc, "gm2pad", [P, E], F32)
        idx8 = self.salloc(sc, "gidx8", [P, E], mybir.dt.uint32)
        idx1 = self.salloc(sc, "gidx1", [P, 1], F32)
        ge = self.salloc(sc, "gge", [P, E], F32)
        e1m = self.salloc(sc, "ge1m", [P, E], F32)
        zt = self.salloc(sc, "gzt", [P, 1], F32)
        aux = [self.salloc(sc, f"gaux{i}", [P, 2], F32) for i in range(2)]
        auxsem = [self.S(f"gaux{i}") for i in range(2)]
        lt_sb = self.salloc(sc, "glt", [P, P], BF16)
        onescol = self.salloc(sc, "gonescol", [P, 1], BF16)
        iota_sb = self.salloc(sc, "giota", [P, 1], F32)
        iota8_sb = self.salloc(sc, "giota8", [P, E], F32)
        iotar = self.salloc(sc, "giotar", [P, NCHK], F32)
        cval_sb = self.salloc(sc, "gcval", [1, NCHK], F32)
        big_sb = self.salloc(sc, "gbig", [P, NCHK * 4], F32)
        big_sb2 = self.salloc(sc, "gbig2", [P, NCHK * 4], F32)
        totals = self.salloc(sc, "gtot", [1, NT * E], F32)
        excl = self.salloc(sc, "gexcl", [1, NT * E], F32)
        run = self.salloc(sc, "grun", [1, E], F32)
        nrow = self.salloc(sc, "gnrow", [1, E], F32)
        pad = self.salloc(sc, "gpad", [1, E], F32)
        base = self.salloc(sc, "gbase", [1, E], F32)
        endr = self.salloc(sc, "gend", [1, E], F32)
        brep = self.salloc(sc, "gbrep", [1, NT * E], F32)
        segrow = self.salloc(sc, "gsegrow", [1, NT * E], F32)
        echk = self.salloc(sc, "gechk", [1, NCHK], F32)
        tmpc = self.salloc(sc, "gtmpc", [1, NCHK], F32)
        tmpc2 = self.salloc(sc, "gtmpc2", [1, NCHK], F32)
        padi = self.salloc(sc, "gpadi", [1, E], I32)
        Rl = self.salloc(sc, "gRl", [P, NT * E], F32)
        pall = self.salloc(sc, "gpall", [P, NT * E], F32)
        ohb = self.salloc(sc, "gohb", [P, E], F32)
        posf2 = [self.salloc(sc, f"gposf{i}", [P, 1], F32) for i in range(2)]
        posi2 = [self.salloc(sc, f"gposi{i}", [P, 1], I32) for i in range(2)]
        tv2 = [self.salloc(sc, f"gtv{i}", [P, 1], F32) for i in range(2)]
        ts2 = [self.salloc(sc, f"gts{i}", [P, 1], F32) for i in range(2)]
        ph = Phase("gate")
        for c in range(DC):
            dma(ph, "sync", sp, xT[:, c * T_:(c + 1) * T_], self.x1fT_hbm.ap()[c])
        dma(ph, "gpsimd", gp, wg_sb[:].rearrange("p (ic e) -> p ic e", ic=DC),
            self.ext["Wg"].ap().rearrange("(ic p) e -> p ic e", p=P))
        dma(ph, "gpsimd", gp, wn_sb[:].rearrange("p (ic e) -> p ic e", ic=DC),
            self.ext["Wn"].ap().rearrange("(ic p) e -> p ic e", p=P))
        for dst, src in [(lt_sb[:], self.ltu_bf.ap()), (iota_sb[:], self.iota_p.ap()),
                         (iota8_sb[:], self.iota8.ap()), (cval_sb[:], self.cvals.ap()),
                         (big_sb[:], self.padg_c.ap()),
                         (big_sb2[:], self.pads_c.ap()),
                         (iotar[:], self.iota_pr.ap())]:
            dma(ph, "sync", sp, dst, src)
        dma(ph, "sync", sp,
            self.idxg_hbm.ap().rearrange("(p b) -> p b", p=P), big_sb[:],
            waits=[(sp, sp.n)])
        dma(ph, "sync", sp,
            self.idxs_hbm.ap().rearrange("(p b) -> p b", p=P), big_sb2[:])
        ph.op("vector", lambda v: v.memset(onescol[:], 1.0), inc=dv)
        ld_rdy = sp.n
        ph.wait("sync", sp, sp.n)
        ph.wait("gpsimd", gp, gp.n)
        g_rdy = gp.n
        with nc.psum_tensor("gpsG", [P, 512], F32) as pgG, \
             nc.psum_tensor("gpsN", [P, 512], F32) as pgN, \
             nc.psum_tensor("gps1", [P, 512], F32) as ps1, \
             nc.psum_tensor("gps2", [P, 512], F32) as ps2:
            st_aux = {}
            for tb in range(NT):
                nl = dma(ph, "sync", nzsem[tb % 2], nzb[tb % 2][:],
                         self.noise.ap()[tb * P:(tb + 1) * P, :],
                         waits=[(dv, dv.n)])
                wlg = [(sp, ld_rdy), (gp, g_rdy)] if tb == 0 else [(dv, dv.n), (ac, ac.n)]
                for ic in range(DC):
                    ph.op("tensor", lambda t, tb=tb, ic=ic: t.matmul(
                        pgG[:, 0:E],
                        xT[:, T_ * ic + P * tb:T_ * ic + P * (tb + 1)],
                        wg_sb[:, E * ic:E * (ic + 1)],
                        start=(ic == 0), stop=False),
                        waits=(wlg if ic == 0 else []), inc=None)
                ph.op("tensor", lambda t: t.matmul(
                    pgG[:, 0:E], self.sb_ones[:], self.r_bg[:],
                    start=False, stop=True), inc=pe)
                for ic in range(DC):
                    ph.op("tensor", lambda t, tb=tb, ic=ic: t.matmul(
                        pgN[:, 0:E],
                        xT[:, T_ * ic + P * tb:T_ * ic + P * (tb + 1)],
                        wn_sb[:, E * ic:E * (ic + 1)],
                        start=(ic == 0), stop=False), inc=None)
                ph.op("tensor", lambda t: t.matmul(
                    pgN[:, 0:E], self.sb_ones[:], self.r_bn[:],
                    start=False, stop=True), inc=pe)
                gmm = pe.n
                a1 = ph.op("scalar", lambda s: s.activation(
                    gt2[:], pgN[:, 0:E], AF.Abs), waits=[(pe, gmm), (dv, dv.n)],
                    inc=ac)
                a2 = ph.op("scalar", lambda s: s.activation(
                    gt2[:], gt2[:], AF.Exp, scale=self.cst[:, 1:2]),
                    waits=[(ac, a1)], inc=ac)
                a3 = ph.op("scalar", lambda s: s.activation(
                    gt2[:], gt2[:], AF.Ln, bias=1.0), waits=[(ac, a2)], inc=ac)
                r1 = ph.op("vector", lambda v: v.tensor_scalar_max(
                    gt1[:], pgN[:, 0:E], 0.0), waits=[(pe, gmm), (dv, dv.n)], inc=dv)
                s1_ = ph.op("vector", lambda v: v.tensor_add(
                    gt1[:], gt1[:], gt2[:]), waits=[(dv, r1), (ac, a3)], inc=dv)
                n1 = ph.op("vector", lambda v, tb=tb: v.tensor_mul(
                    gt1[:], gt1[:], nzb[tb % 2][:]),
                    waits=[(dv, s1_), (nzsem[tb % 2], nl)], inc=dv)
                hx1 = ph.op("vector", lambda v: v.tensor_add(
                    hx[:], pgG[:, 0:E], gt1[:]), waits=[(dv, n1)], inc=dv)
                mm1 = ph.op("vector", lambda v: v.reduce_max(
                    m1[:], hx[:], axis=AX.X), waits=[(dv, hx1)], inc=dv)
                nm1 = ph.op("vector", lambda v: v.reduce_max(
                    negm1[:], hx[:], axis=AX.X, negate=True), waits=[(dv, hx1)],
                    inc=dv)
                mp = ph.op("vector", lambda v: v.memset(m1pad[:], -1e38),
                           waits=[(dv, dv.n)], inc=dv)
                mp2 = ph.op("vector", lambda v: v.tensor_copy(
                    m1pad[:, 0:1], m1[:]), waits=[(dv, mm1), (dv, mp)], inc=dv)
                mr = ph.op("vector", lambda v: v.match_replace(
                    hrep[:], m1pad[:], hx[:], -1e30), waits=[(dv, mp2)], inc=dv)
                mm2 = ph.op("vector", lambda v: v.reduce_max(
                    m2[:], hrep[:], axis=AX.X), waits=[(dv, mr)], inc=dv)
                ii1a = ph.op("vector", lambda v: v.max_index(
                    idx8[:], m1pad[:], hx[:]), waits=[(dv, mp2), (dv, dv.n)],
                    inc=dv)
                ii1 = ph.op("vector", lambda v, tb=tb: v.tensor_copy(
                    i12[:, tb:tb + 1], idx8[:, 0:1]), waits=[(dv, ii1a)], inc=dv)
                mq = ph.op("vector", lambda v: v.memset(m2pad[:], -1e38),
                           waits=[(dv, dv.n)], inc=dv)
                mq2 = ph.op("vector", lambda v: v.tensor_copy(
                    m2pad[:, 0:1], m2[:]), waits=[(dv, mm2), (dv, mq)], inc=dv)
                ii2a = ph.op("vector", lambda v: v.max_index(
                    idx8[:], m2pad[:], hrep[:]), waits=[(dv, mq2)], inc=dv)
                ii2 = ph.op("vector", lambda v, tb=tb: v.tensor_copy(
                    i12[:, NT + tb:NT + tb + 1], idx8[:, 0:1]),
                    waits=[(dv, ii2a)], inc=dv)
                ex1 = ph.op("scalar", lambda s: s.activation(
                    e1t[:], hx[:], AF.Exp, bias=negm1[:]),
                    waits=[(dv, nm1), (dv, hx1)], inc=ac)
                g1_ = ph.op("vector", lambda v: v.tensor_scalar(
                    ge[:], hx[:], m2[:, 0:1], None, ALU.is_ge),
                    waits=[(dv, mm2)], inc=dv)
                sb1 = ph.op("vector", lambda v, tb=tb: v.tensor_copy(
                    selb[:, E * tb:E * (tb + 1)], ge[:]), waits=[(dv, g1_)], inc=dv)
                e1d = ph.op("vector", lambda v: v.tensor_mul(
                    e1m[:], e1t[:], ge[:]), waits=[(dv, g1_), (ac, ex1)], inc=dv)
                zd = ph.op("vector", lambda v: v.reduce_sum(
                    zt[:], e1m[:], axis=AX.X), waits=[(dv, e1d)], inc=dv)
                rz = ph.op("vector", lambda v, tb=tb: v.reciprocal(
                    self.recipZ[:, tb:tb + 1], zt[:]), waits=[(dv, zd)], inc=dv)
                wl_aux = [(dv, zd)]
                if tb >= 2:
                    wl_aux.append((auxsem[tb % 2], st_aux[tb - 2]))
                w1d = ph.op("vector", lambda v, tb=tb: v.reduce_max(
                    aux[tb % 2][:, 0:1], e1m[:], axis=AX.X), waits=wl_aux, inc=dv)
                w2d = ph.op("vector", lambda v, tb=tb: v.tensor_sub(
                    aux[tb % 2][:, 1:2], zt[:], aux[tb % 2][:, 0:1]),
                    waits=[(dv, w1d)], inc=dv)
                st_aux[tb] = dma(ph, "sync", auxsem[tb % 2],
                                 self.aux2_hbm.ap()[tb * P:(tb + 1) * P, :],
                                 aux[tb % 2][:], waits=[(dv, w2d)])
            t1 = ph.op("tensor", lambda t: t.matmul(
                ps1[0:1, 0:NT * E], onescol[:], selb[:], start=True, stop=True),
                waits=[(dv, dv.n)], inc=pe)
            t2 = ph.op("tensor", lambda t: t.matmul(
                ps2[:, 0:NT * E], lt_sb[:], selb[:], start=True, stop=True),
                inc=pe)
            e1_ = ph.op("vector", lambda v: v.tensor_copy(
                totals[:], ps1[0:1, 0:NT * E]), waits=[(pe, t1)], inc=dv)
            e2_ = ph.op("vector", lambda v: v.tensor_copy(
                Rl[:], ps2[:, 0:NT * E]), waits=[(pe, t2)], inc=dv)
            ph.op("vector", lambda v: v.memset(run[:], 0.0),
                  waits=[(dv, dv.n)], inc=dv)
            for tb in range(NT):
                ph.op("vector", lambda v, tb=tb: v.tensor_copy(
                    excl[:, E * tb:E * (tb + 1)], run[:]),
                    waits=[(dv, dv.n)], inc=dv)
                ph.op("vector", lambda v, tb=tb: v.tensor_add(
                    run[:], run[:], totals[:, E * tb:E * (tb + 1)]),
                    waits=[(dv, dv.n)], inc=dv)
            ph.op("vector", lambda v: v.tensor_copy(nrow[:], run[:]),
                  waits=[(dv, dv.n)], inc=dv)
            ph.op("vector", lambda v: v.tensor_scalar_add(pad[:], nrow[:], 511.0),
                  waits=[(dv, dv.n)], inc=dv)
            ph.op("vector", lambda v: v.tensor_copy(padi[:], pad[:]),
                  waits=[(dv, dv.n)], inc=dv)
            ph.op("vector", lambda v: v.tensor_scalar(
                padi[:], padi[:], 9, None, ALU.arith_shift_right),
                waits=[(dv, dv.n)], inc=dv)
            ph.op("vector", lambda v: v.tensor_scalar(
                padi[:], padi[:], 9, None, ALU.logical_shift_left),
                waits=[(dv, dv.n)], inc=dv)
            ph.op("vector", lambda v: v.tensor_copy(pad[:], padi[:]),
                  waits=[(dv, dv.n)], inc=dv)
            ph.op("vector", lambda v: v.memset(run[:], 0.0),
                  waits=[(dv, dv.n)], inc=dv)
            for e_ in range(E):
                ph.op("vector", lambda v, e_=e_: v.tensor_copy(
                    base[:, e_:e_ + 1], run[:, e_:e_ + 1] if False
                    else run[:, 0:1]), waits=[(dv, dv.n)], inc=dv)
                ph.op("vector", lambda v, e_=e_: v.tensor_add(
                    run[:, 0:1], run[:, 0:1], pad[:, e_:e_ + 1]),
                    waits=[(dv, dv.n)], inc=dv)
            ph.op("vector", lambda v: v.tensor_add(endr[:], base[:], pad[:]),
                  waits=[(dv, dv.n)], inc=dv)
            ph.op("vector", lambda v: v.memset(brep[:], 0.0),
                  waits=[(dv, dv.n)], inc=dv)
            for e_ in range(E):
                ph.op("vector", lambda v, e_=e_: v.tensor_scalar_add(
                    brep[0:1, :].rearrange("o (t e) -> o t e", e=E)[:, :, e_:e_ + 1],
                    brep[0:1, :].rearrange("o (t e) -> o t e", e=E)[:, :, e_:e_ + 1],
                    base[:, e_:e_ + 1]), waits=[(dv, dv.n)], inc=dv)
            ph.op("vector", lambda v: v.tensor_add(segrow[:], excl[:], brep[:]),
                  waits=[(dv, dv.n)], inc=dv)
            sgd = ph.op("vector", lambda v: v.memset(echk[:], 8.0),
                        waits=[(dv, dv.n)], inc=dv)
            for e_ in range(E):
                ph.op("vector", lambda v, e_=e_: v.tensor_scalar(
                    tmpc[:], cval_sb[:], base[:, e_:e_ + 1], None, ALU.is_ge),
                    waits=[(dv, dv.n)], inc=dv)
                ph.op("vector", lambda v, e_=e_: v.tensor_scalar(
                    tmpc2[:], cval_sb[:], endr[:, e_:e_ + 1], None, ALU.is_lt),
                    waits=[(dv, dv.n)], inc=dv)
                ph.op("vector", lambda v, e_=e_: v.tensor_mul(
                    tmpc[:], tmpc[:], tmpc2[:]), waits=[(dv, dv.n)], inc=dv)
                ph.op("vector", lambda v, e_=e_: v.scalar_tensor_tensor(
                    echk[:], tmpc[:], float(e_ - 8), echk[:], ALU.mult, ALU.add),
                    waits=[(dv, dv.n)], inc=dv)
            ph.op("vector", lambda v: v.tensor_scalar_min(
                echk[:], echk[:], 7.0), waits=[(dv, dv.n)], inc=dv)
            ecd = ph.op("vector", lambda v: v.tensor_copy(self.idxE_i[:], echk[:]),
                        waits=[(dv, dv.n)], inc=dv)
            b1_ = ph.op("tensor", lambda t: t.matmul(
                ps1[:, 0:NT * E], self.sb_ones32[:], segrow[:],
                start=True, stop=True), waits=[(dv, dv.n), (dv, e1_)], inc=pe)
            pa = ph.op("vector", lambda v: v.tensor_add(
                pall[:], ps1[:, 0:NT * E], Rl[:]),
                waits=[(pe, b1_), (dv, e2_), (dv, dv.n)], inc=dv)
            b2_ = ph.op("tensor", lambda t: t.matmul(
                ps2[:, 0:NCHK], self.sb_ones32[:], echk[:],
                start=True, stop=True), waits=[(dv, dv.n)], inc=pe)
            ph.op("vector", lambda v: v.tensor_scalar(
                self._idxWf_sb[:], ps2[:, 0:NCHK], 128.0, None, ALU.mult),
                waits=[(pe, b2_), (dv, dv.n)], inc=dv)
            ph.op("vector", lambda v: v.tensor_add(
                self._idxWf_sb[:], self._idxWf_sb[:], iotar[:]),
                waits=[(dv, dv.n)], inc=dv)
            iwd = ph.op("vector", lambda v: v.tensor_copy(
                self.idxW_i[:], self._idxWf_sb[:]), waits=[(dv, dv.n)], inc=dv)
            sc_done = {}
            idxg_v = self.idxg_hbm.ap().rearrange("(n o) -> n o", o=1)
            idxs_v = self.idxs_hbm.ap().rearrange("(n o) -> n o", o=1)
            for tb in range(NT):
                for sl in range(2):
                    it = tb * 2 + sl
                    wl0 = [(dv, dv.n)]
                    if it >= 2:
                        wl0.append((gp, gp.n))
                    pr = it % 2
                    oh = ph.op("vector", lambda v, tb=tb, sl=sl: v.tensor_scalar(
                        ohb[:], iota8_sb[:], i12[:, NT * sl + tb:NT * sl + tb + 1],
                        None, ALU.is_equal), waits=wl0, inc=dv)
                    ph.op("vector", lambda v, tb=tb: v.tensor_mul(
                        ohb[:], ohb[:], pall[:, E * tb:E * (tb + 1)]),
                        waits=[(dv, oh), (dv, pa)], inc=dv)
                    ph.op("vector", lambda v, pr=pr: v.reduce_sum(
                        posf2[pr][:], ohb[:], axis=AX.X), waits=[(dv, dv.n)], inc=dv)
                    ph.op("vector", lambda v, pr=pr: v.tensor_copy(
                        posi2[pr][:], posf2[pr][:]), waits=[(dv, dv.n)], inc=dv)
                    pid = ph.op("vector", lambda v, tb=tb, sl=sl, pr=pr:
                                v.tensor_scalar_add(
                        tv2[pr][:], iota_sb[:], float(128 * tb)),
                        waits=[(dv, dv.n)], inc=dv)
                    tsd = ph.op("vector", lambda v, sl=sl, pr=pr:
                                v.tensor_scalar_add(
                        ts2[pr][:], tv2[pr][:], float(self.T * sl)),
                        waits=[(dv, pid)], inc=dv)
                    dma_i = ph.op("gpsimd", lambda g, pr=pr: g.indirect_dma_start(
                        idxg_v, IndirectOffsetOnAxis(posi2[pr][:], 0),
                        tv2[pr][:], None),
                        waits=[(dv, tsd)], inc=gp, inc_n=16)
                    sc_done[it] = ph.op("gpsimd", lambda g, pr=pr:
                                        g.indirect_dma_start(
                        idxs_v, IndirectOffsetOnAxis(posi2[pr][:], 0),
                        ts2[pr][:], None),
                        waits=[], inc=gp, inc_n=16)
            ph.wait("gpsimd", gp, gp.n)
            ph.wait("sync", sp, sp.n)
            for j in range(2):
                ph.wait("sync", auxsem[j], auxsem[j].n)
                ph.wait("sync", nzsem[j], nzsem[j].n)
            ph.realize(nc)

    def phase_moe_sparse(self, sc):
        nc = self.nc
        sp, gp, pe, ac, dv = (self.sem[k] for k in ("sp", "gp", "pe", "ac", "dv"))
        NT, T_, NCHK = self.NT, self.T, self.NCHK
        w1sb = self.salloc(sc, "sw1", [P, DC * FF], BF16)
        w2sb = self.salloc(sc, "sw2", [P, FFC * D], BF16)
        h_sb = self.salloc(sc, "sh", [P, FFC * 512], BF16)
        xg = [self.salloc(sc, f"sxg{i}", [P, D], BF16) for i in range(4)]
        xTs = self.salloc(sc, "sxT", [P, DC * 512], BF16)
        idg = [self.salloc(sc, f"sidg{i}", [P, 4], F32) for i in range(2)]
        ids = [self.salloc(sc, f"sids{i}", [P, 4], F32) for i in range(2)]
        idgi = [self.salloc(sc, f"sidgi{i}", [P, 1], I32) for i in range(4)]
        idsi = [[self.salloc(sc, f"sidsi{j}_{i}", [P, 1], I32) for i in range(4)]
                for j in range(2)]
        idai = [self.salloc(sc, f"sidai{i}", [P, 1], I32) for i in range(4)]
        tmpa = self.salloc(sc, "stmpa", [P, 1], F32)
        tmpb = self.salloc(sc, "stmpb", [P, 1], F32)
        b1g = [self.salloc(sc, f"sb1g{i}", [P, FFC], F32) for i in range(2)]
        idxB = self.salloc(sc, "sidxB", [2, 1], I32)
        idxBf = self.salloc(sc, "sidxBf", [2, 1], F32)
        iota2 = self.salloc(sc, "siota2", [P, 1], F32)
        b2rb = [self.salloc(sc, f"sb2rb{i}", [2, D], BF16) for i in range(2)]
        auxw = [self.salloc(sc, f"sauxw{i}", [P, 4], F32) for i in range(2)]
        osc = [self.salloc(sc, f"sosc{i}", [P, D], F32) for i in range(4)]
        idsem = [self.S(f"sid{i}") for i in range(2)]
        gq = self.S("gq_w2")
        gx = self.S("gx_rows")
        wgsem = [self.S(f"swg{i}") for i in range(2)]
        xgsem = [self.S(f"sxg{i}") for i in range(2)]
        axsem = [self.S(f"sax{i}") for i in range(2)]
        scsem = [self.S(f"ssc{i}") for i in range(4)]
        w1v = self.w1bf.ap().rearrange("e p f -> (e p) f")
        w2v = self.w2bf.ap().rearrange("e p f -> (e p) f")
        b1v = self.ext["b1p"].ap().rearrange("e p f -> (e p) f")
        aux_v = self.aux2_hbm.ap().rearrange("t a -> (t a) 1" if False
                                             else "t a -> (t a)").rearrange(
                                                 "(n o) -> n o", o=1)
        ph = Phase("moeS")
        dma(ph, "sync", sp, iota2[:], self.iota_p.ap())
        iota_rdy = sp.n
        ph.wait("sync", sp, sp.n)

        with nc.psum_tensor("spst0", [P, P], BF16) as pt0, \
             nc.psum_tensor("spst1", [P, P], BF16) as pt1, \
             nc.psum_tensor("spsh0", [P, 512], F32) as h0, \
             nc.psum_tensor("spsh1", [P, 512], F32) as h1, \
             nc.psum_tensor("spso0", [P, 512], F32) as o0, \
             nc.psum_tensor("spso1", [P, 512], F32) as o1:
            pst, psH, psO = [pt0, pt1], [h0, h1], [o0, o1]
            w1_use = w2_use = 0
            idx_use = {}
            sc_st = {}
            self._gx_tgt, self._ga_tgt, self._sc_tgt = {}, {}, {}
            self._pst_free = {}
            self._idg_used = {}
            self._tp_done = {}
            self._pend_sc = []
            for c in range(NCHK):
                cp = c % 2
                wl = [(dv, self._idg_used.get(c - 2, 0))]
                for tbi in range(4):
                    dma(ph, "sync", idsem[cp], idg[cp][:, tbi:tbi + 1],
                        self.idxg_hbm.ap()[512 * c + 128 * tbi:
                                           512 * c + 128 * (tbi + 1)]
                        .rearrange("(p o) -> p o", o=1),
                        waits=(wl if tbi == 0 else []))
                for tbi in range(4):
                    dma(ph, "sync", idsem[cp], ids[cp][:, tbi:tbi + 1],
                        self.idxs_hbm.ap()[512 * c + 128 * tbi:
                                           512 * c + 128 * (tbi + 1)]
                        .rearrange("(p o) -> p o", o=1))
                id_tgt = idsem[cp].n
                tp_ev = {}
                for tbi in range(4):
                    c0 = ph.op("vector", lambda v, cp=cp, tbi=tbi: v.tensor_copy(
                        idgi[tbi][:], idg[cp][:, tbi:tbi + 1]),
                        waits=[(idsem[cp], id_tgt), (dv, dv.n),
                               (gx, gx.n)], inc=dv)
                    c1 = ph.op("vector", lambda v, cp=cp, tbi=tbi: v.tensor_copy(
                        idsi[cp][tbi][:], ids[cp][:, tbi:tbi + 1]),
                        waits=[(dv, dv.n),
                               (scsem[tbi], self._sc_tgt.get((c - 2, tbi), 0))],
                        inc=dv)
                    self._c1t = getattr(self, "_c1t", {})
                    self._c1t[(c, tbi)] = c1
                    d1 = ph.op("vector", lambda v, cp=cp, tbi=tbi: v.tensor_sub(
                        tmpa[:], ids[cp][:, tbi:tbi + 1], idg[cp][:, tbi:tbi + 1]),
                        waits=[(dv, dv.n)], inc=dv)
                    d2 = ph.op("vector", lambda v, tbi=tbi: v.tensor_scalar(
                        tmpb[:], tmpa[:], 1.0 / self.T, None, ALU.mult),
                        waits=[(dv, d1)], inc=dv)
                    d3 = ph.op("vector", lambda v, cp=cp, tbi=tbi:
                               v.scalar_tensor_tensor(
                        tmpa[:], idg[cp][:, tbi:tbi + 1], 2.0, tmpb[:],
                        ALU.mult, ALU.add), waits=[(dv, d2)], inc=dv)
                    d4 = ph.op("vector", lambda v, tbi=tbi: v.tensor_copy(
                        idai[tbi][:], tmpa[:]),
                        waits=[(dv, d3), (gx, gx.n)], inc=dv)
                    gxt = ph.op("gpsimd", lambda g, cp=cp, tbi=tbi:
                               g.indirect_dma_start(
                        xg[tbi][:], None, self.x1fbf.ap(),
                        IndirectOffsetOnAxis(idgi[tbi][:], 0)),
                        waits=[(dv, c0),
                               (pe, self._tp_done.get((c - 1, tbi), 0))],
                        inc=gx, inc_n=16)
                    self._gx_tgt[(c, tbi)] = gxt
                    wla = [(dv, d4)]
                    if tbi == 0:
                        wla.append((ac, sc_st.get((c - 2, "aux"), 0)))
                    ga = ph.op("gpsimd", lambda g, cp=cp, tbi=tbi:
                               g.indirect_dma_start(
                        auxw[cp][:, tbi:tbi + 1], None, aux_v,
                        IndirectOffsetOnAxis(idai[tbi][:], 0)),
                        waits=wla, inc=gx, inc_n=16)
                    self._ga_tgt[(c, tbi)] = ga
                    gx_t = gx.n
                    for ic in range(DC):
                        wlt = [(gx, gx_t)] if ic == 0 else []
                        if (ic % 2) in self._pst_free:
                            wlt = wlt + [self._pst_free[ic % 2]]
                        ph.op("tensor", lambda t, tbi=tbi, ic=ic: t.transpose(
                            pst[ic % 2][:], xg[tbi][:, P * ic:P * (ic + 1)],
                            self.sb_ident[:]), waits=wlt, inc=pe)
                        ev = ph.op("scalar", lambda s2, tbi=tbi, ic=ic: s2.copy(
                            xTs[:, 512 * ic + P * tbi:512 * ic + P * (tbi + 1)],
                            pst[ic % 2][:]), waits=[(pe, pe.n)], inc=ac)
                        self._pst_free[ic % 2] = (ac, ev)
                    tp_ev[(c, tbi)] = pe.n
                    self._tp_done[(c, tbi)] = pe.n
                wg1 = ph.op("gpsimd", lambda g, c=c: g.indirect_dma_start(
                    w1sb[:], None, w1v, IndirectOffsetOnAxis(
                        self.idxW_i[:, c:c + 1], 0)),
                    waits=[(pe, w1_use)], inc=gp, inc_n=16)
                bg1 = ph.op("gpsimd", lambda g, c=c, cp=cp: g.indirect_dma_start(
                    b1g[cp][:], None, b1v, IndirectOffsetOnAxis(
                        self.idxW_i[:, c:c + 1], 0)),
                    waits=[(ac, sc_st.get((c - 2, "b1"), 0))], inc=gp, inc_n=16)
                bi1 = ph.op("vector", lambda v, c=c: v.tensor_sub(
                    idxBf[:], self._idxWf_sb[0:2, c:c + 1], iota2[0:2, :]),
                    waits=[(dv, dv.n), (sp, iota_rdy), (gp, gp.n)], inc=dv)
                bi2 = ph.op("vector", lambda v: v.tensor_scalar(
                    idxBf[:], idxBf[:], 1.0 / 128.0, None, ALU.mult),
                    waits=[(dv, bi1)], inc=dv)
                bi3 = ph.op("vector", lambda v: v.tensor_copy(
                    idxB[:], idxBf[:]), waits=[(dv, bi2)], inc=dv)
                bg2 = ph.op("gpsimd", lambda g, c=c, cp=cp: g.indirect_dma_start(
                    b2rb[cp][:], None, self.ext["b2"].ap(), IndirectOffsetOnAxis(
                        idxB[:], 0)),
                    waits=[(dv, bi3), (pe, sc_st.get((c - 2, "b2use"), 0))],
                    inc=gp, inc_n=16)
                for fgo in self._pend_sc:
                    fgo()
                self._pend_sc = []
                wg2 = ph.op("gpsimd", lambda g, c=c: g.indirect_dma_start(
                    w2sb[:], None, w2v, IndirectOffsetOnAxis(
                        self.idxW_i[:, c:c + 1], 0)),
                    waits=[(pe, w2_use)], inc=gq, inc_n=16)
                gp_w = gp.n
                self._idg_used[c] = dv.n
                gp_now = gp.n
                gx_now = gx.n
                xT_rdy = ac.n
                hev = {}
                for f in range(FFC):
                    wlf = []
                    if f == 0:
                        wlf = [(gp, gp_now), (ac, xT_rdy)]
                    if f >= 2:
                        wlf.append((ac, hev[f - 2]))
                    for ic in range(DC):
                        ph.op("tensor", lambda t, f=f, ic=ic: t.matmul(
                            psH[f % 2][:, 0:512],
                            w1sb[:, FF * ic + P * f:FF * ic + P * (f + 1)],
                            xTs[:, 512 * ic:512 * (ic + 1)],
                            start=(ic == 0), stop=(ic == DC - 1)),
                            waits=(wlf if ic == 0 else []),
                            inc=(pe if ic == DC - 1 else None))
                    mmt = pe.n
                    wl2 = [(pe, mmt)]
                    if f == 0:
                        wl2.append((gp, gp_now))
                    hev[f] = ph.op("scalar", lambda s2, f=f, cp=cp: s2.activation(
                        h_sb[:, 512 * f:512 * (f + 1)], psH[f % 2][:, 0:512],
                        AF.Relu, bias=b1g[cp][:, f:f + 1]),
                        waits=wl2, inc=ac)
                w1_use = pe.n
                h_done = hev[FFC - 1]
                sc_st[(c, "b1")] = h_done
                sttv = {}
                for tbi in range(4):
                    for dh in range(2):
                        gi = tbi * 2 + dh
                        wl3 = []
                        if gi == 0:
                            wl3 = [(ac, h_done), (gq, gq.n)]
                        if gi >= 2:
                            wl3.append((ac, sttv[gi - 2]))
                        for f in range(FFC):
                            ph.op("tensor", lambda t, tbi=tbi, dh=dh, f=f, gi=gi:
                                  t.matmul(
                                psO[gi % 2][:, 0:512],
                                h_sb[:, 512 * f + P * tbi:512 * f + P * (tbi + 1)],
                                w2sb[:, D * f + 512 * dh:D * f + 512 * (dh + 1)],
                                start=(f == 0), stop=False),
                                waits=(wl3 if f == 0 else []), inc=None)
                        ph.op("tensor", lambda t, dh=dh, gi=gi, cp=cp: t.matmul(
                            psO[gi % 2][:, 0:512], self.sb_ones[:],
                            b2rb[cp][0:1, 512 * dh:512 * (dh + 1)],
                            start=False, stop=True),
                            waits=[(gp, gp_w)] if gi == 0 else [],
                            inc=pe)
                        mmt = pe.n
                        wl4 = [(pe, mmt), (gx, gx_now)]
                        if dh == 0:
                            wl4.append((scsem[tbi],
                                        self._sc_tgt.get((c - 1, tbi), 0)))
                        sttv[gi] = ph.op(
                            "scalar", lambda s2, tbi=tbi, dh=dh, gi=gi, cp=cp:
                            s2.activation(
                                osc[tbi][:, 512 * dh:512 * (dh + 1)],
                                psO[gi % 2][:, 0:512], AF.Identity,
                                scale=auxw[cp][:, tbi:tbi + 1]),
                            waits=wl4, inc=ac)
                    def mk_scatter(c=c, cp=cp, tbi=tbi, ev=sttv[tbi * 2 + 1],
                                   c1e=self._c1t[(c, tbi)]):
                        def go():
                            self._sc_tgt[(c, tbi)] = ph.op(
                                "gpsimd", lambda g: g.indirect_dma_start(
                                    self.moe2_hbm.ap(),
                                    IndirectOffsetOnAxis(idsi[cp][tbi][:], 0),
                                    osc[tbi][:], None),
                                waits=[(ac, ev), (dv, c1e)],
                                inc=scsem[tbi], inc_n=16)
                        return go
                    self._pend_sc.append(mk_scatter())
                w2_use = pe.n
                sc_st[(c, "aux")] = sttv[7]
                sc_st[(c, "b2use")] = pe.n
                idx_use[c] = pe.n
            for fgo in self._pend_sc:
                fgo()
            self._pend_sc = []
            ph.wait("gpsimd", gp, gp.n)
            ph.wait("gpsimd", gq, gq.n)
            ph.wait("gpsimd", gx, gx.n)
            for j in range(4):
                ph.wait("gpsimd", scsem[j], scsem[j].n)
            for j in range(2):
                ph.wait("sync", idsem[j], idsem[j].n)
            ph.realize(nc)

    def phase_final(self, sc):
        nc = self.nc
        sp, gp, pe, ac, dv = (self.sem[k] for k in ("sp", "gp", "pe", "ac", "dv"))
        NT = self.NT
        mo = [self.salloc(sc, f"fmo{i}", [P, D], F32) for i in range(2)]
        mo2 = [self.salloc(sc, f"fmo2{i}", [P, D], F32) for i in range(2)]
        x1fe = [self.salloc(sc, f"x1fe{i}", [P, D], F32) for i in range(2)]
        x1e = [self.salloc(sc, f"x1e{i}", [P, D], F32) for i in range(2)]
        fb = self.salloc(sc, "fb", [P, D], F32)
        yb = [self.salloc(sc, f"yb{i}", [P, D], F32) for i in range(2)]
        sqm = self.salloc(sc, "sqm", [P, D], F32)
        statm = [self.salloc(sc, f"mst{i}", [P, 8], F32) for i in range(4)]
        g2bc = self.salloc(sc, "g2bcl", [P, D], BF16)
        be2bc = self.salloc(sc, "be2bcl", [P, D], BF16)
        mosem = [self.S(f"fmo{i}") for i in range(2)]
        x1fesem = [self.S(f"x1fe{i}") for i in range(2)]
        x1esem = [self.S(f"x1e{i}") for i in range(2)]
        ysem = [self.S(f"yst{i}") for i in range(2)]
        ph = Phase("final")
        dma(ph, "sync", sp, g2bc[:], self.gb_hbm.ap()[2 * P:3 * P, :])
        dma(ph, "sync", sp, be2bc[:], self.gb_hbm.ap()[3 * P:4 * P, :])
        ph.wait("sync", sp, sp.n)
        ystore, chend = {}, {}
        for tb in range(NT):
            w = [(dv, chend[tb - 2])] if tb >= 2 else []
            mot = dma(ph, "sync", mosem[tb % 2], mo[tb % 2][:],
                      self.moe2_hbm.ap()[tb * P:(tb + 1) * P, :], waits=w)
            mot = dma(ph, "sync", mosem[tb % 2], mo2[tb % 2][:],
                      self.moe2_hbm.ap()[self.T + tb * P:self.T + (tb + 1) * P, :])
            xft = dma(ph, "sync", x1fesem[tb % 2], x1fe[tb % 2][:],
                      self.x1ff32.ap()[tb * P:(tb + 1) * P, :])
            x1t = dma(ph, "sync", x1esem[tb % 2], x1e[tb % 2][:],
                      self.x1f32.ap()[tb * P:(tb + 1) * P, :])
            ad = ph.op("vector", lambda v, tb=tb: v.tensor_add(
                mo[tb % 2][:], mo[tb % 2][:], mo2[tb % 2][:]),
                waits=[(dv, dv.n), (mosem[tb % 2], mot)], inc=dv)
            sd = ph.op("vector", lambda v, tb=tb: v.scalar_tensor_tensor(
                fb[:], mo[tb % 2][:], self.recipZ[:, tb:tb + 1], x1fe[tb % 2][:],
                ALU.mult, ALU.add),
                waits=[(dv, ad), (x1fesem[tb % 2], xft)], inc=dv)
            self._ln_tile(ph, src=fb[:], pre_wait=[(dv, sd)],
                          sq=sqm, stat=statm[tb % 2], out=fb[:],
                          out_guard=(dv, dv.n))
            g2d = ph.op("vector", lambda v: v.tensor_mul(
                fb[:], fb[:], g2bc[:]), waits=[(dv, dv.n)], inc=dv)
            b2d = ph.op("vector", lambda v: v.tensor_add(
                fb[:], fb[:], be2bc[:]), waits=[(dv, g2d)], inc=dv)
            self._ln_tile(ph, src=fb[:], pre_wait=[(dv, b2d)],
                          sq=sqm, stat=statm[2 + tb % 2], out=fb[:],
                          out_guard=(dv, dv.n))
            yd = ph.op("vector", lambda v, tb=tb: v.tensor_add(
                yb[tb % 2][:], x1e[tb % 2][:], fb[:]),
                waits=[(dv, dv.n), (x1esem[tb % 2], x1t),
                       (ysem[tb % 2], ystore.get(tb - 2, 0))], inc=dv)
            chend[tb] = dv.n
            ystore[tb] = dma(ph, "sync", ysem[tb % 2],
                             self.y.ap()[tb * P:(tb + 1) * P, :],
                             yb[tb % 2][:], waits=[(dv, yd)])
        for j in range(2):
            ph.wait("sync", ysem[j], ysem[j].n)
        ph.realize(nc)

    def _pc_done_tgt(self):
        return self.sem["gp"].n


def kernel(**inputs):
    T = 4096
    b = Builder(T)
    nc = b.build()
    names = {"x": None}
    base = {
        "aWq": inputs["L0_attn_Wq"], "abq": inputs["L0_attn_bq"],
        "aWk": inputs["L0_attn_Wk"], "abk": inputs["L0_attn_bk"],
        "aWv": inputs["L0_attn_Wv"], "abv": inputs["L0_attn_bv"],
        "aWo": inputs["L0_attn_Wo"], "abo": inputs["L0_attn_bo"],
        "fWdkv": inputs["L0_ffn_Wdkv"], "fbdkv": inputs["L0_ffn_bdkv"],
        "fWuk": inputs["L0_ffn_Wuk"], "fbuk": inputs["L0_ffn_buk"],
        "fWuv": inputs["L0_ffn_Wuv"], "fbuv": inputs["L0_ffn_buv"],
        "fWdq": inputs["L0_ffn_Wdq"], "fbdq": inputs["L0_ffn_bdq"],
        "fWuq": inputs["L0_ffn_Wuq"], "fbuq": inputs["L0_ffn_buq"],
        "fWo": inputs["L0_ffn_Wo"], "fbo": inputs["L0_ffn_bo"],
        "g1": inputs["L0_ffn_gamma1"], "be1": inputs["L0_ffn_beta1"],
        "g2": inputs["L0_ffn_gamma2"], "be2": inputs["L0_ffn_beta2"],
        "Wg": inputs["L0_ffn_Wg"], "bg": inputs["L0_ffn_bg"],
        "Wn": inputs["L0_ffn_Wn"], "bn": inputs["L0_ffn_bn"],
        "W1t": np.asarray(inputs["L0_ffn_W1"], dtype=np.float32).reshape(
            E, DC, P, FF).transpose(0, 2, 1, 3).reshape(E, P, DC * FF),
        "b1p": np.asarray(inputs["L0_ffn_b1"], dtype=np.float32).reshape(
            E, FFC, P).transpose(0, 2, 1),
        "W2t": np.asarray(inputs["L0_ffn_W2"], dtype=np.float32).reshape(
            E, FFC, P, D).transpose(0, 2, 1, 3).reshape(E, P, FFC * D),
        "b2": inputs["L0_ffn_b2"],
    }
    base = {k: np.ascontiguousarray(np.asarray(v, dtype=np.float32))
            for k, v in base.items()}
    import jax
    import jax.numpy as jnp
    with jax.default_device(jax.devices("cpu")[0]):
        key = jax.random.wrap_key_data(jnp.array([0, 42], dtype=jnp.uint32),
                                       impl="threefry2x32")
        noise = np.asarray(jax.random.normal(key, (8, T, E), dtype=np.float32))
    xs = np.ascontiguousarray(np.asarray(inputs["L0_attn_hidden_states"],
                                         dtype=np.float32))
    in_maps = []
    for c in range(8):
        m = dict(base)
        m["x"] = xs[c]
        m["noise"] = np.ascontiguousarray(noise[c])
        in_maps.append(m)
    res = run_bass_kernel_spmd(nc, in_maps, core_ids=list(range(8)))
    return np.stack([np.asarray(r["y"]) for r in res.results], axis=0)


# version 12
# speedup vs baseline: 1.2225x; 1.1014x over previous
import numpy as np
import ml_dtypes
import concourse.bass as bass
import concourse.mybir as mybir
from concourse.bass import IndirectOffsetOnAxis
from concourse.bass_utils import run_bass_kernel_spmd

F32 = mybir.dt.float32
BF16 = mybir.dt.bfloat16
AF = mybir.ActivationFunctionType
ALU = mybir.AluOpType
AX = mybir.AxisListType
BF = ml_dtypes.bfloat16

P = 128
D = 1024
DC = D // P
NH = 16
DH = 64
NHP = NH // 2
L = 512
LC = L // P
E = 8
FF = 4096
FFC = FF // P
QSB = 512
NEG_A = -10000.0
NEG_B = -9e15
EPS = 1e-5
SCALE = 0.125
I32 = mybir.dt.int32



def _sparse_mask_bool(T):
    pos = np.arange(T)
    dist = pos[None, :] - pos[:, None]
    window = np.abs(dist) <= 64
    num_global = max(1, int(T * 0.1))
    isg = pos < num_global
    return window | isg[:, None] | isg[None, :]


def _causal_mask_bool(T):
    return np.tril(np.ones((T, T), dtype=bool))


def _classify(mask_qk, T, neg):
    mb = mask_qk.T
    nq, nk = T // QSB, T // P
    tiles, index, cls, kb_lists = [], {}, {}, []
    for qsb in range(nq):
        lst = []
        for kb in range(nk):
            blk = mb[kb * P:(kb + 1) * P, qsb * QSB:(qsb + 1) * QSB]
            if not blk.any():
                continue
            lst.append(kb)
            if blk.all():
                cls[(qsb, kb)] = 'clean'
            else:
                key = blk.tobytes()
                if key not in index:
                    index[key] = len(tiles)
                    tiles.append(np.where(blk, 0.0, neg).astype(np.float32))
                cls[(qsb, kb)] = ('tile', index[key])
        kb_lists.append(lst)
    return tiles, cls, kb_lists



class Sem:
    def __init__(self, nc, name):
        self.h = nc.alloc_semaphore(name)
        self.n = 0
        self.name = name

    def bump(self, k):
        self.n += k
        return self.n


class Phase:
    ENGS = ("tensor", "scalar", "vector", "sync", "gpsimd")

    def __init__(self, name):
        self.name = name
        self.ops = {e: [] for e in self.ENGS}

    def op(self, eng, fn, waits=(), inc=None, inc_n=1):
        tgt = inc.bump(inc_n) if inc is not None else None
        waits = tuple((s, v) for (s, v) in waits)

        def _f(e, fn=fn, waits=waits, inc=inc, inc_n=inc_n):
            for s, v in waits:
                e.wait_ge(s.h, v)
            ins = fn(e)
            if inc is not None:
                ins.then_inc(inc.h, inc_n)
        self.ops[eng].append(_f)
        return tgt

    def wait(self, eng, sem, val):
        self.ops[eng].append(lambda e, s=sem, v=val: e.wait_ge(s.h, v))

    def realize(self, nc):
        with nc.Block(self.name) as blk:
            for eng in self.ENGS:
                lst = self.ops[eng]
                if not lst:
                    continue
                getattr(blk, eng)(lambda e, lst=lst: [f(e) for f in lst])


def dma(ph, eng, sem, dst, src, waits=(), **kw):
    sem.eng = eng
    return ph.op(eng, lambda e: e.dma_start(dst, src, **kw), waits=waits,
                 inc=sem, inc_n=16)



class Builder:
    def __init__(self, T=4096, dbg=False):
        self.dbg = dbg
        self.T = T
        self.NT = T // P
        self.NQ = T // QSB
        nc = bass.Bass("TRN2", target_bir_lowering=False)
        self.nc = nc
        self.sem = {k: Sem(nc, k) for k in ("sp", "gp", "pe", "ac", "dv")}
        self._semcache = {}

        ext = {}
        def inp(name, shape):
            ext[name] = nc.dram_tensor(name, list(shape), F32, kind="ExternalInput")
            return ext[name]
        self.x = inp("x", (T, D))
        self.noise = inp("noise", (T, E))
        for nm, shp in [
            ("aWq", (D, D)), ("abq", (D,)), ("aWk", (D, D)), ("abk", (D,)),
            ("aWv", (D, D)), ("abv", (D,)), ("aWo", (D, D)), ("abo", (D,)),
            ("fWdkv", (D, L)), ("fbdkv", (L,)), ("fWuk", (L, D)), ("fbuk", (D,)),
            ("fWuv", (L, D)), ("fbuv", (D,)), ("fWdq", (D, L)), ("fbdq", (L,)),
            ("fWuq", (L, D)), ("fbuq", (D,)), ("fWo", (D, D)), ("fbo", (D,)),
            ("g1", (D,)), ("be1", (D,)), ("g2", (D,)), ("be2", (D,)),
            ("Wg", (D, E)), ("bg", (E,)), ("Wn", (D, E)), ("bn", (E,)),
            ("W1t", (E, P, DC * FF)), ("b1p", (E, P, FFC)), ("W2t", (E, P, FFC * D)), ("b2", (E, D)),
        ]:
            inp(nm, shp)
        self.ext = ext
        self.y = nc.dram_tensor("y", [T, D], F32, kind="ExternalOutput")

        T_ = self.T
        dt_ = lambda n, s, d: nc.dram_tensor(
            n, list(s), d, kind="ExternalOutput" if self.dbg else "Internal")
        self.ln0bf = dt_("ln0bf", (T_, D), BF16)
        self.ln1bf = dt_("ln1bf", (T_, D), BF16)
        self.x1f32 = dt_("x1f32", (T_, D), F32)
        self.x1ff32 = dt_("x1ff32", (T_, D), F32)
        self.x1fbf = dt_("x1fbf", (T_, D), BF16)
        self.x1fT_hbm = dt_("x1fT_hbm", (DC, P, T_), BF16)
        self.qt_hbm = dt_("qt_hbm", (NHP, P, T_), BF16)
        self.kt_hbm = dt_("kt_hbm", (NHP, P, T_), BF16)
        self.v_hbm = dt_("v_hbm", (NHP, P, 2 * 65 * self.NT), BF16)
        self.kt2_hbm = dt_("kt2_hbm", (NHP, P, T_), BF16)
        self.qt2_hbm = dt_("qt2_hbm", (NHP, P, T_), BF16)
        self.v2_hbm = dt_("v2_hbm", (NHP, P, 2 * 65 * self.NT), BF16)
        self.moe_hbm = dt_("moe_hbm", (T_, D), F32)
        self.NCHK = (2 * T_ + 8 * 511 + 511) // 512
        self.moe2_hbm = dt_("moe2_hbm", (2 * T_ + 2048, D), F32)
        self.aux2_hbm = dt_("aux2_hbm", (T_, 2), F32)
        self.gb_hbm = dt_("gb_hbm", (4 * P, D), BF16)
        self.idxg_hbm = dt_("idxg_hbm", (self.NCHK * 512,), F32)
        self.idxs_hbm = dt_("idxs_hbm", (self.NCHK * 512,), F32)
        self.w1bf = dt_("w1bf", (E, P, DC * FF), BF16)
        self.w2bf = dt_("w2bf", (E, P, FFC * D), BF16)

        self.ident_bf = nc.inline_tensor(np.eye(P, dtype=BF), "ident_bf")
        self.ident_f32 = nc.inline_tensor(np.eye(P, dtype=np.float32), "ident_f32")
        self.ones_bf = nc.inline_tensor(np.ones((1, P), dtype=BF), "ones_bf")
        self.ones_f32 = nc.inline_tensor(np.ones((1, P), dtype=np.float32), "ones_f32")
        self.ltu_bf = nc.inline_tensor(np.triu(np.ones((P, P), dtype=BF), 1), "ltu_bf")
        self.iota_p = nc.inline_tensor(np.arange(P, dtype=np.float32).reshape(P, 1),
                                       "iota_p")
        self.iota8 = nc.inline_tensor(
            np.tile(np.arange(E, dtype=np.float32), (P, 1)), "iota8")
        self.cvals = nc.inline_tensor(
            (512.0 * np.arange(self.NCHK, dtype=np.float32)).reshape(1, self.NCHK),
            "cvals")
        self.padg_c = nc.inline_tensor(np.tile(
            np.arange(P, dtype=np.float32).reshape(P, 1),
            (1, self.NCHK * 4)), "padg_c")
        self.pads_c = nc.inline_tensor(
            (2.0 * T_ + (np.arange(P * self.NCHK * 4) % 2048))
            .astype(np.float32).reshape(P, self.NCHK * 4), "pads_c")
        self.iota_pr = nc.inline_tensor(
            np.tile(np.arange(P, dtype=np.float32).reshape(P, 1),
                    (1, self.NCHK)), "iota_pr")
        tA, cA, kA = _classify(_sparse_mask_bool(T_), T_, NEG_A)
        tB, cB, kB = _classify(_causal_mask_bool(T_), T_, NEG_B)
        self.maskA = (nc.inline_tensor(np.stack(tA).reshape(len(tA) * P, QSB), "maskA")
                      if tA else None)
        self.maskB = (nc.inline_tensor(np.stack(tB).reshape(len(tB) * P, QSB), "maskB")
                      if tB else None)
        self.nmA, self.clsA, self.kbA = len(tA), cA, kA
        self.nmB, self.clsB, self.kbB = len(tB), cB, kB

        A = nc.alloc_sbuf_tensor
        self.sb_ident = A("sb_ident", [P, P], BF16)
        self.sb_ident32 = A("sb_ident32", [P, P], F32)
        self.sb_ones = A("sb_ones", [1, P], BF16)
        self.sb_ones32 = A("sb_ones32", [1, P], F32)
        self.b_bqT = A("b_bqT", [P, DC], F32)
        self.b_bkT = A("b_bkT", [P, DC], F32)
        self.b_bdkvT = A("b_bdkvT", [P, LC], F32)
        self.b_bdqT = A("b_bdqT", [P, LC], F32)
        self.b_bukT = A("b_bukT", [P, DC], F32)
        self.b_buqT = A("b_buqT", [P, DC], F32)
        self.r_bv = A("r_bv", [1, D], BF16)
        self.r_bo = A("r_bo", [1, D], BF16)
        self.r_buv = A("r_buv", [1, D], BF16)
        self.r_fbo = A("r_fbo", [1, D], BF16)
        self.r_bg = A("r_bg", [1, E], BF16)
        self.r_bn = A("r_bn", [1, E], BF16)
        self.recipZ = A("recipZ", [P, self.NT], F32)
        self.idxW_i = A("idxW_i", [P, (2 * self.T + 8 * 511 + 511) // 512], I32)
        self._idxWf_sb = A("idxWf_p", [P, (2 * self.T + 8 * 511 + 511) // 512], F32)
        self.idxE_i = A("idxE_i", [1, (2 * self.T + 8 * 511 + 511) // 512], I32)
        self.cst = A("cst", [P, 2], F32)
        self._pc_jobs = None
        self._alive = []

    def _preconvert_jobs(self):
        for e in range(E):
            for g in range(DC):
                src = self.ext["W1t"].ap()[e, :, g * FF:(g + 1) * FF]
                dst = self.w1bf.ap()[e, :, g * FF:(g + 1) * FF]
                yield (src, dst, [P, FF])
            for g in range(E):
                src = self.ext["W2t"].ap()[e, :, g * FF:(g + 1) * FF]
                dst = self.w2bf.ap()[e, :, g * FF:(g + 1) * FF]
                yield (src, dst, [P, FF])

    def emit_preconvert(self, ph, count):
        if self._pc_jobs is None:
            self._pc_jobs = self._preconvert_jobs()
            self._pc_store_tgt = [0, 0]
            self._pc_i = 0
        gp = self.sem["gp"]
        for _ in range(count):
            job = next(self._pc_jobs, None)
            if job is None:
                return
            src, dst, shp = job
            i = self._pc_i
            st = self.pcs[i % 2][:, 0:shp[1]]
            if len(src.shape) == 3:
                st = st.rearrange("p (a n) -> p a n", a=src.shape[1])
            ld = dma(ph, "gpsimd", gp, st, src, waits=[(gp, gp.n)])
            dma(ph, "gpsimd", gp, dst, st, waits=[(gp, ld)])
            self._pc_i = i + 1

    def build(self):
        from contextlib import ExitStack
        self.phase_setup()
        self.scP = ExitStack()
        self.pcs = [self.scP.enter_context(
            self.nc.sbuf_tensor(f"pcs{i}", [P, FF], BF16)) for i in range(2)]
        with ExitStack() as scA:
            self.scA = scA
            self.phase_ln0()
            self.phase_qkvA()
        with ExitStack() as scB1:
            self.ctxT = scB1.enter_context(
                self.nc.sbuf_tensor("ctxT", [P, NHP * self.T], BF16))
            with ExitStack() as sc:
                self.phase_attn(sc, "A")
            with ExitStack() as sc:
                self.phase_woA(sc)
        with ExitStack() as scC:
            self.ckvT = scC.enter_context(
                self.nc.sbuf_tensor("ckvT", [P, LC * self.T], BF16))
            self.cqT = scC.enter_context(
                self.nc.sbuf_tensor("cqT", [P, LC * self.T], BF16))
            with ExitStack() as sc:
                self.phase_dkv(sc)
            with ExitStack() as sc:
                self.phase_uproj(sc)
        with ExitStack() as scB2:
            self.ctxT = scB2.enter_context(
                self.nc.sbuf_tensor("ctxT2", [P, NHP * self.T], BF16))
            with ExitStack() as sc:
                self.phase_attn(sc, "B")
            with ExitStack() as sc:
                self.phase_woB(sc)
        with ExitStack() as sc:
            self.phase_x1fT(sc)
        self.scP.close()
        with ExitStack() as sc:
            self.phase_gate(sc)
        with ExitStack() as sc:
            self.phase_moe_sparse(sc)
        with ExitStack() as sc:
            self.phase_final(sc)
        self.finish()
        return self.nc

    def salloc(self, stack, name, shape, dtype):
        return stack.enter_context(self.nc.sbuf_tensor(name, shape, dtype))

    def finish(self):
        ph = Phase("finish")
        for sem in list(self.sem.values()) + list(self._semcache.values()):
            if sem.n > 0 and getattr(sem, "eng", None):
                ph.wait(sem.eng, sem, sem.n)
        ph.realize(self.nc)

    def S(self, name):
        if name not in self._semcache:
            self._semcache[name] = Sem(self.nc, name)
        return self._semcache[name]

    def alloc(self, name, shape, dtype):
        t = self.nc.alloc_sbuf_tensor(name, shape, dtype)
        self._alive.append(t)
        return t

    def phase_setup(self):
        nc, sem = self.nc, self.sem
        sp, gp, pe, ac, dv = (sem[k] for k in ("sp", "gp", "pe", "ac", "dv"))
        ph = Phase("setup")
        ph.op("vector", lambda v: v.memset(self.cst[:, 0:1], EPS), inc=dv)
        ph.op("vector", lambda v: v.memset(self.cst[:, 1:2], -1.0), inc=dv)
        dma(ph, "sync", sp, self.sb_ident[:], self.ident_bf.ap())
        dma(ph, "sync", sp, self.sb_ident32[:], self.ident_f32.ap())
        dma(ph, "sync", sp, self.sb_ones[:], self.ones_bf.ap())
        dma(ph, "sync", sp, self.sb_ones32[:], self.ones_f32.ap())
        for dst, src in [(self.r_bv, "abv"), (self.r_bo, "abo"), (self.r_buv, "fbuv"),
                         (self.r_fbo, "fbo"), (self.r_bg, "bg"), (self.r_bn, "bn")]:
            dma(ph, "gpsimd", gp, dst[:], self.ext[src].ap().rearrange("(o d) -> o d", o=1))
        from contextlib import ExitStack
        self._scS = ExitStack()
        stage = self.salloc(self._scS, "bias_stage", [P, P], F32)
        rows = self.salloc(self._scS, "grows", [1, 4 * D], F32)
        gbc = self.salloc(self._scS, "gbc_tmp", [P, D], BF16)
        for i, nm in enumerate(("g1", "be1", "g2", "be2")):
            dma(ph, "sync", sp, rows[:, i * D:(i + 1) * D],
                self.ext[nm].ap().rearrange("(o d) -> o d", o=1))
        jobs = [(self.b_bqT, "abq", DC), (self.b_bkT, "abk", DC),
                (self.b_bdkvT, "fbdkv", LC), (self.b_bdqT, "fbdq", LC),
                (self.b_bukT, "fbuk", DC), (self.b_buqT, "fbuq", DC)]

        with nc.psum_tensor("ps_setup", [P, P], F32) as pst, \
             nc.psum_tensor("ps_bc", [P, 512], F32) as psb:
            tp_tgt = 0
            for dst, src, n in jobs:
                if isinstance(src, tuple):
                    srcap = self.ext[src[0]].ap()[src[1]].rearrange("(c p) -> c p", p=P)
                else:
                    srcap = self.ext[src].ap().rearrange("(c p) -> c p", p=P)
                tgt = dma(ph, "sync", sp, stage[0:n, :], srcap)
                ph.ops["sync"].insert(len(ph.ops["sync"]) - 1,
                    (lambda e, s_=pe, v=tp_tgt: e.wait_ge(s_.h, v)))
                tp_tgt = ph.op("tensor", lambda t, n=n: t.transpose(
                    pst[:, 0:n], stage[0:n, :], self.sb_ident32[0:n, 0:n]),
                    waits=[(sp, tgt), (ac, ac.n)], inc=pe)
                if dst is None:
                    e_ = src[1]
                    out = self.b_b1T[:, e_ * FFC:(e_ + 1) * FFC]
                else:
                    out = dst[:, 0:n]
                ph.op("scalar", lambda s, out=out, n=n: s.copy(out, pst[:, 0:n]),
                      waits=[(pe, pe.n)], inc=ac)
            grow_tgt = sp.n
            for i in range(4):
                for h2 in range(2):
                    ph.op("tensor", lambda t, i=i, h2=h2: t.matmul(
                        psb[:, 0:512], self.sb_ones32[:],
                        rows[:, i * D + h2 * 512: i * D + (h2 + 1) * 512],
                        start=True, stop=True),
                        waits=[(sp, grow_tgt), (dv, dv.n), (sp, sp.n)], inc=pe)
                    ph.op("vector", lambda v, h2=h2: v.tensor_copy(
                        gbc[:, h2 * 512:(h2 + 1) * 512], psb[:, 0:512]),
                        waits=[(pe, pe.n)], inc=dv)
                dma(ph, "sync", sp, self.gb_hbm.ap()[i * P:(i + 1) * P, :],
                    gbc[:], waits=[(dv, dv.n)])
            ph.wait("sync", sp, sp.n)
            ph.wait("gpsimd", gp, gp.n)
            ph.realize(nc)
        self._scS.close()

    def phase_ln0(self):
        nc, sem = self.nc, self.sem
        sp, gp, pe, ac, dv = (sem[k] for k in ("sp", "gp", "pe", "ac", "dv"))
        self.ln0T = self.salloc(self.scA, "ln0T", [P, DC * self.T], BF16)
        from contextlib import ExitStack
        self.sc0 = self.scA.enter_context(ExitStack())
        sc0 = self.sc0
        xb = [self.salloc(sc0, f"xb{i}", [P, D], F32) for i in range(3)]
        sq = self.salloc(sc0, "sq0", [P, D], F32)
        lnb = [self.salloc(sc0, f"lnb{i}", [P, D], BF16) for i in range(2)]
        stat = [self.salloc(sc0, f"stat{i}", [P, 8], F32) for i in range(2)]
        ph = Phase("ln0")
        xsem = [self.S(f"xb{j}") for j in range(3)]
        lnsem = [self.S(f"lnst{j}") for j in range(2)]
        chain_end, ln_store = {}, {}
        for i in range(self.NT):
            w = [(dv, chain_end[i - 3])] if i >= 3 else []
            x_tgt = dma(ph, "sync", xsem[i % 3], xb[i % 3][:],
                        self.x.ap()[i * P:(i + 1) * P, :], waits=w)
            self._ln_tile(ph, src=xb[i % 3][:], pre_wait=[(xsem[i % 3], x_tgt)],
                          sq=sq, stat=stat[i % 2], out=lnb[i % 2][:],
                          out_guard=(lnsem[i % 2], ln_store.get(i - 2, 0)))
            chain_end[i] = dv.n
            ln_store[i] = dma(ph, "sync", lnsem[i % 2],
                              self.ln0bf.ap()[i * P:(i + 1) * P, :], lnb[i % 2][:],
                              waits=[(dv, chain_end[i])])
        ph.wait("sync", lnsem[0], lnsem[0].n)
        ph.wait("sync", lnsem[1], lnsem[1].n)
        for c in range(DC):
            dma(ph, "sync", sp, self.ln0T[:, c * self.T:(c + 1) * self.T],
                self.ln0bf.ap()[:, c * P:(c + 1) * P], transpose=True)
        self.ln0T_ready = sp.n
        self.emit_preconvert(ph, 20)
        ph.realize(nc)
        self.sc0.close()

    def _ln_tile(self, ph, src, pre_wait, sq, stat, out, out_guard,
                 add=None, gamma=None, beta=None, out2=None, res=None):
        sem = self.sem
        pe, ac, dv = sem["pe"], sem["ac"], sem["dv"]
        ph.op("scalar", lambda s: s.activation(
            sq[:], src, AF.Square, accum_out=stat[:, 1:2]),
            waits=pre_wait + [(dv, dv.n)], inc=ac)
        ph.op("vector", lambda v: v.reduce_sum(stat[:, 0:1], src, axis=AX.X),
              waits=pre_wait + [(ac, ac.n)], inc=dv)
        ph.op("vector", lambda v: v.tensor_scalar(
            stat[:, 2:3], stat[:, 0:1], 1.0 / D, None, ALU.mult),
            waits=[(dv, dv.n), (ac, ac.n)], inc=dv)
        ph.op("vector", lambda v: v.tensor_tensor(
            stat[:, 3:4], stat[:, 2:3], stat[:, 2:3], ALU.mult),
            waits=[(dv, dv.n)], inc=dv)
        ph.op("vector", lambda v: v.tensor_scalar(
            stat[:, 4:5], stat[:, 1:2], 1.0 / D, None, ALU.mult),
            waits=[(dv, dv.n)], inc=dv)
        ph.op("vector", lambda v: v.tensor_sub(
            stat[:, 4:5], stat[:, 4:5], stat[:, 3:4]), waits=[(dv, dv.n)], inc=dv)
        ph.op("scalar", lambda s: s.activation(
            stat[:, 5:6], stat[:, 4:5], AF.Sqrt, bias=self.cst[:, 0:1]),
            waits=[(dv, dv.n)], inc=ac)
        ph.op("vector", lambda v: v.reciprocal(stat[:, 6:7], stat[:, 5:6]),
              waits=[(ac, ac.n)], inc=dv)
        ph.op("vector", lambda v: v.tensor_scalar(
            stat[:, 7:8], stat[:, 2:3], stat[:, 6:7], -1.0, ALU.mult, ALU.mult),
            waits=[(dv, dv.n)], inc=dv)
        ph.op("vector", lambda v: v.tensor_scalar(
            out, src, stat[:, 6:7], stat[:, 7:8], ALU.mult, ALU.add),
            waits=[(dv, dv.n), out_guard], inc=dv)



    def _emit_wT(self, ph, sc, wname, wt, pst, stage, stgsem):
        pe, ac = self.sem["pe"], self.sem["ac"]
        tp_tgt = {}
        for r in range(DC):
            w = [(pe, tp_tgt[r - 2])] if r >= 2 else []
            ld = dma(ph, "gpsimd", stgsem[r % 2], stage[r % 2][:],
                     self.ext[wname].ap()[r * P:(r + 1) * P, :], waits=w)
            for c in range(DC):
                ph.op("tensor", lambda t, r=r, c=c: t.transpose(
                    pst[c % 2][0:P, 0:P], stage[r % 2][:, c * P:(c + 1) * P],
                    self.sb_ident[:]),
                    waits=[(stgsem[r % 2], ld), (ac, max(0, ac.n - 1))], inc=pe)
                ph.op("scalar", lambda s, r=r, c=c: s.copy(
                    wt[:, D * c + P * r: D * c + P * (r + 1)], pst[c % 2][0:P, 0:P]),
                    waits=[(pe, pe.n)], inc=ac)
            tp_tgt[r] = pe.n

    def phase_qkvA(self):
        from contextlib import ExitStack
        nc = self.nc
        sp, gp, pe, ac, dv = (self.sem[k] for k in ("sp", "gp", "pe", "ac", "dv"))
        NT, T_ = self.NT, self.T
        for wname, kind, bias, outh in [
            ("aWq", "feat", self.b_bqT, self.qt_hbm),
            ("aWk", "feat", self.b_bkT, self.kt_hbm),
            ("aWv", "tok", None, self.v_hbm),
        ]:
            with ExitStack() as sc:
                wt = self.salloc(sc, "wt_" + wname, [P, DC * D], BF16)
                stage = [self.salloc(sc, f"stg{i}_" + wname, [P, D], BF16)
                         for i in range(2)]
                stgsem = [self.S(f"stg{i}_{wname}") for i in range(2)]
                ob = [self.salloc(sc, f"ob{i}_" + wname,
                                  [P, T_ if kind == "feat" else 2 * 65 * NT], BF16)
                      for i in range(2)]
                obsem = [self.S(f"ob{i}_{wname}") for i in range(2)]
                ph = Phase("qkvA_" + wname)
                with nc.psum_tensor("pst0" + wname, [P, P], BF16) as p0, \
                     nc.psum_tensor("pst1" + wname, [P, P], BF16) as p1, \
                     nc.psum_tensor("psq0" + wname, [P, 512], F32) as q0, \
                     nc.psum_tensor("psq1" + wname, [P, 512], F32) as q1:
                    self._emit_wT(ph, sc, wname, wt, [p0, p1], stage, stgsem)
                    wT_done = self.sem["ac"].n
                    psq = [q0, q1]
                    ev_tgt = {}
                    st_tgt = {}
                    for hp in range(NHP):
                        if kind == "feat":
                            for nb in range(T_ // 512):
                                gi = hp * (T_ // 512) + nb
                                for ic in range(DC):
                                    wl = [(ac, wT_done), (sp, self.ln0T_ready)] \
                                        if gi == 0 and ic == 0 else []
                                    if ic == 0 and gi >= 2:
                                        wl = wl + [(ac, ev_tgt[gi - 2])]
                                    ph.op("tensor", lambda t, hp=hp, nb=nb, ic=ic,
                                          g=gi: t.matmul(
                                        psq[g % 2][:, 0:512],
                                        wt[:, D * ic + P * hp: D * ic + P * (hp + 1)],
                                        self.ln0T[:, T_ * ic + 512 * nb:
                                                  T_ * ic + 512 * (nb + 1)],
                                        start=(ic == 0), stop=(ic == DC - 1)),
                                        waits=wl, inc=(pe if ic == DC - 1 else None))
                                mmt = pe.n
                                wl = [(pe, mmt)]
                                if hp >= 2 and nb == 0:
                                    wl.append((obsem[hp % 2], st_tgt[hp - 2]))
                                ev_tgt[gi] = ph.op("scalar", lambda s, hp=hp, nb=nb,
                                                   g=gi: s.activation(
                                    ob[hp % 2][:, 512 * nb:512 * (nb + 1)],
                                    psq[g % 2][:, 0:512], AF.Identity,
                                    bias=bias[:, hp:hp + 1]),
                                    waits=wl, inc=ac)
                            st_tgt[hp] = dma(ph, "sync", obsem[hp % 2],
                                             outh.ap()[hp], ob[hp % 2][:],
                                             waits=[(ac, ev_tgt[hp * (T_ // 512)
                                                                + T_ // 512 - 1])])
                        else:
                            for tb in range(NT):
                                gi = hp * NT + tb
                                for ic in range(DC):
                                    wl = [(ac, wT_done), (sp, self.ln0T_ready)] \
                                        if gi == 0 and ic == 0 else []
                                    if ic == 0 and gi >= 2:
                                        wl = wl + [(ac, ev_tgt[gi - 2])]
                                    ph.op("tensor", lambda t, hp=hp, tb=tb, ic=ic,
                                          g=gi: t.matmul(
                                        psq[g % 2][:, 0:P],
                                        self.ln0T[:, T_ * ic + P * tb:
                                                  T_ * ic + P * (tb + 1)],
                                        wt[:, D * ic + P * hp: D * ic + P * (hp + 1)],
                                        start=(ic == 0), stop=False),
                                        waits=wl, inc=None)
                                ph.op("tensor", lambda t, hp=hp, g=gi: t.matmul(
                                    psq[g % 2][:, 0:P], self.sb_ones[:],
                                    self.r_bv[0:1, P * hp:P * (hp + 1)],
                                    start=False, stop=True), inc=pe)
                                mmt = pe.n
                                wl = [(pe, mmt)]
                                if hp >= 2 and tb == 0:
                                    wl.append((obsem[hp % 2], st_tgt[hp - 2]))
                                for h2 in range(2):
                                    col = 65 * NT * h2 + 65 * tb
                                    ev_tgt[gi] = ph.op(
                                        "scalar", lambda s, col=col, h2=h2, g=gi:
                                        s.copy(ob[hp % 2][:, col:col + 64]
                                               if False else
                                               ob[(g // NT) % 2][:, col:col + 64],
                                               psq[g % 2][:, 64 * h2:64 * (h2 + 1)]),
                                        waits=wl if h2 == 0 else [], inc=ac)
                            onescol = ph.op("vector", lambda v, hp=hp: v.memset(
                                ob[hp % 2][:, 0:2 * 65 * NT].rearrange(
                                    "p (t c) -> p t c", c=65)[:, :, 64:65], 1.0),
                                waits=[(ac, ev_tgt[hp * NT + NT - 1])], inc=dv)
                            st_tgt[hp] = dma(ph, "sync", obsem[hp % 2],
                                             outh.ap()[hp], ob[hp % 2][:],
                                             waits=[(dv, onescol)])
                    for j in range(2):
                        ph.wait("sync", obsem[j], obsem[j].n)
                    self.emit_preconvert(ph, 10)
                    ph.realize(nc)

    def phase_attn(self, sc, kind):
        nc = self.nc
        sp, gp, pe, ac, dv = (self.sem[k] for k in ("sp", "gp", "pe", "ac", "dv"))
        NT, T_, NQ = self.NT, self.T, self.NQ
        if kind == "A":
            qth, kth, vh = self.qt_hbm, self.kt_hbm, self.v_hbm
            nm, cls, kbl, maskh = self.nmA, self.clsA, self.kbA, self.maskA
        else:
            qth, kth, vh = self.qt2_hbm, self.kt2_hbm, self.v2_hbm
            nm, cls, kbl, maskh = self.nmB, self.clsB, self.kbB, self.maskB
        mask_sb = self.salloc(sc, f"masksb{kind}", [P, max(nm, 1) * QSB], F32)
        qb = [self.salloc(sc, f"at_q{i}{kind}", [P, T_], BF16) for i in range(2)]
        kb = [self.salloc(sc, f"at_k{i}{kind}", [P, T_], BF16) for i in range(2)]
        vb = [self.salloc(sc, f"at_v{i}{kind}", [P, 2 * 65 * NT], BF16) for i in range(2)]
        qkvsem = [self.S(f"at{kind}{i}") for i in range(2)]
        probs = [self.salloc(sc, f"at_p{i}{kind}", [P, 2 * QSB], BF16)
                 for i in range(3)]
        mtmp = [self.salloc(sc, f"at_m{i}{kind}", [P, 2 * QSB], F32)
                for i in range(2)]
        recip2 = [self.salloc(sc, f"at_recip{i}" + kind, [1, QSB], F32)
                  for i in range(2)]
        bcast = self.salloc(sc, "at_bc" + kind, [64, QSB], F32)
        ph = Phase("attn" + kind)
        for mi in range(nm):
            dma(ph, "sync", sp, mask_sb[:, mi * QSB:(mi + 1) * QSB],
                maskh.ap()[mi * P:(mi + 1) * P, :])
        mask_rdy = sp.n
        ph.wait("sync", sp, mask_rdy)
        with nc.psum_tensor("ps_s0" + kind, [P, 1024], F32) as s0, \
             nc.psum_tensor("ps_s1" + kind, [P, 1024], F32) as s1, \
             nc.psum_tensor("ps_c0" + kind, [P, 512], F32) as c0, \
             nc.psum_tensor("ps_c1" + kind, [P, 512], F32) as c1, \
             nc.psum_tensor("ps_b" + kind, [P, 512], F32) as pb:
            ps_s, ps_c = [s0, s1], [c0, c1]
            ld_tgt, hp_last_use = {}, {}

            def plan_load(hp):
                w = [(pe, hp_last_use[hp - 2])] if hp >= 2 else []
                dma(ph, "sync", qkvsem[hp % 2], qb[hp % 2][:], qth.ap()[hp], waits=w)
                dma(ph, "sync", qkvsem[hp % 2], kb[hp % 2][:], kth.ap()[hp])
                ld_tgt[hp] = dma(ph, "sync", qkvsem[hp % 2], vb[hp % 2][:],
                                 vh.ap()[hp])
            plan_load(0)
            plan_load(1)
            ji, cseq = 0, 0
            bank_free, mtmp_free, ctx_free = {}, {}, {}
            LAG = 2
            pending_norm = []
            norm_state = {"bt": {}, "g": 0}

            def plan_norm(cps, hp, hrow, qsb, cmt):
                def go():
                    g = norm_state["g"]; norm_state["g"] = g + 1
                    rb = recip2[g % 2]
                    wl1 = [(pe, cmt)]
                    if g - 2 in norm_state["bt"]:
                        wl1.append((pe, norm_state["bt"][g - 2]))
                    l1 = ph.op("scalar", lambda s, cps=cps, rb=rb: s.activation(
                        rb[:], cps[64:65, 0:512], AF.Ln), waits=wl1, inc=ac)
                    l2 = ph.op("scalar", lambda s, rb=rb: s.activation(
                        rb[:], rb[:], AF.Exp, scale=self.cst[0:1, 1:2]),
                        waits=[(ac, l1)], inc=ac)
                    bt = ph.op("tensor", lambda t, rb=rb: t.matmul(
                        pb[0:64, 0:512], self.sb_ones32[0:1, 0:64], rb[:],
                        start=True, stop=True),
                        waits=[(ac, l2), (dv, norm_state.get("ct_prev", 0))],
                        inc=pe)
                    norm_state["bt"][g] = bt
                    ct = ph.op("vector", lambda v: v.tensor_copy(
                        bcast[:], pb[0:64, 0:512]),
                        waits=[(pe, bt), (dv, dv.n)], inc=dv)
                    norm_state["ct_prev"] = ct
                    ev = ph.op("vector", lambda v, cps=cps, hp=hp, hrow=hrow,
                               qsb=qsb: v.tensor_mul(
                        self.ctxT[hrow:hrow + 64,
                                  T_ * hp + QSB * qsb:T_ * hp + QSB * (qsb + 1)],
                        cps[0:64, 0:512], bcast[:]),
                        waits=[(dv, ct)], inc=dv)
                    return ev
                return go

            for hp in range(NHP):
                for qsb in range(NQ):
                    lst = kbl[qsb]
                    nk = len(lst)
                    cA, cB = ps_c[0], ps_c[1]
                    pws = {}

                    def plan_S(j):
                        nonlocal ji
                        kbi = lst[j]
                        spair = ps_s[ji % 2]
                        wl = [(qkvsem[hp % 2], ld_tgt[hp])]
                        if ji % 2 in bank_free:
                            wl.append(bank_free[ji % 2])
                        for h2 in range(2):
                            hrow = 64 * h2
                            ph.op("tensor", lambda t, hrow=hrow, kbi=kbi,
                                  spair=spair, h2=h2, hp=hp, qsb=qsb: t.matmul(
                                spair[:, 512 * h2:512 * (h2 + 1)],
                                kb[hp % 2][hrow:hrow + 64, P * kbi:P * (kbi + 1)],
                                qb[hp % 2][hrow:hrow + 64,
                                           QSB * qsb:QSB * (qsb + 1)],
                                start=True, stop=True),
                                waits=(wl if h2 == 0 else []),
                                inc=(pe if h2 == 1 else None))
                        smt = pe.n
                        c = cls[(qsb, kbi)]
                        if c == 'clean':
                            et = ph.op("scalar", lambda s, spair=spair, ji=ji:
                                       s.activation(probs[ji % 3][:],
                                                    spair[:, 0:1024], AF.Exp,
                                                    scale=SCALE),
                                       waits=[(pe, smt)], inc=ac)
                            bank_free[ji % 2] = (ac, et)
                            pws[j] = (ac, et)
                        else:
                            mi = c[1]
                            wv = [(pe, smt), (dv, dv.n)]
                            if ji % 2 in mtmp_free:
                                wv.append(mtmp_free[ji % 2])
                            for h2 in range(2):
                                ph.op("vector", lambda v, spair=spair, mi=mi,
                                      ji=ji, h2=h2: v.scalar_tensor_tensor(
                                    mtmp[ji % 2][:, 512 * h2:512 * (h2 + 1)],
                                    spair[:, 512 * h2:512 * (h2 + 1)], SCALE,
                                    mask_sb[:, mi * QSB:(mi + 1) * QSB],
                                    ALU.mult, ALU.add),
                                    waits=(wv if h2 == 0 else []), inc=dv)
                            vt = dv.n
                            bank_free[ji % 2] = (dv, vt)
                            et = ph.op("scalar", lambda s, ji=ji: s.activation(
                                probs[ji % 3][:], mtmp[ji % 2][:, 0:1024],
                                AF.Exp), waits=[(dv, vt)], inc=ac)
                            mtmp_free[ji % 2] = (ac, et)
                            pws[j] = (ac, et)
                        pws[j, "pi"] = ji
                        ji += 1

                    def plan_C(j):
                        kbi = lst[j]
                        pi = pws[j, "pi"]
                        wl2 = [pws[j]]
                        if j == 0 and 0 in ctx_free:
                            wl2.append(ctx_free[0])
                            wl2.append(ctx_free[1])
                        for h2 in range(2):
                            vcol = 65 * NT * h2 + 65 * kbi
                            cps = cA if h2 == 0 else cB
                            ph.op("tensor", lambda t, vcol=vcol, cps=cps,
                                  pi=pi, j=j, h2=h2, hp=hp,
                                  last=(j == nk - 1): t.matmul(
                                cps[0:65, 0:512],
                                vb[hp % 2][:, vcol:vcol + 65],
                                probs[pi % 3][:, 512 * h2:512 * (h2 + 1)],
                                start=(j == 0), stop=last),
                                waits=(wl2 if h2 == 0 else []),
                                inc=(pe if (h2 == 1 and j == nk - 1) else None))

                    for j in range(nk):
                        plan_S(j)
                        if j == LAG and pending_norm:
                            ctx_free[0] = (dv, pending_norm.pop(0)())
                            ctx_free[1] = (dv, pending_norm.pop(0)())
                        if j >= LAG:
                            plan_C(j - LAG)
                    if nk <= LAG and pending_norm:
                        ctx_free[0] = (dv, pending_norm.pop(0)())
                        ctx_free[1] = (dv, pending_norm.pop(0)())
                    for j in range(max(0, nk - LAG), nk):
                        plan_C(j)
                    cmt = pe.n
                    pending_norm.append(plan_norm(cA, hp, 0, qsb, cmt))
                    pending_norm.append(plan_norm(cB, hp, 64, qsb, cmt))
                    cseq += 1
                hp_last_use[hp] = pe.n
                if hp + 2 < NHP:
                    plan_load(hp + 2)
            while pending_norm:
                pending_norm.pop(0)()
            for j2 in range(2):
                ph.wait("sync", qkvsem[j2], qkvsem[j2].n)
            self.emit_preconvert(ph, 20)
            ph.realize(nc)
        self.attn_done = dv.n


    def phase_woA(self, sc):
        nc = self.nc
        sp, gp, pe, ac, dv = (self.sem[k] for k in ("sp", "gp", "pe", "ac", "dv"))
        NT, T_ = self.NT, self.T
        wt = self.salloc(sc, "wt_aWo", [P, DC * D], BF16)
        stage = [self.salloc(sc, f"stgo{i}", [P, D], BF16) for i in range(2)]
        stgsem = [self.S(f"stgo{i}") for i in range(2)]
        xb = [self.salloc(sc, f"wx{i}", [P, D], F32) for i in range(2)]
        xsem = [self.S(f"wx{i}") for i in range(2)]
        ao = [self.salloc(sc, f"ao{i}", [P, D], F32) for i in range(2)]
        lnat = self.salloc(sc, "lnat", [P, D], F32)
        x1b = [self.salloc(sc, f"x1b{i}", [P, D], F32) for i in range(2)]
        ln1b = [self.salloc(sc, f"ln1b{i}", [P, D], BF16) for i in range(2)]
        sq = self.salloc(sc, "wsq", [P, D], F32)
        stat = [self.salloc(sc, f"wst{i}", [P, 8], F32) for i in range(4)]
        x1sem = [self.S(f"x1st{i}") for i in range(2)]
        ln1sem = [self.S(f"ln1st{i}") for i in range(2)]
        ph = Phase("woA")
        with nc.psum_tensor("pwt0", [P, P], BF16) as p0, \
             nc.psum_tensor("pwt1", [P, P], BF16) as p1, \
             nc.psum_tensor("po0", [P, 512], F32) as o0, \
             nc.psum_tensor("po1", [P, 512], F32) as o1, \
             nc.psum_tensor("po2", [P, 512], F32) as o2, \
             nc.psum_tensor("po3", [P, 512], F32) as o3:
            self._emit_wT(ph, sc, "aWo", wt, [p0, p1], stage, stgsem)
            wT_done = ac.n
            pos = [[o0, o1], [o2, o3]]
            ev_tgt, x1_tgt, ln1_tgt = {}, {}, {}
            for tb in range(NT):
                wx = [(dv, ev_tgt[tb - 2])] if tb >= 2 else []
                xt = dma(ph, "sync", xsem[tb % 2], xb[tb % 2][:],
                         self.x.ap()[tb * P:(tb + 1) * P, :], waits=wx)
                for dh in range(2):
                    for hp in range(NHP):
                        wl = []
                        if tb == 0 and dh == 0 and hp == 0:
                            wl.append((ac, wT_done))
                        if hp == 0 and tb >= 2:
                            wl.append((ac, ev_tgt[tb - 2, dh] if isinstance(
                                ev_tgt.get((tb - 2, dh)), int) else
                                ev_tgt[(tb - 2, dh)]))
                        ph.op("tensor", lambda t, tb=tb, dh=dh, hp=hp: t.matmul(
                            pos[tb % 2][dh][:, 0:512],
                            self.ctxT[:, T_ * hp + P * tb:T_ * hp + P * (tb + 1)],
                            wt[:, D * hp + 512 * dh:D * hp + 512 * (dh + 1)],
                            start=(hp == 0), stop=False), waits=wl, inc=None)
                    ph.op("tensor", lambda t, tb=tb, dh=dh: t.matmul(
                        pos[tb % 2][dh][:, 0:512], self.sb_ones[:],
                        self.r_bo[0:1, 512 * dh:512 * (dh + 1)],
                        start=False, stop=True), inc=pe)
                    mmt = pe.n
                    ev_tgt[(tb, dh)] = ph.op(
                        "scalar", lambda s, tb=tb, dh=dh: s.copy(
                            ao[tb % 2][:, 512 * dh:512 * (dh + 1)],
                            pos[tb % 2][dh][:, 0:512]),
                        waits=[(pe, mmt)], inc=ac)
                self._ln_tile(ph, src=ao[tb % 2][:],
                              pre_wait=[(ac, ev_tgt[(tb, 1)])],
                              sq=sq, stat=stat[tb % 2], out=lnat[:],
                              out_guard=(dv, dv.n))
                x1d = ph.op("vector", lambda v, tb=tb: v.tensor_add(
                    x1b[tb % 2][:], xb[tb % 2][:], lnat[:]),
                    waits=[(dv, dv.n), (xsem[tb % 2], xt),
                           (x1sem[tb % 2], x1_tgt.get(tb - 2, 0))], inc=dv)
                x1_tgt[tb] = dma(ph, "sync", x1sem[tb % 2],
                                 self.x1f32.ap()[tb * P:(tb + 1) * P, :],
                                 x1b[tb % 2][:], waits=[(dv, x1d)])
                self._ln_tile(ph, src=x1b[tb % 2][:], pre_wait=[(dv, x1d)],
                              sq=sq, stat=stat[2 + tb % 2], out=ln1b[tb % 2][:],
                              out_guard=(ln1sem[tb % 2], ln1_tgt.get(tb - 2, 0)))
                ev_tgt[tb] = dv.n
                ln1_tgt[tb] = dma(ph, "sync", ln1sem[tb % 2],
                                  self.ln1bf.ap()[tb * P:(tb + 1) * P, :],
                                  ln1b[tb % 2][:], waits=[(dv, dv.n)])
            for j in range(2):
                ph.wait("sync", x1sem[j], x1sem[j].n)
                ph.wait("sync", ln1sem[j], ln1sem[j].n)
            self.emit_preconvert(ph, 15)
            ph.realize(nc)

    def phase_dkv(self, sc):
        nc = self.nc
        sp, gp, pe, ac, dv = (self.sem[k] for k in ("sp", "gp", "pe", "ac", "dv"))
        NT, T_ = self.NT, self.T
        ln1T = self.salloc(sc, "ln1T", [P, DC * T_], BF16)
        wdkv = self.salloc(sc, "wdkv", [P, DC * L], BF16)
        wdq = self.salloc(sc, "wdq", [P, DC * L], BF16)
        ph = Phase("dkv")
        for c in range(DC):
            dma(ph, "sync", sp, ln1T[:, c * T_:(c + 1) * T_],
                self.ln1bf.ap()[:, c * P:(c + 1) * P], transpose=True)
        dma(ph, "gpsimd", gp, wdkv[:].rearrange("p (ic l) -> p ic l", ic=DC),
            self.ext["fWdkv"].ap().rearrange("(ic p) l -> p ic l", p=P))
        dma(ph, "gpsimd", gp, wdq[:].rearrange("p (ic l) -> p ic l", ic=DC),
            self.ext["fWdq"].ap().rearrange("(ic p) l -> p ic l", p=P))
        ph.wait("sync", sp, sp.n)
        ph.wait("gpsimd", gp, gp.n)
        ld_rdy_sp, ld_rdy_gp = sp.n, gp.n
        with nc.psum_tensor("pd0", [P, 512], F32) as d0, \
             nc.psum_tensor("pd1", [P, 512], F32) as d1:
            psd = [d0, d1]
            gi = 0
            ev_tgt = {}
            for (wsb, bias, out) in [(wdkv, self.b_bdkvT, self.ckvT),
                                     (wdq, self.b_bdqT, self.cqT)]:
                for lc in range(LC):
                    for nb in range(T_ // 512):
                        wl = []
                        if gi == 0:
                            wl = [(sp, ld_rdy_sp), (gp, ld_rdy_gp)]
                        if gi >= 2:
                            wl.append((ac, ev_tgt[gi - 2]))
                        for ic in range(DC):
                            ph.op("tensor", lambda t, wsb=wsb, lc=lc, nb=nb, ic=ic,
                                  gi=gi: t.matmul(
                                psd[gi % 2][:, 0:512],
                                wsb[:, L * ic + P * lc:L * ic + P * (lc + 1)],
                                ln1T[:, T_ * ic + 512 * nb:T_ * ic + 512 * (nb + 1)],
                                start=(ic == 0), stop=(ic == DC - 1)),
                                waits=(wl if ic == 0 else []),
                                inc=(pe if ic == DC - 1 else None))
                        ev_tgt[gi] = ph.op(
                            "scalar", lambda s, out=out, bias=bias, lc=lc, nb=nb,
                            gi=gi: s.activation(
                                out[:, T_ * lc + 512 * nb:T_ * lc + 512 * (nb + 1)],
                                psd[gi % 2][:, 0:512], AF.Identity,
                                bias=bias[:, lc:lc + 1]),
                            waits=[(pe, pe.n)], inc=ac)
                        gi += 1
            self.emit_preconvert(ph, 10)
            ph.realize(nc)

    def phase_uproj(self, sc):
        nc = self.nc
        sp, gp, pe, ac, dv = (self.sem[k] for k in ("sp", "gp", "pe", "ac", "dv"))
        NT, T_ = self.NT, self.T
        for wname, kind, bias, outh, src in [
            ("fWuk", "feat", self.b_bukT, self.kt2_hbm, "ckvT"),
            ("fWuq", "feat", self.b_buqT, self.qt2_hbm, "cqT"),
            ("fWuv", "tok", None, self.v2_hbm, "ckvT"),
        ]:
            from contextlib import ExitStack
            with ExitStack() as sc2:
                wu = self.salloc(sc2, "wu_" + wname, [P, LC * D], BF16)
                ob = [self.salloc(sc2, f"ub{i}_" + wname,
                                  [P, T_ if kind == "feat" else 2 * 65 * NT], BF16)
                      for i in range(2)]
                obsem = [self.S(f"ub{i}_{wname}") for i in range(2)]
                srcT = getattr(self, src)
                ph = Phase("uproj_" + wname)
                dma(ph, "gpsimd", gp, wu[:].rearrange("p (lc d) -> p lc d", lc=LC),
                    self.ext[wname].ap().rearrange("(lc p) d -> p lc d", p=P))
                ph.wait("gpsimd", gp, gp.n)
                w_rdy = gp.n
                with nc.psum_tensor("pu0" + wname, [P, 512], F32) as u0, \
                     nc.psum_tensor("pu1" + wname, [P, 512], F32) as u1:
                    psu = [u0, u1]
                    ev_tgt, st_tgt = {}, {}
                    for hp in range(NHP):
                        if kind == "feat":
                            for nb in range(T_ // 512):
                                gi = hp * (T_ // 512) + nb
                                wl = [(gp, w_rdy)] if gi == 0 else []
                                if gi >= 2:
                                    wl = wl + [(ac, ev_tgt[gi - 2])]
                                for lc in range(LC):
                                    ph.op("tensor", lambda t, hp=hp, nb=nb, lc=lc,
                                          gi=gi: t.matmul(
                                        psu[gi % 2][:, 0:512],
                                        wu[:, D * lc + P * hp:D * lc + P * (hp + 1)],
                                        srcT[:, T_ * lc + 512 * nb:
                                             T_ * lc + 512 * (nb + 1)],
                                        start=(lc == 0), stop=(lc == LC - 1)),
                                        waits=(wl if lc == 0 else []),
                                        inc=(pe if lc == LC - 1 else None))
                                mmt = pe.n
                                wl2 = [(pe, mmt)]
                                if hp >= 2 and nb == 0:
                                    wl2.append((obsem[hp % 2], st_tgt[hp - 2]))
                                ev_tgt[gi] = ph.op("scalar", lambda s, hp=hp, nb=nb,
                                                   gi=gi: s.activation(
                                    ob[hp % 2][:, 512 * nb:512 * (nb + 1)],
                                    psu[gi % 2][:, 0:512], AF.Identity,
                                    bias=bias[:, hp:hp + 1]),
                                    waits=wl2, inc=ac)
                            st_tgt[hp] = dma(ph, "sync", obsem[hp % 2],
                                             outh.ap()[hp], ob[hp % 2][:],
                                             waits=[(ac, ev_tgt[hp * (T_ // 512)
                                                                + T_ // 512 - 1])])
                        else:
                            for tb in range(NT):
                                gi = hp * NT + tb
                                wl = [(gp, w_rdy)] if gi == 0 else []
                                if gi >= 2:
                                    wl = wl + [(ac, ev_tgt[gi - 2])]
                                for lc in range(LC):
                                    ph.op("tensor", lambda t, hp=hp, tb=tb, lc=lc,
                                          gi=gi: t.matmul(
                                        psu[gi % 2][:, 0:P],
                                        srcT[:, T_ * lc + P * tb:T_ * lc + P * (tb + 1)],
                                        wu[:, D * lc + P * hp:D * lc + P * (hp + 1)],
                                        start=(lc == 0), stop=False),
                                        waits=(wl if lc == 0 else []), inc=None)
                                ph.op("tensor", lambda t, hp=hp, gi=gi: t.matmul(
                                    psu[gi % 2][:, 0:P], self.sb_ones[:],
                                    self.r_buv[0:1, P * hp:P * (hp + 1)],
                                    start=False, stop=True), inc=pe)
                                mmt = pe.n
                                wl2 = [(pe, mmt)]
                                if hp >= 2 and tb == 0:
                                    wl2.append((obsem[hp % 2], st_tgt[hp - 2]))
                                for h2 in range(2):
                                    col = 65 * NT * h2 + 65 * tb
                                    ev_tgt[gi] = ph.op(
                                        "scalar", lambda s, col=col, h2=h2, gi=gi,
                                        hp=hp: s.copy(
                                            ob[hp % 2][:, col:col + 64],
                                            psu[gi % 2][:, 64 * h2:64 * (h2 + 1)]),
                                        waits=(wl2 if h2 == 0 else []), inc=ac)
                            onescol = ph.op("vector", lambda v, hp=hp: v.memset(
                                ob[hp % 2][:, 0:2 * 65 * NT].rearrange(
                                    "p (t c) -> p t c", c=65)[:, :, 64:65], 1.0),
                                waits=[(ac, ev_tgt[hp * NT + NT - 1]), (dv, dv.n)],
                                inc=dv)
                            st_tgt[hp] = dma(ph, "sync", obsem[hp % 2],
                                             outh.ap()[hp], ob[hp % 2][:],
                                             waits=[(dv, onescol)])
                    for j in range(2):
                        ph.wait("sync", obsem[j], obsem[j].n)
                    self.emit_preconvert(ph, 8)
                    ph.realize(nc)

    def phase_woB(self, sc):
        nc = self.nc
        sp, gp, pe, ac, dv = (self.sem[k] for k in ("sp", "gp", "pe", "ac", "dv"))
        NT, T_ = self.NT, self.T
        wo2 = self.salloc(sc, "wo2", [P, DC * D], BF16)
        g1bc = self.salloc(sc, "g1bcl", [P, D], BF16)
        be1bc = self.salloc(sc, "be1bcl", [P, D], BF16)
        ln1b = [self.salloc(sc, f"bl{i}", [P, D], BF16) for i in range(2)]
        lnsem = [self.S(f"bln{i}") for i in range(2)]
        ai = [self.salloc(sc, f"ai{i}", [P, D], F32) for i in range(2)]
        ssum = self.salloc(sc, "ssum", [P, D], F32)
        x1fb = [self.salloc(sc, f"x1fb{i}", [P, D], F32) for i in range(2)]
        sq = self.salloc(sc, "bsq", [P, D], F32)
        stat = [self.salloc(sc, f"bst{i}", [P, 8], F32) for i in range(2)]
        x1fsem = [self.S(f"x1fst{i}") for i in range(2)]
        x1fbsem = [self.S(f"x1fbst{i}") for i in range(2)]
        ph = Phase("woB")
        dma(ph, "sync", sp, g1bc[:], self.gb_hbm.ap()[0:P, :])
        dma(ph, "sync", sp, be1bc[:], self.gb_hbm.ap()[P:2 * P, :])
        ph.wait("sync", sp, sp.n)
        dma(ph, "gpsimd", gp, wo2[:].rearrange("p (jc d) -> p jc d", jc=DC),
            self.ext["fWo"].ap().rearrange("(jc p) d -> p jc d", p=P))
        ph.wait("gpsimd", gp, gp.n)
        w_rdy = gp.n
        with nc.psum_tensor("bo0", [P, 512], F32) as o0, \
             nc.psum_tensor("bo1", [P, 512], F32) as o1, \
             nc.psum_tensor("bo2", [P, 512], F32) as o2, \
             nc.psum_tensor("bo3", [P, 512], F32) as o3:
            pos = [[o0, o1], [o2, o3]]
            ev_tgt, st1, st2 = {}, {}, {}
            for tb in range(NT):
                wlx = [(dv, ev_tgt[tb - 2])] if tb >= 2 else []
                lt = dma(ph, "sync", lnsem[tb % 2], ln1b[tb % 2][:],
                         self.ln1bf.ap()[tb * P:(tb + 1) * P, :], waits=wlx)
                for dh in range(2):
                    wl = [(gp, w_rdy)] if tb == 0 and dh == 0 else []
                    if tb >= 2:
                        wl.append((ac, ev_tgt[(tb - 2, dh)]))
                    for jc in range(DC):
                        ph.op("tensor", lambda t, tb=tb, dh=dh, jc=jc: t.matmul(
                            pos[tb % 2][dh][:, 0:512],
                            self.ctxT[:, T_ * jc + P * tb:T_ * jc + P * (tb + 1)],
                            wo2[:, D * jc + 512 * dh:D * jc + 512 * (dh + 1)],
                            start=(jc == 0), stop=False),
                            waits=(wl if jc == 0 else []), inc=None)
                    ph.op("tensor", lambda t, tb=tb, dh=dh: t.matmul(
                        pos[tb % 2][dh][:, 0:512], self.sb_ones[:],
                        self.r_fbo[0:1, 512 * dh:512 * (dh + 1)],
                        start=False, stop=True), inc=pe)
                    mmt = pe.n
                    ev_tgt[(tb, dh)] = ph.op(
                        "scalar", lambda s, tb=tb, dh=dh: s.copy(
                            ai[tb % 2][:, 512 * dh:512 * (dh + 1)],
                            pos[tb % 2][dh][:, 0:512]),
                        waits=[(pe, mmt)], inc=ac)
                sd = ph.op("vector", lambda v, tb=tb: v.tensor_add(
                    ssum[:], ln1b[tb % 2][:], ai[tb % 2][:]),
                    waits=[(ac, ev_tgt[(tb, 1)]), (lnsem[tb % 2], lt),
                           (dv, dv.n)], inc=dv)
                self._ln_tile(ph, src=ssum[:], pre_wait=[(dv, sd)],
                              sq=sq, stat=stat[tb % 2], out=ssum[:],
                              out_guard=(dv, dv.n))
                g1d = ph.op("vector", lambda v, tb=tb: v.tensor_mul(
                    ssum[:], ssum[:], g1bc[:]), waits=[(dv, dv.n)], inc=dv)
                b1d = ph.op("vector", lambda v, tb=tb: v.tensor_add(
                    x1fb[tb % 2][:], ssum[:], be1bc[:]),
                    waits=[(dv, g1d),
                           (x1fsem[tb % 2], st1.get(tb - 2, 0)),
                           (x1fbsem[tb % 2], st2.get(tb - 2, 0))], inc=dv)
                st1[tb] = dma(ph, "sync", x1fsem[tb % 2],
                              self.x1ff32.ap()[tb * P:(tb + 1) * P, :],
                              x1fb[tb % 2][:], waits=[(dv, b1d)])
                st2[tb] = dma(ph, "gpsimd", x1fbsem[tb % 2],
                              self.x1fbf.ap()[tb * P:(tb + 1) * P, :],
                              x1fb[tb % 2][:], waits=[(dv, b1d)])
                ev_tgt[tb] = dv.n
            for j in range(2):
                ph.wait("sync", x1fsem[j], x1fsem[j].n)
                ph.wait("gpsimd", x1fbsem[j], x1fbsem[j].n)
                ph.wait("sync", lnsem[j], lnsem[j].n)
            self.emit_preconvert(ph, 15)
            ph.realize(nc)

    def phase_x1fT(self, sc):
        nc = self.nc
        sp, gp, pe, ac, dv = (self.sem[k] for k in ("sp", "gp", "pe", "ac", "dv"))
        T_ = self.T
        stg = [self.salloc(sc, f"xft{i}", [P, T_], BF16) for i in range(2)]
        tsem = [self.S(f"xftt{i}") for i in range(2)]
        ssem = [self.S(f"xfts{i}") for i in range(2)]
        ph = Phase("x1fT")
        st_tgt = {}
        for c in range(DC):
            w = [(ssem[c % 2], st_tgt[c - 2])] if c >= 2 else []
            tt = dma(ph, "sync", tsem[c % 2], stg[c % 2][:],
                     self.x1fbf.ap()[:, c * P:(c + 1) * P], transpose=True,
                     waits=w)
            st_tgt[c] = dma(ph, "sync", ssem[c % 2], self.x1fT_hbm.ap()[c],
                            stg[c % 2][:], waits=[(tsem[c % 2], tt)])
        for j in range(2):
            ph.wait("sync", ssem[j], ssem[j].n)
        self.emit_preconvert(ph, 30)
        ph.realize(nc)

    def phase_moe(self, sc):
        nc = self.nc
        sp, gp, pe, ac, dv = (self.sem[k] for k in ("sp", "gp", "pe", "ac", "dv"))
        NT, T_ = self.NT, self.T
        NTC = T_ // 512
        w1sb = self.salloc(sc, "w1sb", [P, DC * FF], BF16)
        w2sb = self.salloc(sc, "w2sb", [P, FFC * D], BF16)
        h_sb = self.salloc(sc, "h_sb", [P, FFC * 512], BF16)
        moe = self.salloc(sc, "moe", [P, 4 * D], F32)
        xtc = self.salloc(sc, "xtc", [P, DC * 512], BF16)
        wg_sb = self.salloc(sc, "wg_sb", [P, DC * E], BF16)
        wn_sb = self.salloc(sc, "wn_sb", [P, DC * E], BF16)
        nz = self.salloc(sc, "nz", [P, 4 * E], F32)
        gt1 = self.salloc(sc, "gt1", [P, E], F32)
        gt2 = self.salloc(sc, "gt2", [P, E], F32)
        hx = self.salloc(sc, "hx", [P, E], F32)
        m1 = self.salloc(sc, "m1", [P, 1], F32)
        negm1 = self.salloc(sc, "negm1", [P, 1], F32)
        m1pad = self.salloc(sc, "m1pad", [P, E], F32)
        hrep = self.salloc(sc, "hrep", [P, E], F32)
        m2 = self.salloc(sc, "m2", [P, 1], F32)
        e1t = self.salloc(sc, "e1t", [P, E], F32)
        ge = self.salloc(sc, "ge", [P, E], F32)
        zt = self.salloc(sc, "zt", [P, 1], F32)
        w1sem, w2sem = self.S("w1s"), self.S("w2s")
        xtcsem, nzsem = self.S("xtc"), self.S("nz")
        self._moe_st = {}
        ph = Phase("moe")
        dma(ph, "gpsimd", gp, wg_sb[:].rearrange("p (ic e) -> p ic e", ic=DC),
            self.ext["Wg"].ap().rearrange("(ic p) e -> p ic e", p=P))
        dma(ph, "gpsimd", gp, wn_sb[:].rearrange("p (ic e) -> p ic e", ic=DC),
            self.ext["Wn"].ap().rearrange("(ic p) e -> p ic e", p=P))
        ph.wait("gpsimd", gp, gp.n)
        g_rdy = gp.n
        with nc.psum_tensor("mh0", [P, 512], F32) as h0, \
             nc.psum_tensor("mh1", [P, 512], F32) as h1, \
             nc.psum_tensor("mo0", [P, 512], F32) as po0, \
             nc.psum_tensor("mo1", [P, 512], F32) as po1, \
             nc.psum_tensor("mgG", [P, 512], F32) as pgG, \
             nc.psum_tensor("mgN", [P, 512], F32) as pgN, \
             nc.psum_tensor("mgT", [P, P], F32) as pgT:
            psH, psO = [h0, h1], [po0, po1]
            w1_use, w2_use = 0, 0
            xtc_use = 0
            ystore = {}
            for tc in range(NTC):
                xl = 0
                for c in range(DC):
                    xl = dma(ph, "sync", xtcsem, xtc[:, c * 512:(c + 1) * 512],
                             self.x1fT_hbm.ap()[c, :, tc * 512:(tc + 1) * 512],
                             waits=([(pe, xtc_use)] if c == 0 else []))
                nl = dma(ph, "sync", nzsem,
                         nz[:].rearrange("p (b e) -> p b e", e=E),
                         self.noise.ap()[tc * 512:(tc + 1) * 512, :]
                         .rearrange("(b p) e -> p b e", p=P),
                         waits=[(dv, dv.n)])
                w1l = dma(ph, "sync", w1sem,
                          w1sb[:].rearrange("p (ic f) -> p ic f", ic=DC),
                          self.w1bf.ap()[0].rearrange("(ic p) f -> p ic f", p=P),
                          waits=[(pe, w1_use), (gp, self._pc_done_tgt())])
                w2l = dma(ph, "sync", w2sem,
                          w2sb[:].rearrange("p (f d) -> p f d", f=FFC),
                          self.w2bf.ap()[0].rearrange("(f p) d -> p f d", p=P),
                          waits=[(pe, w2_use)])
                gmt = {}
                for tbi in range(4):
                    wlg = [(xtcsem, xl)]
                    if tc == 0 and tbi == 0:
                        wlg.append((gp, g_rdy))
                    if tbi >= 1:
                        wlg.append((dv, gmt[tbi - 1]))
                    for ic in range(DC):
                        ph.op("tensor", lambda t, tbi=tbi, ic=ic: t.matmul(
                            pgG[:, 0:E],
                            xtc[:, 512 * ic + P * tbi:512 * ic + P * (tbi + 1)],
                            wg_sb[:, E * ic:E * (ic + 1)],
                            start=(ic == 0), stop=False),
                            waits=(wlg if ic == 0 else []), inc=None)
                    ph.op("tensor", lambda t: t.matmul(
                        pgG[:, 0:E], self.sb_ones[:], self.r_bg[:],
                        start=False, stop=True), inc=pe)
                    for ic in range(DC):
                        ph.op("tensor", lambda t, tbi=tbi, ic=ic: t.matmul(
                            pgN[:, 0:E],
                            xtc[:, 512 * ic + P * tbi:512 * ic + P * (tbi + 1)],
                            wn_sb[:, E * ic:E * (ic + 1)],
                            start=(ic == 0), stop=False), inc=None)
                    ph.op("tensor", lambda t: t.matmul(
                        pgN[:, 0:E], self.sb_ones[:], self.r_bn[:],
                        start=False, stop=True), inc=pe)
                    gmm = pe.n
                    a1 = ph.op("scalar", lambda s: s.activation(
                        gt2[:], pgN[:, 0:E], AF.Abs), waits=[(pe, gmm)], inc=ac)
                    a2 = ph.op("scalar", lambda s: s.activation(
                        gt2[:], gt2[:], AF.Exp, scale=self.cst[:, 1:2]),
                        waits=[(ac, a1)], inc=ac)
                    a3 = ph.op("scalar", lambda s: s.activation(
                        gt2[:], gt2[:], AF.Ln, bias=1.0), waits=[(ac, a2)], inc=ac)
                    r1 = ph.op("vector", lambda v: v.tensor_scalar_max(
                        gt1[:], pgN[:, 0:E], 0.0),
                        waits=[(pe, gmm), (dv, dv.n)], inc=dv)
                    s1_ = ph.op("vector", lambda v: v.tensor_add(
                        gt1[:], gt1[:], gt2[:]), waits=[(dv, r1), (ac, a3)], inc=dv)
                    n1 = ph.op("vector", lambda v, tbi=tbi: v.tensor_mul(
                        gt1[:], gt1[:], nz[:, E * tbi:E * (tbi + 1)]),
                        waits=[(dv, s1_), (nzsem, nl)], inc=dv)
                    hx1 = ph.op("vector", lambda v: v.tensor_add(
                        hx[:], pgG[:, 0:E], gt1[:]), waits=[(dv, n1)], inc=dv)
                    mm1 = ph.op("vector", lambda v: v.reduce_max(
                        m1[:], hx[:], axis=AX.X), waits=[(dv, hx1)], inc=dv)
                    nm1 = ph.op("vector", lambda v: v.reduce_max(
                        negm1[:], hx[:], axis=AX.X, negate=True),
                        waits=[(dv, hx1)], inc=dv)
                    mp = ph.op("vector", lambda v: v.memset(m1pad[:], -1e38),
                               waits=[(dv, dv.n)], inc=dv)
                    mp2 = ph.op("vector", lambda v: v.tensor_copy(
                        m1pad[:, 0:1], m1[:]), waits=[(dv, mm1), (dv, mp)], inc=dv)
                    mr = ph.op("vector", lambda v: v.match_replace(
                        hrep[:], m1pad[:], hx[:], -1e30),
                        waits=[(dv, mp2)], inc=dv)
                    mm2 = ph.op("vector", lambda v: v.reduce_max(
                        m2[:], hrep[:], axis=AX.X), waits=[(dv, mr)], inc=dv)
                    ex1 = ph.op("scalar", lambda s: s.activation(
                        e1t[:], hx[:], AF.Exp, bias=negm1[:]),
                        waits=[(dv, nm1), (dv, hx1)], inc=ac)
                    g1_ = ph.op("vector", lambda v: v.tensor_scalar(
                        ge[:], hx[:], m2[:, 0:1], None, ALU.is_ge),
                        waits=[(dv, mm2)], inc=dv)
                    e1d = ph.op("vector", lambda v, tbi=tbi: v.tensor_mul(
                        self.e1m_sb[:, E * tbi:E * (tbi + 1)], e1t[:], ge[:]),
                        waits=[(dv, g1_), (ac, ex1)], inc=dv)
                    zd = ph.op("vector", lambda v, tbi=tbi: v.reduce_sum(
                        zt[:], self.e1m_sb[:, E * tbi:E * (tbi + 1)], axis=AX.X),
                        waits=[(dv, e1d)], inc=dv)
                    rz = ph.op("vector", lambda v, tb=tc * 4 + tbi: v.reciprocal(
                        self.recipZ[:, tb:tb + 1], zt[:]),
                        waits=[(dv, zd)], inc=dv)
                    et1 = ph.op("tensor", lambda t, tbi=tbi: t.transpose(
                        pgT[0:E, 0:P], self.e1m_sb[:, E * tbi:E * (tbi + 1)],
                        self.sb_ident32[:]), waits=[(dv, e1d), (ac, ac.n)], inc=pe)
                    ph.op("scalar", lambda s, tbi=tbi: s.copy(
                        self.e1mT_sb[0:E, P * tbi:P * (tbi + 1)], pgT[0:E, 0:P]),
                        waits=[(pe, et1)], inc=ac)
                    gmt[tbi] = dv.n
                gate_done_ac = ac.n
                b2t = {}
                for tbi in range(4):
                    for dh in range(2):
                        wbm = [(ac, gate_done_ac), (dv, dv.n)]
                        if tbi > 0:
                            wbm.append((ac, b2t[(tbi - 1, dh)]))
                        bt = ph.op("tensor", lambda t, tbi=tbi, dh=dh: t.matmul(
                            psO[dh][:, 0:512],
                            self.e1mT_sb[0:E, P * tbi:P * (tbi + 1)],
                            self.b2_sb[0:E, 512 * dh:512 * (dh + 1)],
                            start=True, stop=True),
                            waits=wbm, inc=pe)
                        wb2 = [(pe, bt)]
                        if tc > 0:
                            wb2.append((self.S(f"moest{tbi}"), self._moe_st[tbi]))
                        b2t[(tbi, dh)] = ph.op(
                            "scalar", lambda s, tbi=tbi, dh=dh: s.copy(
                                moe[:, D * tbi + 512 * dh:D * tbi + 512 * (dh + 1)],
                                psO[dh][:, 0:512]), waits=wb2, inc=ac)
                moe_init_ac = ac.n
                hev = {}
                for e in range(E):
                    if e > 0:
                        w1l = dma(ph, "sync", w1sem,
                                  w1sb[:].rearrange("p (ic f) -> p ic f", ic=DC),
                                  self.w1bf.ap()[e].rearrange(
                                      "(ic p) f -> p ic f", p=P),
                                  waits=[(pe, w1_use)])
                        w2l = dma(ph, "sync", w2sem,
                                  w2sb[:].rearrange("p (f d) -> p f d", f=FFC),
                                  self.w2bf.ap()[e].rearrange(
                                      "(f p) d -> p f d", p=P),
                                  waits=[(pe, w2_use)])
                    w2_prev_done = pe.n if e > 0 else 0
                    for f in range(FFC):
                        wl = [(w1sem, w1l)] if f == 0 else []
                        if f == 0:
                            wl.append((xtcsem, xl))
                        if f >= 2:
                            wl.append((ac, hev[f - 2]))
                        for ic in range(DC):
                            ph.op("tensor", lambda t, e=e, f=f, ic=ic: t.matmul(
                                psH[f % 2][:, 0:512],
                                w1sb[:, FF * ic + P * f:FF * ic + P * (f + 1)],
                                xtc[:, 512 * ic:512 * (ic + 1)],
                                start=(ic == 0), stop=(ic == DC - 1)),
                                waits=(wl if ic == 0 else []),
                                inc=(pe if ic == DC - 1 else None))
                        mmt = pe.n
                        wl2 = [(pe, mmt)]
                        if f == 0 and e > 0:
                            wl2.append((pe, w2_prev_done))
                        hev[f] = ph.op("scalar", lambda s, e=e, f=f: s.activation(
                            h_sb[:, 512 * f:512 * (f + 1)], psH[f % 2][:, 0:512],
                            AF.Relu, bias=self.b_b1T[:, FFC * e + f:FFC * e + f + 1]),
                            waits=wl2, inc=ac)
                    w1_use = pe.n
                    h_done = hev[FFC - 1]
                    sttv = {}
                    for tbi in range(4):
                        for dh in range(2):
                            gsel = (tbi * 2 + dh) % 2
                            wl = [(ac, h_done), (w2sem, w2l)] \
                                if tbi == 0 and dh == 0 else []
                            if e == 0 and tbi * 2 + dh < 2:
                                wl.append((ac, moe_init_ac))
                            if (tbi, dh) in (( 1, 0),) and False:
                                pass
                            prev = (tbi * 2 + dh) - 2
                            if prev >= 0:
                                wl.append((dv, sttv.get(prev, dv.n)))
                            elif e > 0 or tc > 0:
                                wl.append((dv, dv.n))
                            for f in range(FFC):
                                ph.op("tensor", lambda t, tbi=tbi, dh=dh, f=f,
                                      gsel=gsel: t.matmul(
                                    psO[gsel][:, 0:512],
                                    h_sb[:, 512 * f + P * tbi:512 * f + P * (tbi + 1)],
                                    w2sb[:, D * f + 512 * dh:D * f + 512 * (dh + 1)],
                                    start=(f == 0), stop=(f == FFC - 1)),
                                    waits=(wl if f == 0 else []),
                                    inc=(pe if f == FFC - 1 else None))
                            mmt = pe.n
                            wme = [(pe, mmt), (dv, dv.n)]
                            if e == 0:
                                wme.append((ac, b2t[(tbi, dh)]))
                            sttv[tbi * 2 + dh] = ph.op(
                                "vector", lambda v, tbi=tbi, dh=dh, e=e:
                                v.scalar_tensor_tensor(
                                    moe[:, D * tbi + 512 * dh:
                                        D * tbi + 512 * (dh + 1)],
                                    psO[gsel][:, 0:512] if False else
                                    psO[(tbi * 2 + dh) % 2][:, 0:512],
                                    self.e1m_sb[:, E * tbi + e:E * tbi + e + 1],
                                    moe[:, D * tbi + 512 * dh:
                                        D * tbi + 512 * (dh + 1)],
                                    ALU.mult, ALU.add),
                                waits=wme, inc=dv)
                    w2_use = pe.n
                xtc_use = pe.n
                for tbi in range(4):
                    tb = tc * 4 + tbi
                    self._moe_st[tbi] = dma(
                        ph, "sync", self.S(f"moest{tbi}"),
                        self.moe_hbm.ap()[tb * P:(tb + 1) * P, :],
                        moe[:, D * tbi:D * (tbi + 1)],
                        waits=[(dv, sttv[tbi * 2 + 1])])
            for tbi in range(4):
                sm = self.S(f"moest{tbi}")
                ph.wait("sync", sm, sm.n)
            ph.realize(nc)

    def phase_gate(self, sc):
        nc = self.nc
        sp, gp, pe, ac, dv = (self.sem[k] for k in ("sp", "gp", "pe", "ac", "dv"))
        NT, T_, NCHK = self.NT, self.T, self.NCHK
        xT = self.salloc(sc, "gxT", [P, DC * T_], BF16)
        wg_sb = self.salloc(sc, "gwg", [P, DC * E], BF16)
        wn_sb = self.salloc(sc, "gwn", [P, DC * E], BF16)
        selb = self.salloc(sc, "selb", [P, NT * E], BF16)
        i12 = self.salloc(sc, "i12", [P, 2 * NT], F32)
        nzb = [self.salloc(sc, f"gnz{i}", [P, E], F32) for i in range(2)]
        nzsem = [self.S(f"gnz{i}") for i in range(2)]
        gt1 = self.salloc(sc, "ggt1", [P, E], F32)
        gt2 = self.salloc(sc, "ggt2", [P, E], F32)
        hx = self.salloc(sc, "ghx", [P, E], F32)
        m1 = self.salloc(sc, "gm1", [P, 1], F32)
        negm1 = self.salloc(sc, "gnegm1", [P, 1], F32)
        m1pad = self.salloc(sc, "gm1pad", [P, E], F32)
        hrep = self.salloc(sc, "ghrep", [P, E], F32)
        m2 = self.salloc(sc, "gm2", [P, 1], F32)
        e1t = self.salloc(sc, "ge1t", [P, E], F32)
        m2pad = self.salloc(sc, "gm2pad", [P, E], F32)
        idx8 = self.salloc(sc, "gidx8", [P, E], mybir.dt.uint32)
        idx1 = self.salloc(sc, "gidx1", [P, 1], F32)
        ge = self.salloc(sc, "gge", [P, E], F32)
        e1m = self.salloc(sc, "ge1m", [P, E], F32)
        zt = self.salloc(sc, "gzt", [P, 1], F32)
        aux = [self.salloc(sc, f"gaux{i}", [P, 2], F32) for i in range(2)]
        auxsem = [self.S(f"gaux{i}") for i in range(2)]
        lt_sb = self.salloc(sc, "glt", [P, P], BF16)
        onescol = self.salloc(sc, "gonescol", [P, 1], BF16)
        iota_sb = self.salloc(sc, "giota", [P, 1], F32)
        iota8_sb = self.salloc(sc, "giota8", [P, E], F32)
        iotar = self.salloc(sc, "giotar", [P, NCHK], F32)
        cval_sb = self.salloc(sc, "gcval", [1, NCHK], F32)
        big_sb = self.salloc(sc, "gbig", [P, NCHK * 4], F32)
        big_sb2 = self.salloc(sc, "gbig2", [P, NCHK * 4], F32)
        totals = self.salloc(sc, "gtot", [1, NT * E], F32)
        excl = self.salloc(sc, "gexcl", [1, NT * E], F32)
        run = self.salloc(sc, "grun", [1, E], F32)
        nrow = self.salloc(sc, "gnrow", [1, E], F32)
        pad = self.salloc(sc, "gpad", [1, E], F32)
        base = self.salloc(sc, "gbase", [1, E], F32)
        endr = self.salloc(sc, "gend", [1, E], F32)
        brep = self.salloc(sc, "gbrep", [1, NT * E], F32)
        segrow = self.salloc(sc, "gsegrow", [1, NT * E], F32)
        echk = self.salloc(sc, "gechk", [1, NCHK], F32)
        tmpc = self.salloc(sc, "gtmpc", [1, NCHK], F32)
        tmpc2 = self.salloc(sc, "gtmpc2", [1, NCHK], F32)
        padi = self.salloc(sc, "gpadi", [1, E], I32)
        Rl = self.salloc(sc, "gRl", [P, NT * E], F32)
        pall = self.salloc(sc, "gpall", [P, NT * E], F32)
        ohb = self.salloc(sc, "gohb", [P, E], F32)
        posf2 = [self.salloc(sc, f"gposf{i}", [P, 1], F32) for i in range(2)]
        posi2 = [self.salloc(sc, f"gposi{i}", [P, 1], I32) for i in range(2)]
        tv2 = [self.salloc(sc, f"gtv{i}", [P, 1], F32) for i in range(2)]
        ts2 = [self.salloc(sc, f"gts{i}", [P, 1], F32) for i in range(2)]
        ph = Phase("gate")
        for c in range(DC):
            dma(ph, "sync", sp, xT[:, c * T_:(c + 1) * T_], self.x1fT_hbm.ap()[c])
        dma(ph, "gpsimd", gp, wg_sb[:].rearrange("p (ic e) -> p ic e", ic=DC),
            self.ext["Wg"].ap().rearrange("(ic p) e -> p ic e", p=P))
        dma(ph, "gpsimd", gp, wn_sb[:].rearrange("p (ic e) -> p ic e", ic=DC),
            self.ext["Wn"].ap().rearrange("(ic p) e -> p ic e", p=P))
        for dst, src in [(lt_sb[:], self.ltu_bf.ap()), (iota_sb[:], self.iota_p.ap()),
                         (iota8_sb[:], self.iota8.ap()), (cval_sb[:], self.cvals.ap()),
                         (big_sb[:], self.padg_c.ap()),
                         (big_sb2[:], self.pads_c.ap()),
                         (iotar[:], self.iota_pr.ap())]:
            dma(ph, "sync", sp, dst, src)
        dma(ph, "sync", sp,
            self.idxg_hbm.ap().rearrange("(p b) -> p b", p=P), big_sb[:],
            waits=[(sp, sp.n)])
        dma(ph, "sync", sp,
            self.idxs_hbm.ap().rearrange("(p b) -> p b", p=P), big_sb2[:])
        ph.op("vector", lambda v: v.memset(onescol[:], 1.0), inc=dv)
        ld_rdy = sp.n
        ph.wait("sync", sp, sp.n)
        ph.wait("gpsimd", gp, gp.n)
        g_rdy = gp.n
        with nc.psum_tensor("gpsG", [P, 512], F32) as pgG, \
             nc.psum_tensor("gpsN", [P, 512], F32) as pgN, \
             nc.psum_tensor("gps1", [P, 512], F32) as ps1, \
             nc.psum_tensor("gps2", [P, 512], F32) as ps2:
            st_aux = {}
            for tb in range(NT):
                nl = dma(ph, "sync", nzsem[tb % 2], nzb[tb % 2][:],
                         self.noise.ap()[tb * P:(tb + 1) * P, :],
                         waits=[(dv, dv.n)])
                wlg = [(sp, ld_rdy), (gp, g_rdy)] if tb == 0 else [(dv, dv.n), (ac, ac.n)]
                for ic in range(DC):
                    ph.op("tensor", lambda t, tb=tb, ic=ic: t.matmul(
                        pgG[:, 0:E],
                        xT[:, T_ * ic + P * tb:T_ * ic + P * (tb + 1)],
                        wg_sb[:, E * ic:E * (ic + 1)],
                        start=(ic == 0), stop=False),
                        waits=(wlg if ic == 0 else []), inc=None)
                ph.op("tensor", lambda t: t.matmul(
                    pgG[:, 0:E], self.sb_ones[:], self.r_bg[:],
                    start=False, stop=True), inc=pe)
                for ic in range(DC):
                    ph.op("tensor", lambda t, tb=tb, ic=ic: t.matmul(
                        pgN[:, 0:E],
                        xT[:, T_ * ic + P * tb:T_ * ic + P * (tb + 1)],
                        wn_sb[:, E * ic:E * (ic + 1)],
                        start=(ic == 0), stop=False), inc=None)
                ph.op("tensor", lambda t: t.matmul(
                    pgN[:, 0:E], self.sb_ones[:], self.r_bn[:],
                    start=False, stop=True), inc=pe)
                gmm = pe.n
                a1 = ph.op("scalar", lambda s: s.activation(
                    gt2[:], pgN[:, 0:E], AF.Abs), waits=[(pe, gmm), (dv, dv.n)],
                    inc=ac)
                a2 = ph.op("scalar", lambda s: s.activation(
                    gt2[:], gt2[:], AF.Exp, scale=self.cst[:, 1:2]),
                    waits=[(ac, a1)], inc=ac)
                a3 = ph.op("scalar", lambda s: s.activation(
                    gt2[:], gt2[:], AF.Ln, bias=1.0), waits=[(ac, a2)], inc=ac)
                r1 = ph.op("vector", lambda v: v.tensor_scalar_max(
                    gt1[:], pgN[:, 0:E], 0.0), waits=[(pe, gmm), (dv, dv.n)], inc=dv)
                s1_ = ph.op("vector", lambda v: v.tensor_add(
                    gt1[:], gt1[:], gt2[:]), waits=[(dv, r1), (ac, a3)], inc=dv)
                n1 = ph.op("vector", lambda v, tb=tb: v.tensor_mul(
                    gt1[:], gt1[:], nzb[tb % 2][:]),
                    waits=[(dv, s1_), (nzsem[tb % 2], nl)], inc=dv)
                hx1 = ph.op("vector", lambda v: v.tensor_add(
                    hx[:], pgG[:, 0:E], gt1[:]), waits=[(dv, n1)], inc=dv)
                mm1 = ph.op("vector", lambda v: v.reduce_max(
                    m1[:], hx[:], axis=AX.X), waits=[(dv, hx1)], inc=dv)
                nm1 = ph.op("vector", lambda v: v.reduce_max(
                    negm1[:], hx[:], axis=AX.X, negate=True), waits=[(dv, hx1)],
                    inc=dv)
                mp = ph.op("vector", lambda v: v.memset(m1pad[:], -1e38),
                           waits=[(dv, dv.n)], inc=dv)
                mp2 = ph.op("vector", lambda v: v.tensor_copy(
                    m1pad[:, 0:1], m1[:]), waits=[(dv, mm1), (dv, mp)], inc=dv)
                mr = ph.op("vector", lambda v: v.match_replace(
                    hrep[:], m1pad[:], hx[:], -1e30), waits=[(dv, mp2)], inc=dv)
                mm2 = ph.op("vector", lambda v: v.reduce_max(
                    m2[:], hrep[:], axis=AX.X), waits=[(dv, mr)], inc=dv)
                ii1a = ph.op("vector", lambda v: v.max_index(
                    idx8[:], m1pad[:], hx[:]), waits=[(dv, mp2), (dv, dv.n)],
                    inc=dv)
                ii1 = ph.op("vector", lambda v, tb=tb: v.tensor_copy(
                    i12[:, tb:tb + 1], idx8[:, 0:1]), waits=[(dv, ii1a)], inc=dv)
                mq = ph.op("vector", lambda v: v.memset(m2pad[:], -1e38),
                           waits=[(dv, dv.n)], inc=dv)
                mq2 = ph.op("vector", lambda v: v.tensor_copy(
                    m2pad[:, 0:1], m2[:]), waits=[(dv, mm2), (dv, mq)], inc=dv)
                ii2a = ph.op("vector", lambda v: v.max_index(
                    idx8[:], m2pad[:], hrep[:]), waits=[(dv, mq2)], inc=dv)
                ii2 = ph.op("vector", lambda v, tb=tb: v.tensor_copy(
                    i12[:, NT + tb:NT + tb + 1], idx8[:, 0:1]),
                    waits=[(dv, ii2a)], inc=dv)
                ex1 = ph.op("scalar", lambda s: s.activation(
                    e1t[:], hx[:], AF.Exp, bias=negm1[:]),
                    waits=[(dv, nm1), (dv, hx1)], inc=ac)
                g1_ = ph.op("vector", lambda v: v.tensor_scalar(
                    ge[:], hx[:], m2[:, 0:1], None, ALU.is_ge),
                    waits=[(dv, mm2)], inc=dv)
                sb1 = ph.op("vector", lambda v, tb=tb: v.tensor_copy(
                    selb[:, E * tb:E * (tb + 1)], ge[:]), waits=[(dv, g1_)], inc=dv)
                e1d = ph.op("vector", lambda v: v.tensor_mul(
                    e1m[:], e1t[:], ge[:]), waits=[(dv, g1_), (ac, ex1)], inc=dv)
                zd = ph.op("vector", lambda v: v.reduce_sum(
                    zt[:], e1m[:], axis=AX.X), waits=[(dv, e1d)], inc=dv)
                rz = ph.op("vector", lambda v, tb=tb: v.reciprocal(
                    self.recipZ[:, tb:tb + 1], zt[:]), waits=[(dv, zd)], inc=dv)
                wl_aux = [(dv, zd)]
                if tb >= 2:
                    wl_aux.append((auxsem[tb % 2], st_aux[tb - 2]))
                w1d = ph.op("vector", lambda v, tb=tb: v.reduce_max(
                    aux[tb % 2][:, 0:1], e1m[:], axis=AX.X), waits=wl_aux, inc=dv)
                w2d = ph.op("vector", lambda v, tb=tb: v.tensor_sub(
                    aux[tb % 2][:, 1:2], zt[:], aux[tb % 2][:, 0:1]),
                    waits=[(dv, w1d)], inc=dv)
                st_aux[tb] = dma(ph, "sync", auxsem[tb % 2],
                                 self.aux2_hbm.ap()[tb * P:(tb + 1) * P, :],
                                 aux[tb % 2][:], waits=[(dv, w2d)])
            t1 = ph.op("tensor", lambda t: t.matmul(
                ps1[0:1, 0:NT * E], onescol[:], selb[:], start=True, stop=True),
                waits=[(dv, dv.n)], inc=pe)
            t2 = ph.op("tensor", lambda t: t.matmul(
                ps2[:, 0:NT * E], lt_sb[:], selb[:], start=True, stop=True),
                inc=pe)
            e1_ = ph.op("vector", lambda v: v.tensor_copy(
                totals[:], ps1[0:1, 0:NT * E]), waits=[(pe, t1)], inc=dv)
            e2_ = ph.op("vector", lambda v: v.tensor_copy(
                Rl[:], ps2[:, 0:NT * E]), waits=[(pe, t2)], inc=dv)
            ph.op("vector", lambda v: v.memset(run[:], 0.0),
                  waits=[(dv, dv.n)], inc=dv)
            for tb in range(NT):
                ph.op("vector", lambda v, tb=tb: v.tensor_copy(
                    excl[:, E * tb:E * (tb + 1)], run[:]),
                    waits=[(dv, dv.n)], inc=dv)
                ph.op("vector", lambda v, tb=tb: v.tensor_add(
                    run[:], run[:], totals[:, E * tb:E * (tb + 1)]),
                    waits=[(dv, dv.n)], inc=dv)
            ph.op("vector", lambda v: v.tensor_copy(nrow[:], run[:]),
                  waits=[(dv, dv.n)], inc=dv)
            ph.op("vector", lambda v: v.tensor_scalar_add(pad[:], nrow[:], 511.0),
                  waits=[(dv, dv.n)], inc=dv)
            ph.op("vector", lambda v: v.tensor_copy(padi[:], pad[:]),
                  waits=[(dv, dv.n)], inc=dv)
            ph.op("vector", lambda v: v.tensor_scalar(
                padi[:], padi[:], 9, None, ALU.arith_shift_right),
                waits=[(dv, dv.n)], inc=dv)
            ph.op("vector", lambda v: v.tensor_scalar(
                padi[:], padi[:], 9, None, ALU.logical_shift_left),
                waits=[(dv, dv.n)], inc=dv)
            ph.op("vector", lambda v: v.tensor_copy(pad[:], padi[:]),
                  waits=[(dv, dv.n)], inc=dv)
            ph.op("vector", lambda v: v.memset(run[:], 0.0),
                  waits=[(dv, dv.n)], inc=dv)
            for e_ in range(E):
                ph.op("vector", lambda v, e_=e_: v.tensor_copy(
                    base[:, e_:e_ + 1], run[:, e_:e_ + 1] if False
                    else run[:, 0:1]), waits=[(dv, dv.n)], inc=dv)
                ph.op("vector", lambda v, e_=e_: v.tensor_add(
                    run[:, 0:1], run[:, 0:1], pad[:, e_:e_ + 1]),
                    waits=[(dv, dv.n)], inc=dv)
            ph.op("vector", lambda v: v.tensor_add(endr[:], base[:], pad[:]),
                  waits=[(dv, dv.n)], inc=dv)
            ph.op("vector", lambda v: v.memset(brep[:], 0.0),
                  waits=[(dv, dv.n)], inc=dv)
            for e_ in range(E):
                ph.op("vector", lambda v, e_=e_: v.tensor_scalar_add(
                    brep[0:1, :].rearrange("o (t e) -> o t e", e=E)[:, :, e_:e_ + 1],
                    brep[0:1, :].rearrange("o (t e) -> o t e", e=E)[:, :, e_:e_ + 1],
                    base[:, e_:e_ + 1]), waits=[(dv, dv.n)], inc=dv)
            ph.op("vector", lambda v: v.tensor_add(segrow[:], excl[:], brep[:]),
                  waits=[(dv, dv.n)], inc=dv)
            sgd = ph.op("vector", lambda v: v.memset(echk[:], 8.0),
                        waits=[(dv, dv.n)], inc=dv)
            for e_ in range(E):
                ph.op("vector", lambda v, e_=e_: v.tensor_scalar(
                    tmpc[:], cval_sb[:], base[:, e_:e_ + 1], None, ALU.is_ge),
                    waits=[(dv, dv.n)], inc=dv)
                ph.op("vector", lambda v, e_=e_: v.tensor_scalar(
                    tmpc2[:], cval_sb[:], endr[:, e_:e_ + 1], None, ALU.is_lt),
                    waits=[(dv, dv.n)], inc=dv)
                ph.op("vector", lambda v, e_=e_: v.tensor_mul(
                    tmpc[:], tmpc[:], tmpc2[:]), waits=[(dv, dv.n)], inc=dv)
                ph.op("vector", lambda v, e_=e_: v.scalar_tensor_tensor(
                    echk[:], tmpc[:], float(e_ - 8), echk[:], ALU.mult, ALU.add),
                    waits=[(dv, dv.n)], inc=dv)
            ph.op("vector", lambda v: v.tensor_scalar_min(
                echk[:], echk[:], 7.0), waits=[(dv, dv.n)], inc=dv)
            ecd = ph.op("vector", lambda v: v.tensor_copy(self.idxE_i[:], echk[:]),
                        waits=[(dv, dv.n)], inc=dv)
            b1_ = ph.op("tensor", lambda t: t.matmul(
                ps1[:, 0:NT * E], self.sb_ones32[:], segrow[:],
                start=True, stop=True), waits=[(dv, dv.n), (dv, e1_)], inc=pe)
            pa = ph.op("vector", lambda v: v.tensor_add(
                pall[:], ps1[:, 0:NT * E], Rl[:]),
                waits=[(pe, b1_), (dv, e2_), (dv, dv.n)], inc=dv)
            b2_ = ph.op("tensor", lambda t: t.matmul(
                ps2[:, 0:NCHK], self.sb_ones32[:], echk[:],
                start=True, stop=True), waits=[(dv, dv.n)], inc=pe)
            ph.op("vector", lambda v: v.tensor_scalar(
                self._idxWf_sb[:], ps2[:, 0:NCHK], 128.0, None, ALU.mult),
                waits=[(pe, b2_), (dv, dv.n)], inc=dv)
            ph.op("vector", lambda v: v.tensor_add(
                self._idxWf_sb[:], self._idxWf_sb[:], iotar[:]),
                waits=[(dv, dv.n)], inc=dv)
            iwd = ph.op("vector", lambda v: v.tensor_copy(
                self.idxW_i[:], self._idxWf_sb[:]), waits=[(dv, dv.n)], inc=dv)
            sc_done = {}
            idxg_v = self.idxg_hbm.ap().rearrange("(n o) -> n o", o=1)
            idxs_v = self.idxs_hbm.ap().rearrange("(n o) -> n o", o=1)
            for tb in range(NT):
                for sl in range(2):
                    it = tb * 2 + sl
                    wl0 = [(dv, dv.n)]
                    if it >= 2:
                        wl0.append((gp, gp.n))
                    pr = it % 2
                    oh = ph.op("vector", lambda v, tb=tb, sl=sl: v.tensor_scalar(
                        ohb[:], iota8_sb[:], i12[:, NT * sl + tb:NT * sl + tb + 1],
                        None, ALU.is_equal), waits=wl0, inc=dv)
                    ph.op("vector", lambda v, tb=tb: v.tensor_mul(
                        ohb[:], ohb[:], pall[:, E * tb:E * (tb + 1)]),
                        waits=[(dv, oh), (dv, pa)], inc=dv)
                    ph.op("vector", lambda v, pr=pr: v.reduce_sum(
                        posf2[pr][:], ohb[:], axis=AX.X), waits=[(dv, dv.n)], inc=dv)
                    ph.op("vector", lambda v, pr=pr: v.tensor_copy(
                        posi2[pr][:], posf2[pr][:]), waits=[(dv, dv.n)], inc=dv)
                    pid = ph.op("vector", lambda v, tb=tb, sl=sl, pr=pr:
                                v.tensor_scalar_add(
                        tv2[pr][:], iota_sb[:], float(128 * tb)),
                        waits=[(dv, dv.n)], inc=dv)
                    tsd = ph.op("vector", lambda v, sl=sl, pr=pr:
                                v.tensor_scalar_add(
                        ts2[pr][:], tv2[pr][:], float(self.T * sl)),
                        waits=[(dv, pid)], inc=dv)
                    dma_i = ph.op("gpsimd", lambda g, pr=pr: g.indirect_dma_start(
                        idxg_v, IndirectOffsetOnAxis(posi2[pr][:], 0),
                        tv2[pr][:], None),
                        waits=[(dv, tsd)], inc=gp, inc_n=16)
                    sc_done[it] = ph.op("gpsimd", lambda g, pr=pr:
                                        g.indirect_dma_start(
                        idxs_v, IndirectOffsetOnAxis(posi2[pr][:], 0),
                        ts2[pr][:], None),
                        waits=[], inc=gp, inc_n=16)
            ph.wait("gpsimd", gp, gp.n)
            ph.wait("sync", sp, sp.n)
            for j in range(2):
                ph.wait("sync", auxsem[j], auxsem[j].n)
                ph.wait("sync", nzsem[j], nzsem[j].n)
            ph.realize(nc)

    def phase_moe_sparse(self, sc):
        nc = self.nc
        sp, gp, pe, ac, dv = (self.sem[k] for k in ("sp", "gp", "pe", "ac", "dv"))
        NT, T_, NCHK = self.NT, self.T, self.NCHK
        w1sb = self.salloc(sc, "sw1", [P, DC * FF], BF16)
        w2sb = self.salloc(sc, "sw2", [P, FFC * D], BF16)
        h_sb = self.salloc(sc, "sh", [P, FFC * 512], BF16)
        xg = [self.salloc(sc, f"sxg{i}", [P, D], BF16) for i in range(4)]
        xTs = self.salloc(sc, "sxT", [P, DC * 512], BF16)
        idg = [self.salloc(sc, f"sidg{i}", [P, 4], F32) for i in range(2)]
        ids = [self.salloc(sc, f"sids{i}", [P, 4], F32) for i in range(2)]
        idgi = [self.salloc(sc, f"sidgi{i}", [P, 1], I32) for i in range(4)]
        idsi = [[self.salloc(sc, f"sidsi{j}_{i}", [P, 1], I32) for i in range(4)]
                for j in range(2)]
        idai = [self.salloc(sc, f"sidai{i}", [P, 1], I32) for i in range(4)]
        tmpa = self.salloc(sc, "stmpa", [P, 1], F32)
        tmpb = self.salloc(sc, "stmpb", [P, 1], F32)
        b1g = [self.salloc(sc, f"sb1g{i}", [P, FFC], F32) for i in range(2)]
        idxB = self.salloc(sc, "sidxB", [2, 1], I32)
        idxBf = self.salloc(sc, "sidxBf", [2, 1], F32)
        iota2 = self.salloc(sc, "siota2", [P, 1], F32)
        b2rb = [self.salloc(sc, f"sb2rb{i}", [2, D], BF16) for i in range(2)]
        auxw = [self.salloc(sc, f"sauxw{i}", [P, 4], F32) for i in range(2)]
        osc = [self.salloc(sc, f"sosc{i}", [P, D], F32) for i in range(4)]
        idsem = [self.S(f"sid{i}") for i in range(2)]
        gq = self.S("gq_w2")
        gx = self.S("gx_rows")
        wgsem = [self.S(f"swg{i}") for i in range(2)]
        xgsem = [self.S(f"sxg{i}") for i in range(2)]
        axsem = [self.S(f"sax{i}") for i in range(2)]
        scsem = [self.S(f"ssc{i}") for i in range(4)]
        w1v = self.w1bf.ap().rearrange("e p f -> (e p) f")
        w2v = self.w2bf.ap().rearrange("e p f -> (e p) f")
        b1v = self.ext["b1p"].ap().rearrange("e p f -> (e p) f")
        aux_v = self.aux2_hbm.ap().rearrange("t a -> (t a) 1" if False
                                             else "t a -> (t a)").rearrange(
                                                 "(n o) -> n o", o=1)
        ph = Phase("moeS")
        dma(ph, "sync", sp, iota2[:], self.iota_p.ap())
        iota_rdy = sp.n
        ph.wait("sync", sp, sp.n)

        with nc.psum_tensor("spst0", [P, P], BF16) as pt0, \
             nc.psum_tensor("spst1", [P, P], BF16) as pt1, \
             nc.psum_tensor("spsh0", [P, 512], F32) as h0, \
             nc.psum_tensor("spsh1", [P, 512], F32) as h1, \
             nc.psum_tensor("spso0", [P, 512], F32) as o0, \
             nc.psum_tensor("spso1", [P, 512], F32) as o1:
            pst, psH, psO = [pt0, pt1], [h0, h1], [o0, o1]
            w1_use = w2_use = 0
            idx_use = {}
            sc_st = {}
            self._gx_tgt, self._ga_tgt, self._sc_tgt = {}, {}, {}
            self._pst_free = {}
            self._idg_used = {}
            self._tp_done = {}
            self._pend_sc = []
            for c in range(NCHK):
                cp = c % 2
                wl = [(dv, self._idg_used.get(c - 2, 0))]
                for tbi in range(4):
                    dma(ph, "sync", idsem[cp], idg[cp][:, tbi:tbi + 1],
                        self.idxg_hbm.ap()[512 * c + 128 * tbi:
                                           512 * c + 128 * (tbi + 1)]
                        .rearrange("(p o) -> p o", o=1),
                        waits=(wl if tbi == 0 else []))
                for tbi in range(4):
                    dma(ph, "sync", idsem[cp], ids[cp][:, tbi:tbi + 1],
                        self.idxs_hbm.ap()[512 * c + 128 * tbi:
                                           512 * c + 128 * (tbi + 1)]
                        .rearrange("(p o) -> p o", o=1))
                id_tgt = idsem[cp].n
                tp_ev = {}
                for tbi in range(4):
                    c0 = ph.op("vector", lambda v, cp=cp, tbi=tbi: v.tensor_copy(
                        idgi[tbi][:], idg[cp][:, tbi:tbi + 1]),
                        waits=[(idsem[cp], id_tgt), (dv, dv.n),
                               (gx, gx.n)], inc=dv)
                    c1 = ph.op("vector", lambda v, cp=cp, tbi=tbi: v.tensor_copy(
                        idsi[cp][tbi][:], ids[cp][:, tbi:tbi + 1]),
                        waits=[(dv, dv.n),
                               (scsem[tbi], self._sc_tgt.get((c - 2, tbi), 0))],
                        inc=dv)
                    self._c1t = getattr(self, "_c1t", {})
                    self._c1t[(c, tbi)] = c1
                    d1 = ph.op("vector", lambda v, cp=cp, tbi=tbi: v.tensor_sub(
                        tmpa[:], ids[cp][:, tbi:tbi + 1], idg[cp][:, tbi:tbi + 1]),
                        waits=[(dv, dv.n)], inc=dv)
                    d2 = ph.op("vector", lambda v, tbi=tbi: v.tensor_scalar(
                        tmpb[:], tmpa[:], 1.0 / self.T, None, ALU.mult),
                        waits=[(dv, d1)], inc=dv)
                    d3 = ph.op("vector", lambda v, cp=cp, tbi=tbi:
                               v.scalar_tensor_tensor(
                        tmpa[:], idg[cp][:, tbi:tbi + 1], 2.0, tmpb[:],
                        ALU.mult, ALU.add), waits=[(dv, d2)], inc=dv)
                    d4 = ph.op("vector", lambda v, tbi=tbi: v.tensor_copy(
                        idai[tbi][:], tmpa[:]),
                        waits=[(dv, d3), (gx, gx.n)], inc=dv)
                    gxt = ph.op("gpsimd", lambda g, cp=cp, tbi=tbi:
                               g.indirect_dma_start(
                        xg[tbi][:], None, self.x1fbf.ap(),
                        IndirectOffsetOnAxis(idgi[tbi][:], 0)),
                        waits=[(dv, c0),
                               (pe, self._tp_done.get((c - 1, tbi), 0))],
                        inc=gx, inc_n=16)
                    self._gx_tgt[(c, tbi)] = gxt
                    wla = [(dv, d4)]
                    if tbi == 0:
                        wla.append((ac, sc_st.get((c - 2, "aux"), 0)))
                    ga = ph.op("gpsimd", lambda g, cp=cp, tbi=tbi:
                               g.indirect_dma_start(
                        auxw[cp][:, tbi:tbi + 1], None, aux_v,
                        IndirectOffsetOnAxis(idai[tbi][:], 0)),
                        waits=wla, inc=gx, inc_n=16)
                    self._ga_tgt[(c, tbi)] = ga
                    gx_t = gx.n
                    for ic in range(DC):
                        wlt = [(gx, gx_t)] if ic == 0 else []
                        if (ic % 2) in self._pst_free:
                            wlt = wlt + [self._pst_free[ic % 2]]
                        ph.op("tensor", lambda t, tbi=tbi, ic=ic: t.transpose(
                            pst[ic % 2][:], xg[tbi][:, P * ic:P * (ic + 1)],
                            self.sb_ident[:]), waits=wlt, inc=pe)
                        ev = ph.op("scalar", lambda s2, tbi=tbi, ic=ic: s2.copy(
                            xTs[:, 512 * ic + P * tbi:512 * ic + P * (tbi + 1)],
                            pst[ic % 2][:]), waits=[(pe, pe.n)], inc=ac)
                        self._pst_free[ic % 2] = (ac, ev)
                    tp_ev[(c, tbi)] = pe.n
                    self._tp_done[(c, tbi)] = pe.n
                wg1 = ph.op("gpsimd", lambda g, c=c: g.indirect_dma_start(
                    w1sb[:], None, w1v, IndirectOffsetOnAxis(
                        self.idxW_i[:, c:c + 1], 0)),
                    waits=[(pe, w1_use)], inc=gp, inc_n=16)
                bg1 = ph.op("gpsimd", lambda g, c=c, cp=cp: g.indirect_dma_start(
                    b1g[cp][:], None, b1v, IndirectOffsetOnAxis(
                        self.idxW_i[:, c:c + 1], 0)),
                    waits=[(ac, sc_st.get((c - 2, "b1"), 0))], inc=gp, inc_n=16)
                bi1 = ph.op("vector", lambda v, c=c: v.tensor_sub(
                    idxBf[:], self._idxWf_sb[0:2, c:c + 1], iota2[0:2, :]),
                    waits=[(dv, dv.n), (sp, iota_rdy), (gp, gp.n)], inc=dv)
                bi2 = ph.op("vector", lambda v: v.tensor_scalar(
                    idxBf[:], idxBf[:], 1.0 / 128.0, None, ALU.mult),
                    waits=[(dv, bi1)], inc=dv)
                bi3 = ph.op("vector", lambda v: v.tensor_copy(
                    idxB[:], idxBf[:]), waits=[(dv, bi2)], inc=dv)
                bg2 = ph.op("gpsimd", lambda g, c=c, cp=cp: g.indirect_dma_start(
                    b2rb[cp][:], None, self.ext["b2"].ap(), IndirectOffsetOnAxis(
                        idxB[:], 0)),
                    waits=[(dv, bi3), (pe, sc_st.get((c - 2, "b2use"), 0))],
                    inc=gp, inc_n=16)
                for fgo in self._pend_sc:
                    fgo()
                self._pend_sc = []
                wg2 = ph.op("gpsimd", lambda g, c=c: g.indirect_dma_start(
                    w2sb[:], None, w2v, IndirectOffsetOnAxis(
                        self.idxW_i[:, c:c + 1], 0)),
                    waits=[(pe, w2_use)], inc=gq, inc_n=16)
                gp_w = gp.n
                self._idg_used[c] = dv.n
                gp_now = gp.n
                gx_now = gx.n
                xT_rdy = ac.n
                hev = {}
                for f in range(FFC):
                    wlf = []
                    if f == 0:
                        wlf = [(gp, gp_now), (ac, xT_rdy)]
                    if f >= 2:
                        wlf.append((ac, hev[f - 2]))
                    for ic in range(DC):
                        ph.op("tensor", lambda t, f=f, ic=ic: t.matmul(
                            psH[f % 2][:, 0:512],
                            w1sb[:, FF * ic + P * f:FF * ic + P * (f + 1)],
                            xTs[:, 512 * ic:512 * (ic + 1)],
                            start=(ic == 0), stop=(ic == DC - 1)),
                            waits=(wlf if ic == 0 else []),
                            inc=(pe if ic == DC - 1 else None))
                    mmt = pe.n
                    wl2 = [(pe, mmt)]
                    if f == 0:
                        wl2.append((gp, gp_now))
                    hev[f] = ph.op("scalar", lambda s2, f=f, cp=cp: s2.activation(
                        h_sb[:, 512 * f:512 * (f + 1)], psH[f % 2][:, 0:512],
                        AF.Relu, bias=b1g[cp][:, f:f + 1]),
                        waits=wl2, inc=ac)
                w1_use = pe.n
                h_done = hev[FFC - 1]
                sc_st[(c, "b1")] = h_done
                sttv = {}
                for tbi in range(4):
                    for dh in range(2):
                        gi = tbi * 2 + dh
                        wl3 = []
                        if gi == 0:
                            wl3 = [(ac, h_done), (gq, gq.n)]
                        if gi >= 2:
                            wl3.append((ac, sttv[gi - 2]))
                        for f in range(FFC):
                            ph.op("tensor", lambda t, tbi=tbi, dh=dh, f=f, gi=gi:
                                  t.matmul(
                                psO[gi % 2][:, 0:512],
                                h_sb[:, 512 * f + P * tbi:512 * f + P * (tbi + 1)],
                                w2sb[:, D * f + 512 * dh:D * f + 512 * (dh + 1)],
                                start=(f == 0), stop=False),
                                waits=(wl3 if f == 0 else []), inc=None)
                        ph.op("tensor", lambda t, dh=dh, gi=gi, cp=cp: t.matmul(
                            psO[gi % 2][:, 0:512], self.sb_ones[:],
                            b2rb[cp][0:1, 512 * dh:512 * (dh + 1)],
                            start=False, stop=True),
                            waits=[(gp, gp_w)] if gi == 0 else [],
                            inc=pe)
                        mmt = pe.n
                        wl4 = [(pe, mmt), (gx, gx_now)]
                        if dh == 0:
                            wl4.append((scsem[tbi],
                                        self._sc_tgt.get((c - 1, tbi), 0)))
                        sttv[gi] = ph.op(
                            "scalar", lambda s2, tbi=tbi, dh=dh, gi=gi, cp=cp:
                            s2.activation(
                                osc[tbi][:, 512 * dh:512 * (dh + 1)],
                                psO[gi % 2][:, 0:512], AF.Identity,
                                scale=auxw[cp][:, tbi:tbi + 1]),
                            waits=wl4, inc=ac)
                    def mk_scatter(c=c, cp=cp, tbi=tbi, ev=sttv[tbi * 2 + 1],
                                   c1e=self._c1t[(c, tbi)]):
                        def go():
                            self._sc_tgt[(c, tbi)] = ph.op(
                                "gpsimd", lambda g: g.indirect_dma_start(
                                    self.moe2_hbm.ap(),
                                    IndirectOffsetOnAxis(idsi[cp][tbi][:], 0),
                                    osc[tbi][:], None),
                                waits=[(ac, ev), (dv, c1e)],
                                inc=scsem[tbi], inc_n=16)
                        return go
                    self._pend_sc.append(mk_scatter())
                w2_use = pe.n
                sc_st[(c, "aux")] = sttv[7]
                sc_st[(c, "b2use")] = pe.n
                idx_use[c] = pe.n
            for fgo in self._pend_sc:
                fgo()
            self._pend_sc = []
            ph.wait("gpsimd", gp, gp.n)
            ph.wait("gpsimd", gq, gq.n)
            ph.wait("gpsimd", gx, gx.n)
            for j in range(4):
                ph.wait("gpsimd", scsem[j], scsem[j].n)
            for j in range(2):
                ph.wait("sync", idsem[j], idsem[j].n)
            ph.realize(nc)

    def phase_final(self, sc):
        nc = self.nc
        sp, gp, pe, ac, dv = (self.sem[k] for k in ("sp", "gp", "pe", "ac", "dv"))
        NT = self.NT
        mo = [self.salloc(sc, f"fmo{i}", [P, D], F32) for i in range(2)]
        mo2 = [self.salloc(sc, f"fmo2{i}", [P, D], F32) for i in range(2)]
        x1fe = [self.salloc(sc, f"x1fe{i}", [P, D], F32) for i in range(2)]
        x1e = [self.salloc(sc, f"x1e{i}", [P, D], F32) for i in range(2)]
        fb = self.salloc(sc, "fb", [P, D], F32)
        yb = [self.salloc(sc, f"yb{i}", [P, D], F32) for i in range(2)]
        sqm = self.salloc(sc, "sqm", [P, D], F32)
        statm = [self.salloc(sc, f"mst{i}", [P, 8], F32) for i in range(4)]
        g2bc = self.salloc(sc, "g2bcl", [P, D], BF16)
        be2bc = self.salloc(sc, "be2bcl", [P, D], BF16)
        mosem = [self.S(f"fmo{i}") for i in range(2)]
        x1fesem = [self.S(f"x1fe{i}") for i in range(2)]
        x1esem = [self.S(f"x1e{i}") for i in range(2)]
        ysem = [self.S(f"yst{i}") for i in range(2)]
        ph = Phase("final")
        dma(ph, "sync", sp, g2bc[:], self.gb_hbm.ap()[2 * P:3 * P, :])
        dma(ph, "sync", sp, be2bc[:], self.gb_hbm.ap()[3 * P:4 * P, :])
        ph.wait("sync", sp, sp.n)
        ystore, chend = {}, {}
        for tb in range(NT):
            w = [(dv, chend[tb - 2])] if tb >= 2 else []
            mot = dma(ph, "sync", mosem[tb % 2], mo[tb % 2][:],
                      self.moe2_hbm.ap()[tb * P:(tb + 1) * P, :], waits=w)
            mot = dma(ph, "sync", mosem[tb % 2], mo2[tb % 2][:],
                      self.moe2_hbm.ap()[self.T + tb * P:self.T + (tb + 1) * P, :])
            xft = dma(ph, "sync", x1fesem[tb % 2], x1fe[tb % 2][:],
                      self.x1ff32.ap()[tb * P:(tb + 1) * P, :])
            x1t = dma(ph, "sync", x1esem[tb % 2], x1e[tb % 2][:],
                      self.x1f32.ap()[tb * P:(tb + 1) * P, :])
            ad = ph.op("vector", lambda v, tb=tb: v.tensor_add(
                mo[tb % 2][:], mo[tb % 2][:], mo2[tb % 2][:]),
                waits=[(dv, dv.n), (mosem[tb % 2], mot)], inc=dv)
            sd = ph.op("vector", lambda v, tb=tb: v.scalar_tensor_tensor(
                fb[:], mo[tb % 2][:], self.recipZ[:, tb:tb + 1], x1fe[tb % 2][:],
                ALU.mult, ALU.add),
                waits=[(dv, ad), (x1fesem[tb % 2], xft)], inc=dv)
            self._ln_tile(ph, src=fb[:], pre_wait=[(dv, sd)],
                          sq=sqm, stat=statm[tb % 2], out=fb[:],
                          out_guard=(dv, dv.n))
            g2d = ph.op("vector", lambda v: v.tensor_mul(
                fb[:], fb[:], g2bc[:]), waits=[(dv, dv.n)], inc=dv)
            b2d = ph.op("vector", lambda v: v.tensor_add(
                fb[:], fb[:], be2bc[:]), waits=[(dv, g2d)], inc=dv)
            self._ln_tile(ph, src=fb[:], pre_wait=[(dv, b2d)],
                          sq=sqm, stat=statm[2 + tb % 2], out=fb[:],
                          out_guard=(dv, dv.n))
            yd = ph.op("vector", lambda v, tb=tb: v.tensor_add(
                yb[tb % 2][:], x1e[tb % 2][:], fb[:]),
                waits=[(dv, dv.n), (x1esem[tb % 2], x1t),
                       (ysem[tb % 2], ystore.get(tb - 2, 0))], inc=dv)
            chend[tb] = dv.n
            ystore[tb] = dma(ph, "sync", ysem[tb % 2],
                             self.y.ap()[tb * P:(tb + 1) * P, :],
                             yb[tb % 2][:], waits=[(dv, yd)])
        for j in range(2):
            ph.wait("sync", ysem[j], ysem[j].n)
        ph.realize(nc)

    def _pc_done_tgt(self):
        return self.sem["gp"].n


def kernel(**inputs):
    T = 4096
    b = Builder(T)
    nc = b.build()
    names = {"x": None}
    base = {
        "aWq": inputs["L0_attn_Wq"], "abq": inputs["L0_attn_bq"],
        "aWk": inputs["L0_attn_Wk"], "abk": inputs["L0_attn_bk"],
        "aWv": inputs["L0_attn_Wv"], "abv": inputs["L0_attn_bv"],
        "aWo": inputs["L0_attn_Wo"], "abo": inputs["L0_attn_bo"],
        "fWdkv": inputs["L0_ffn_Wdkv"], "fbdkv": inputs["L0_ffn_bdkv"],
        "fWuk": inputs["L0_ffn_Wuk"], "fbuk": inputs["L0_ffn_buk"],
        "fWuv": inputs["L0_ffn_Wuv"], "fbuv": inputs["L0_ffn_buv"],
        "fWdq": inputs["L0_ffn_Wdq"], "fbdq": inputs["L0_ffn_bdq"],
        "fWuq": inputs["L0_ffn_Wuq"], "fbuq": inputs["L0_ffn_buq"],
        "fWo": inputs["L0_ffn_Wo"], "fbo": inputs["L0_ffn_bo"],
        "g1": inputs["L0_ffn_gamma1"], "be1": inputs["L0_ffn_beta1"],
        "g2": inputs["L0_ffn_gamma2"], "be2": inputs["L0_ffn_beta2"],
        "Wg": inputs["L0_ffn_Wg"], "bg": inputs["L0_ffn_bg"],
        "Wn": inputs["L0_ffn_Wn"], "bn": inputs["L0_ffn_bn"],
        "W1t": np.asarray(inputs["L0_ffn_W1"], dtype=np.float32).reshape(
            E, DC, P, FF).transpose(0, 2, 1, 3).reshape(E, P, DC * FF),
        "b1p": np.asarray(inputs["L0_ffn_b1"], dtype=np.float32).reshape(
            E, FFC, P).transpose(0, 2, 1),
        "W2t": np.asarray(inputs["L0_ffn_W2"], dtype=np.float32).reshape(
            E, FFC, P, D).transpose(0, 2, 1, 3).reshape(E, P, FFC * D),
        "b2": inputs["L0_ffn_b2"],
    }
    base = {k: np.ascontiguousarray(np.asarray(v, dtype=np.float32))
            for k, v in base.items()}
    import jax
    import jax.numpy as jnp
    with jax.default_device(jax.devices("cpu")[0]):
        key = jax.random.wrap_key_data(jnp.array([0, 42], dtype=jnp.uint32),
                                       impl="threefry2x32")
        noise = np.asarray(jax.random.normal(key, (8, T, E), dtype=np.float32))
    xs = np.ascontiguousarray(np.asarray(inputs["L0_attn_hidden_states"],
                                         dtype=np.float32))
    in_maps = []
    for c in range(8):
        m = dict(base)
        m["x"] = xs[c]
        m["noise"] = np.ascontiguousarray(noise[c])
        in_maps.append(m)
    res = run_bass_kernel_spmd(nc, in_maps, core_ids=list(range(8)))
    return np.stack([np.asarray(r["y"]) for r in res.results], axis=0)


# version 13
# speedup vs baseline: 1.2550x; 1.0265x over previous
import numpy as np
import ml_dtypes
import concourse.bass as bass
import concourse.mybir as mybir
from concourse.bass import IndirectOffsetOnAxis
from concourse.bass_utils import run_bass_kernel_spmd

F32 = mybir.dt.float32
BF16 = mybir.dt.bfloat16
AF = mybir.ActivationFunctionType
ALU = mybir.AluOpType
AX = mybir.AxisListType
BF = ml_dtypes.bfloat16

P = 128
D = 1024
DC = D // P
NH = 16
DH = 64
NHP = NH // 2
L = 512
LC = L // P
E = 8
FF = 4096
FFC = FF // P
QSB = 512
NEG_A = -10000.0
NEG_B = -9e15
EPS = 1e-5
SCALE = 0.125
I32 = mybir.dt.int32



def _sparse_mask_bool(T):
    pos = np.arange(T)
    dist = pos[None, :] - pos[:, None]
    window = np.abs(dist) <= 64
    num_global = max(1, int(T * 0.1))
    isg = pos < num_global
    return window | isg[:, None] | isg[None, :]


def _causal_mask_bool(T):
    return np.tril(np.ones((T, T), dtype=bool))


def _classify(mask_qk, T, neg):
    mb = mask_qk.T
    nq, nk = T // QSB, T // P
    tiles, index, cls, kb_lists = [], {}, {}, []
    for qsb in range(nq):
        lst = []
        for kb in range(nk):
            blk = mb[kb * P:(kb + 1) * P, qsb * QSB:(qsb + 1) * QSB]
            if not blk.any():
                continue
            lst.append(kb)
            if blk.all():
                cls[(qsb, kb)] = 'clean'
            else:
                key = blk.tobytes()
                if key not in index:
                    index[key] = len(tiles)
                    tiles.append(np.where(blk, 0.0, neg).astype(np.float32))
                cls[(qsb, kb)] = ('tile', index[key])
        kb_lists.append(lst)
    return tiles, cls, kb_lists



class Sem:
    def __init__(self, nc, name):
        self.h = nc.alloc_semaphore(name)
        self.n = 0
        self.name = name

    def bump(self, k):
        self.n += k
        return self.n


class Phase:
    ENGS = ("tensor", "scalar", "vector", "sync", "gpsimd")

    def __init__(self, name):
        self.name = name
        self.ops = {e: [] for e in self.ENGS}

    def op(self, eng, fn, waits=(), inc=None, inc_n=1):
        tgt = inc.bump(inc_n) if inc is not None else None
        waits = tuple((s, v) for (s, v) in waits)

        def _f(e, fn=fn, waits=waits, inc=inc, inc_n=inc_n):
            for s, v in waits:
                e.wait_ge(s.h, v)
            ins = fn(e)
            if inc is not None:
                ins.then_inc(inc.h, inc_n)
        self.ops[eng].append(_f)
        return tgt

    def wait(self, eng, sem, val):
        self.ops[eng].append(lambda e, s=sem, v=val: e.wait_ge(s.h, v))

    def realize(self, nc):
        with nc.Block(self.name) as blk:
            for eng in self.ENGS:
                lst = self.ops[eng]
                if not lst:
                    continue
                getattr(blk, eng)(lambda e, lst=lst: [f(e) for f in lst])


def dma(ph, eng, sem, dst, src, waits=(), **kw):
    sem.eng = eng
    return ph.op(eng, lambda e: e.dma_start(dst, src, **kw), waits=waits,
                 inc=sem, inc_n=16)



class Builder:
    def __init__(self, T=4096, dbg=False):
        self.dbg = dbg
        self.T = T
        self.NT = T // P
        self.NQ = T // QSB
        nc = bass.Bass("TRN2", target_bir_lowering=False)
        self.nc = nc
        self.sem = {k: Sem(nc, k) for k in ("sp", "gp", "pe", "ac", "dv")}
        self._semcache = {}

        ext = {}
        def inp(name, shape):
            ext[name] = nc.dram_tensor(name, list(shape), F32, kind="ExternalInput")
            return ext[name]
        self.x = inp("x", (T, D))
        self.noise = inp("noise", (T, E))
        for nm, shp in [
            ("aWq", (D, D)), ("abq", (D,)), ("aWk", (D, D)), ("abk", (D,)),
            ("aWv", (D, D)), ("abv", (D,)), ("aWo", (D, D)), ("abo", (D,)),
            ("fWdkv", (D, L)), ("fbdkv", (L,)), ("fWuk", (L, D)), ("fbuk", (D,)),
            ("fWuv", (L, D)), ("fbuv", (D,)), ("fWdq", (D, L)), ("fbdq", (L,)),
            ("fWuq", (L, D)), ("fbuq", (D,)), ("fWo", (D, D)), ("fbo", (D,)),
            ("g1", (D,)), ("be1", (D,)), ("g2", (D,)), ("be2", (D,)),
            ("Wg", (D, E)), ("bg", (E,)), ("Wn", (D, E)), ("bn", (E,)),
            ("W1t", (E, P, DC * FF)), ("b1p", (E, P, FFC)), ("W2t", (E, P, FFC * D)), ("b2", (E, D)),
        ]:
            inp(nm, shp)
        self.ext = ext
        self.y = nc.dram_tensor("y", [T, D], F32, kind="ExternalOutput")

        T_ = self.T
        dt_ = lambda n, s, d: nc.dram_tensor(
            n, list(s), d, kind="ExternalOutput" if self.dbg else "Internal")
        self.ln0bf = dt_("ln0bf", (T_, D), BF16)
        self.ln1bf = dt_("ln1bf", (T_, D), BF16)
        self.x1f32 = dt_("x1f32", (T_, D), F32)
        self.x1ff32 = dt_("x1ff32", (T_, D), F32)
        self.x1fbf = dt_("x1fbf", (T_, D), BF16)
        self.x1fT_hbm = dt_("x1fT_hbm", (DC, P, T_), BF16)
        self.qt_hbm = dt_("qt_hbm", (NHP, P, T_), BF16)
        self.kt_hbm = dt_("kt_hbm", (NHP, P, T_), BF16)
        self.v_hbm = dt_("v_hbm", (NHP, P, 2 * 65 * self.NT), BF16)
        self.kt2_hbm = dt_("kt2_hbm", (NHP, P, T_), BF16)
        self.qt2_hbm = dt_("qt2_hbm", (NHP, P, T_), BF16)
        self.v2_hbm = dt_("v2_hbm", (NHP, P, 2 * 65 * self.NT), BF16)
        self.moe_hbm = dt_("moe_hbm", (T_, D), F32)
        self.NCHK = (2 * T_ + 8 * 511 + 511) // 512
        self.moe2_hbm = dt_("moe2_hbm", (2 * T_ + 2048, D), BF16)
        self.aux2_hbm = dt_("aux2_hbm", (T_, 2), F32)
        self.gb_hbm = dt_("gb_hbm", (4 * P, D), BF16)
        self.idxg_hbm = dt_("idxg_hbm", (self.NCHK * 512,), F32)
        self.idxs_hbm = dt_("idxs_hbm", (self.NCHK * 512,), F32)
        self.w1bf = dt_("w1bf", (E, P, DC * FF), BF16)
        self.w2bf = dt_("w2bf", (E, P, FFC * D), BF16)

        self.ident_bf = nc.inline_tensor(np.eye(P, dtype=BF), "ident_bf")
        self.ident_f32 = nc.inline_tensor(np.eye(P, dtype=np.float32), "ident_f32")
        self.ones_bf = nc.inline_tensor(np.ones((1, P), dtype=BF), "ones_bf")
        self.ones_f32 = nc.inline_tensor(np.ones((1, P), dtype=np.float32), "ones_f32")
        self.ltu_bf = nc.inline_tensor(np.triu(np.ones((P, P), dtype=BF), 1), "ltu_bf")
        self.iota_p = nc.inline_tensor(np.arange(P, dtype=np.float32).reshape(P, 1),
                                       "iota_p")
        self.iota8 = nc.inline_tensor(
            np.tile(np.arange(E, dtype=np.float32), (P, 1)), "iota8")
        self.cvals = nc.inline_tensor(
            (512.0 * np.arange(self.NCHK, dtype=np.float32)).reshape(1, self.NCHK),
            "cvals")
        self.padg_c = nc.inline_tensor(np.tile(
            np.arange(P, dtype=np.float32).reshape(P, 1),
            (1, self.NCHK * 4)), "padg_c")
        self.pads_c = nc.inline_tensor(
            (2.0 * T_ + (np.arange(P * self.NCHK * 4) % 2048))
            .astype(np.float32).reshape(P, self.NCHK * 4), "pads_c")
        self.iota_pr = nc.inline_tensor(
            np.tile(np.arange(P, dtype=np.float32).reshape(P, 1),
                    (1, self.NCHK)), "iota_pr")
        tA, cA, kA = _classify(_sparse_mask_bool(T_), T_, NEG_A)
        tB, cB, kB = _classify(_causal_mask_bool(T_), T_, NEG_B)
        self.maskA = (nc.inline_tensor(np.stack(tA).reshape(len(tA) * P, QSB), "maskA")
                      if tA else None)
        self.maskB = (nc.inline_tensor(np.stack(tB).reshape(len(tB) * P, QSB), "maskB")
                      if tB else None)
        self.nmA, self.clsA, self.kbA = len(tA), cA, kA
        self.nmB, self.clsB, self.kbB = len(tB), cB, kB

        A = nc.alloc_sbuf_tensor
        self.sb_ident = A("sb_ident", [P, P], BF16)
        self.sb_ident32 = A("sb_ident32", [P, P], F32)
        self.sb_ones = A("sb_ones", [1, P], BF16)
        self.sb_ones32 = A("sb_ones32", [1, P], F32)
        self.b_bqT = A("b_bqT", [P, DC], F32)
        self.b_bkT = A("b_bkT", [P, DC], F32)
        self.b_bdkvT = A("b_bdkvT", [P, LC], F32)
        self.b_bdqT = A("b_bdqT", [P, LC], F32)
        self.b_bukT = A("b_bukT", [P, DC], F32)
        self.b_buqT = A("b_buqT", [P, DC], F32)
        self.r_bv = A("r_bv", [1, D], BF16)
        self.r_bo = A("r_bo", [1, D], BF16)
        self.r_buv = A("r_buv", [1, D], BF16)
        self.r_fbo = A("r_fbo", [1, D], BF16)
        self.r_bg = A("r_bg", [1, E], BF16)
        self.r_bn = A("r_bn", [1, E], BF16)
        self.recipZ = A("recipZ", [P, self.NT], F32)
        self.idxW_i = A("idxW_i", [P, (2 * self.T + 8 * 511 + 511) // 512], I32)
        self._idxWf_sb = A("idxWf_p", [P, (2 * self.T + 8 * 511 + 511) // 512], F32)
        self.idxE_i = A("idxE_i", [1, (2 * self.T + 8 * 511 + 511) // 512], I32)
        self.cst = A("cst", [P, 2], F32)
        self._pc_jobs = None
        self._alive = []

    def _preconvert_jobs(self):
        for e in range(E):
            for g in range(DC):
                src = self.ext["W1t"].ap()[e, :, g * FF:(g + 1) * FF]
                dst = self.w1bf.ap()[e, :, g * FF:(g + 1) * FF]
                yield (src, dst, [P, FF])
            for g in range(E):
                src = self.ext["W2t"].ap()[e, :, g * FF:(g + 1) * FF]
                dst = self.w2bf.ap()[e, :, g * FF:(g + 1) * FF]
                yield (src, dst, [P, FF])

    def emit_preconvert(self, ph, count):
        if self._pc_jobs is None:
            self._pc_jobs = self._preconvert_jobs()
            self._pc_store_tgt = [0, 0]
            self._pc_i = 0
        gp = self.sem["gp"]
        for _ in range(count):
            job = next(self._pc_jobs, None)
            if job is None:
                return
            src, dst, shp = job
            i = self._pc_i
            st = self.pcs[i % 2][:, 0:shp[1]]
            if len(src.shape) == 3:
                st = st.rearrange("p (a n) -> p a n", a=src.shape[1])
            ld = dma(ph, "gpsimd", gp, st, src, waits=[(gp, gp.n)])
            dma(ph, "gpsimd", gp, dst, st, waits=[(gp, ld)])
            self._pc_i = i + 1

    def build(self):
        from contextlib import ExitStack
        self.phase_setup()
        self.scP = ExitStack()
        self.pcs = [self.scP.enter_context(
            self.nc.sbuf_tensor(f"pcs{i}", [P, FF], BF16)) for i in range(2)]
        with ExitStack() as scA:
            self.scA = scA
            self.phase_ln0()
            self.phase_qkvA()
        with ExitStack() as scB1:
            self.ctxT = scB1.enter_context(
                self.nc.sbuf_tensor("ctxT", [P, NHP * self.T], BF16))
            with ExitStack() as sc:
                self.phase_attn(sc, "A")
            with ExitStack() as sc:
                self.phase_woA(sc)
        with ExitStack() as scC:
            self.ckvT = scC.enter_context(
                self.nc.sbuf_tensor("ckvT", [P, LC * self.T], BF16))
            self.cqT = scC.enter_context(
                self.nc.sbuf_tensor("cqT", [P, LC * self.T], BF16))
            with ExitStack() as sc:
                self.phase_dkv(sc)
            with ExitStack() as sc:
                self.phase_uproj(sc)
        with ExitStack() as scB2:
            self.ctxT = scB2.enter_context(
                self.nc.sbuf_tensor("ctxT2", [P, NHP * self.T], BF16))
            with ExitStack() as sc:
                self.phase_attn(sc, "B")
            with ExitStack() as sc:
                self.phase_woB(sc)
        with ExitStack() as sc:
            self.phase_x1fT(sc)
        self.scP.close()
        with ExitStack() as sc:
            self.phase_gate(sc)
        with ExitStack() as sc:
            self.phase_moe_sparse(sc)
        with ExitStack() as sc:
            self.phase_final(sc)
        self.finish()
        return self.nc

    def salloc(self, stack, name, shape, dtype):
        return stack.enter_context(self.nc.sbuf_tensor(name, shape, dtype))

    def finish(self):
        ph = Phase("finish")
        for sem in list(self.sem.values()) + list(self._semcache.values()):
            if sem.n > 0 and getattr(sem, "eng", None):
                ph.wait(sem.eng, sem, sem.n)
        ph.realize(self.nc)

    def S(self, name):
        if name not in self._semcache:
            self._semcache[name] = Sem(self.nc, name)
        return self._semcache[name]

    def alloc(self, name, shape, dtype):
        t = self.nc.alloc_sbuf_tensor(name, shape, dtype)
        self._alive.append(t)
        return t

    def phase_setup(self):
        nc, sem = self.nc, self.sem
        sp, gp, pe, ac, dv = (sem[k] for k in ("sp", "gp", "pe", "ac", "dv"))
        ph = Phase("setup")
        ph.op("vector", lambda v: v.memset(self.cst[:, 0:1], EPS), inc=dv)
        ph.op("vector", lambda v: v.memset(self.cst[:, 1:2], -1.0), inc=dv)
        dma(ph, "sync", sp, self.sb_ident[:], self.ident_bf.ap())
        dma(ph, "sync", sp, self.sb_ident32[:], self.ident_f32.ap())
        dma(ph, "sync", sp, self.sb_ones[:], self.ones_bf.ap())
        dma(ph, "sync", sp, self.sb_ones32[:], self.ones_f32.ap())
        for dst, src in [(self.r_bv, "abv"), (self.r_bo, "abo"), (self.r_buv, "fbuv"),
                         (self.r_fbo, "fbo"), (self.r_bg, "bg"), (self.r_bn, "bn")]:
            dma(ph, "gpsimd", gp, dst[:], self.ext[src].ap().rearrange("(o d) -> o d", o=1))
        from contextlib import ExitStack
        self._scS = ExitStack()
        stage = self.salloc(self._scS, "bias_stage", [P, P], F32)
        rows = self.salloc(self._scS, "grows", [1, 4 * D], F32)
        gbc = self.salloc(self._scS, "gbc_tmp", [P, D], BF16)
        for i, nm in enumerate(("g1", "be1", "g2", "be2")):
            dma(ph, "sync", sp, rows[:, i * D:(i + 1) * D],
                self.ext[nm].ap().rearrange("(o d) -> o d", o=1))
        jobs = [(self.b_bqT, "abq", DC), (self.b_bkT, "abk", DC),
                (self.b_bdkvT, "fbdkv", LC), (self.b_bdqT, "fbdq", LC),
                (self.b_bukT, "fbuk", DC), (self.b_buqT, "fbuq", DC)]

        with nc.psum_tensor("ps_setup", [P, P], F32) as pst, \
             nc.psum_tensor("ps_bc", [P, 512], F32) as psb:
            tp_tgt = 0
            for dst, src, n in jobs:
                if isinstance(src, tuple):
                    srcap = self.ext[src[0]].ap()[src[1]].rearrange("(c p) -> c p", p=P)
                else:
                    srcap = self.ext[src].ap().rearrange("(c p) -> c p", p=P)
                tgt = dma(ph, "sync", sp, stage[0:n, :], srcap)
                ph.ops["sync"].insert(len(ph.ops["sync"]) - 1,
                    (lambda e, s_=pe, v=tp_tgt: e.wait_ge(s_.h, v)))
                tp_tgt = ph.op("tensor", lambda t, n=n: t.transpose(
                    pst[:, 0:n], stage[0:n, :], self.sb_ident32[0:n, 0:n]),
                    waits=[(sp, tgt), (ac, ac.n)], inc=pe)
                if dst is None:
                    e_ = src[1]
                    out = self.b_b1T[:, e_ * FFC:(e_ + 1) * FFC]
                else:
                    out = dst[:, 0:n]
                ph.op("scalar", lambda s, out=out, n=n: s.copy(out, pst[:, 0:n]),
                      waits=[(pe, pe.n)], inc=ac)
            grow_tgt = sp.n
            for i in range(4):
                for h2 in range(2):
                    ph.op("tensor", lambda t, i=i, h2=h2: t.matmul(
                        psb[:, 0:512], self.sb_ones32[:],
                        rows[:, i * D + h2 * 512: i * D + (h2 + 1) * 512],
                        start=True, stop=True),
                        waits=[(sp, grow_tgt), (dv, dv.n), (sp, sp.n)], inc=pe)
                    ph.op("vector", lambda v, h2=h2: v.tensor_copy(
                        gbc[:, h2 * 512:(h2 + 1) * 512], psb[:, 0:512]),
                        waits=[(pe, pe.n)], inc=dv)
                dma(ph, "sync", sp, self.gb_hbm.ap()[i * P:(i + 1) * P, :],
                    gbc[:], waits=[(dv, dv.n)])
            ph.wait("sync", sp, sp.n)
            ph.wait("gpsimd", gp, gp.n)
            ph.realize(nc)
        self._scS.close()

    def phase_ln0(self):
        nc, sem = self.nc, self.sem
        sp, gp, pe, ac, dv = (sem[k] for k in ("sp", "gp", "pe", "ac", "dv"))
        self.ln0T = self.salloc(self.scA, "ln0T", [P, DC * self.T], BF16)
        from contextlib import ExitStack
        self.sc0 = self.scA.enter_context(ExitStack())
        sc0 = self.sc0
        xb = [self.salloc(sc0, f"xb{i}", [P, D], F32) for i in range(3)]
        sq = self.salloc(sc0, "sq0", [P, D], F32)
        lnb = [self.salloc(sc0, f"lnb{i}", [P, D], BF16) for i in range(2)]
        stat = [self.salloc(sc0, f"stat{i}", [P, 8], F32) for i in range(2)]
        ph = Phase("ln0")
        xsem = [self.S(f"xb{j}") for j in range(3)]
        lnsem = [self.S(f"lnst{j}") for j in range(2)]
        chain_end, ln_store = {}, {}
        for i in range(self.NT):
            w = [(dv, chain_end[i - 3])] if i >= 3 else []
            x_tgt = dma(ph, "sync", xsem[i % 3], xb[i % 3][:],
                        self.x.ap()[i * P:(i + 1) * P, :], waits=w)
            self._ln_tile(ph, src=xb[i % 3][:], pre_wait=[(xsem[i % 3], x_tgt)],
                          sq=sq, stat=stat[i % 2], out=lnb[i % 2][:],
                          out_guard=(lnsem[i % 2], ln_store.get(i - 2, 0)))
            chain_end[i] = dv.n
            ln_store[i] = dma(ph, "sync", lnsem[i % 2],
                              self.ln0bf.ap()[i * P:(i + 1) * P, :], lnb[i % 2][:],
                              waits=[(dv, chain_end[i])])
        ph.wait("sync", lnsem[0], lnsem[0].n)
        ph.wait("sync", lnsem[1], lnsem[1].n)
        for c in range(DC):
            dma(ph, "sync", sp, self.ln0T[:, c * self.T:(c + 1) * self.T],
                self.ln0bf.ap()[:, c * P:(c + 1) * P], transpose=True)
        self.ln0T_ready = sp.n
        self.emit_preconvert(ph, 20)
        ph.realize(nc)
        self.sc0.close()

    def _ln_tile(self, ph, src, pre_wait, sq, stat, out, out_guard,
                 add=None, gamma=None, beta=None, out2=None, res=None):
        sem = self.sem
        pe, ac, dv = sem["pe"], sem["ac"], sem["dv"]
        ph.op("scalar", lambda s: s.activation(
            sq[:], src, AF.Square, accum_out=stat[:, 1:2]),
            waits=pre_wait + [(dv, dv.n)], inc=ac)
        ph.op("vector", lambda v: v.reduce_sum(stat[:, 0:1], src, axis=AX.X),
              waits=pre_wait + [(ac, ac.n)], inc=dv)
        ph.op("vector", lambda v: v.tensor_scalar(
            stat[:, 2:3], stat[:, 0:1], 1.0 / D, None, ALU.mult),
            waits=[(dv, dv.n), (ac, ac.n)], inc=dv)
        ph.op("vector", lambda v: v.tensor_tensor(
            stat[:, 3:4], stat[:, 2:3], stat[:, 2:3], ALU.mult),
            waits=[(dv, dv.n)], inc=dv)
        ph.op("vector", lambda v: v.tensor_scalar(
            stat[:, 4:5], stat[:, 1:2], 1.0 / D, None, ALU.mult),
            waits=[(dv, dv.n)], inc=dv)
        ph.op("vector", lambda v: v.tensor_sub(
            stat[:, 4:5], stat[:, 4:5], stat[:, 3:4]), waits=[(dv, dv.n)], inc=dv)
        ph.op("scalar", lambda s: s.activation(
            stat[:, 5:6], stat[:, 4:5], AF.Sqrt, bias=self.cst[:, 0:1]),
            waits=[(dv, dv.n)], inc=ac)
        ph.op("vector", lambda v: v.reciprocal(stat[:, 6:7], stat[:, 5:6]),
              waits=[(ac, ac.n)], inc=dv)
        ph.op("vector", lambda v: v.tensor_scalar(
            stat[:, 7:8], stat[:, 2:3], stat[:, 6:7], -1.0, ALU.mult, ALU.mult),
            waits=[(dv, dv.n)], inc=dv)
        ph.op("vector", lambda v: v.tensor_scalar(
            out, src, stat[:, 6:7], stat[:, 7:8], ALU.mult, ALU.add),
            waits=[(dv, dv.n), out_guard], inc=dv)



    def _emit_wT(self, ph, sc, wname, wt, pst, stage, stgsem):
        pe, ac = self.sem["pe"], self.sem["ac"]
        tp_tgt = {}
        for r in range(DC):
            w = [(pe, tp_tgt[r - 2])] if r >= 2 else []
            ld = dma(ph, "gpsimd", stgsem[r % 2], stage[r % 2][:],
                     self.ext[wname].ap()[r * P:(r + 1) * P, :], waits=w)
            for c in range(DC):
                ph.op("tensor", lambda t, r=r, c=c: t.transpose(
                    pst[c % 2][0:P, 0:P], stage[r % 2][:, c * P:(c + 1) * P],
                    self.sb_ident[:]),
                    waits=[(stgsem[r % 2], ld), (ac, max(0, ac.n - 1))], inc=pe)
                ph.op("scalar", lambda s, r=r, c=c: s.copy(
                    wt[:, D * c + P * r: D * c + P * (r + 1)], pst[c % 2][0:P, 0:P]),
                    waits=[(pe, pe.n)], inc=ac)
            tp_tgt[r] = pe.n

    def phase_qkvA(self):
        from contextlib import ExitStack
        nc = self.nc
        sp, gp, pe, ac, dv = (self.sem[k] for k in ("sp", "gp", "pe", "ac", "dv"))
        NT, T_ = self.NT, self.T
        for wname, kind, bias, outh in [
            ("aWq", "feat", self.b_bqT, self.qt_hbm),
            ("aWk", "feat", self.b_bkT, self.kt_hbm),
            ("aWv", "tok", None, self.v_hbm),
        ]:
            with ExitStack() as sc:
                wt = self.salloc(sc, "wt_" + wname, [P, DC * D], BF16)
                stage = [self.salloc(sc, f"stg{i}_" + wname, [P, D], BF16)
                         for i in range(2)]
                stgsem = [self.S(f"stg{i}_{wname}") for i in range(2)]
                ob = [self.salloc(sc, f"ob{i}_" + wname,
                                  [P, T_ if kind == "feat" else 2 * 65 * NT], BF16)
                      for i in range(2)]
                obsem = [self.S(f"ob{i}_{wname}") for i in range(2)]
                ph = Phase("qkvA_" + wname)
                with nc.psum_tensor("pst0" + wname, [P, P], BF16) as p0, \
                     nc.psum_tensor("pst1" + wname, [P, P], BF16) as p1, \
                     nc.psum_tensor("psq0" + wname, [P, 512], F32) as q0, \
                     nc.psum_tensor("psq1" + wname, [P, 512], F32) as q1:
                    self._emit_wT(ph, sc, wname, wt, [p0, p1], stage, stgsem)
                    wT_done = self.sem["ac"].n
                    psq = [q0, q1]
                    ev_tgt = {}
                    st_tgt = {}
                    for hp in range(NHP):
                        if kind == "feat":
                            for nb in range(T_ // 512):
                                gi = hp * (T_ // 512) + nb
                                for ic in range(DC):
                                    wl = [(ac, wT_done), (sp, self.ln0T_ready)] \
                                        if gi == 0 and ic == 0 else []
                                    if ic == 0 and gi >= 2:
                                        wl = wl + [(ac, ev_tgt[gi - 2])]
                                    ph.op("tensor", lambda t, hp=hp, nb=nb, ic=ic,
                                          g=gi: t.matmul(
                                        psq[g % 2][:, 0:512],
                                        wt[:, D * ic + P * hp: D * ic + P * (hp + 1)],
                                        self.ln0T[:, T_ * ic + 512 * nb:
                                                  T_ * ic + 512 * (nb + 1)],
                                        start=(ic == 0), stop=(ic == DC - 1)),
                                        waits=wl, inc=(pe if ic == DC - 1 else None))
                                mmt = pe.n
                                wl = [(pe, mmt)]
                                if hp >= 2 and nb == 0:
                                    wl.append((obsem[hp % 2], st_tgt[hp - 2]))
                                ev_tgt[gi] = ph.op("scalar", lambda s, hp=hp, nb=nb,
                                                   g=gi: s.activation(
                                    ob[hp % 2][:, 512 * nb:512 * (nb + 1)],
                                    psq[g % 2][:, 0:512], AF.Identity,
                                    bias=bias[:, hp:hp + 1]),
                                    waits=wl, inc=ac)
                            st_tgt[hp] = dma(ph, "sync", obsem[hp % 2],
                                             outh.ap()[hp], ob[hp % 2][:],
                                             waits=[(ac, ev_tgt[hp * (T_ // 512)
                                                                + T_ // 512 - 1])])
                        else:
                            for tb in range(NT):
                                gi = hp * NT + tb
                                for ic in range(DC):
                                    wl = [(ac, wT_done), (sp, self.ln0T_ready)] \
                                        if gi == 0 and ic == 0 else []
                                    if ic == 0 and gi >= 2:
                                        wl = wl + [(ac, ev_tgt[gi - 2])]
                                    ph.op("tensor", lambda t, hp=hp, tb=tb, ic=ic,
                                          g=gi: t.matmul(
                                        psq[g % 2][:, 0:P],
                                        self.ln0T[:, T_ * ic + P * tb:
                                                  T_ * ic + P * (tb + 1)],
                                        wt[:, D * ic + P * hp: D * ic + P * (hp + 1)],
                                        start=(ic == 0), stop=False),
                                        waits=wl, inc=None)
                                ph.op("tensor", lambda t, hp=hp, g=gi: t.matmul(
                                    psq[g % 2][:, 0:P], self.sb_ones[:],
                                    self.r_bv[0:1, P * hp:P * (hp + 1)],
                                    start=False, stop=True), inc=pe)
                                mmt = pe.n
                                wl = [(pe, mmt)]
                                if hp >= 2 and tb == 0:
                                    wl.append((obsem[hp % 2], st_tgt[hp - 2]))
                                for h2 in range(2):
                                    col = 65 * NT * h2 + 65 * tb
                                    ev_tgt[gi] = ph.op(
                                        "scalar", lambda s, col=col, h2=h2, g=gi:
                                        s.copy(ob[hp % 2][:, col:col + 64]
                                               if False else
                                               ob[(g // NT) % 2][:, col:col + 64],
                                               psq[g % 2][:, 64 * h2:64 * (h2 + 1)]),
                                        waits=wl if h2 == 0 else [], inc=ac)
                            onescol = ph.op("vector", lambda v, hp=hp: v.memset(
                                ob[hp % 2][:, 0:2 * 65 * NT].rearrange(
                                    "p (t c) -> p t c", c=65)[:, :, 64:65], 1.0),
                                waits=[(ac, ev_tgt[hp * NT + NT - 1])], inc=dv)
                            st_tgt[hp] = dma(ph, "sync", obsem[hp % 2],
                                             outh.ap()[hp], ob[hp % 2][:],
                                             waits=[(dv, onescol)])
                    for j in range(2):
                        ph.wait("sync", obsem[j], obsem[j].n)
                    self.emit_preconvert(ph, 10)
                    ph.realize(nc)

    def phase_attn(self, sc, kind):
        nc = self.nc
        sp, gp, pe, ac, dv = (self.sem[k] for k in ("sp", "gp", "pe", "ac", "dv"))
        NT, T_, NQ = self.NT, self.T, self.NQ
        if kind == "A":
            qth, kth, vh = self.qt_hbm, self.kt_hbm, self.v_hbm
            nm, cls, kbl, maskh = self.nmA, self.clsA, self.kbA, self.maskA
        else:
            qth, kth, vh = self.qt2_hbm, self.kt2_hbm, self.v2_hbm
            nm, cls, kbl, maskh = self.nmB, self.clsB, self.kbB, self.maskB
        mask_sb = self.salloc(sc, f"masksb{kind}", [P, max(nm, 1) * QSB], F32)
        qb = [self.salloc(sc, f"at_q{i}{kind}", [P, T_], BF16) for i in range(2)]
        kb = [self.salloc(sc, f"at_k{i}{kind}", [P, T_], BF16) for i in range(2)]
        vb = [self.salloc(sc, f"at_v{i}{kind}", [P, 2 * 65 * NT], BF16) for i in range(2)]
        qkvsem = [self.S(f"at{kind}{i}") for i in range(2)]
        probs = [self.salloc(sc, f"at_p{i}{kind}", [P, 2 * QSB], BF16)
                 for i in range(3)]
        mtmp = [self.salloc(sc, f"at_m{i}{kind}", [P, 2 * QSB], F32)
                for i in range(2)]
        recip2 = [self.salloc(sc, f"at_recip{i}" + kind, [1, QSB], F32)
                  for i in range(2)]
        bcast = self.salloc(sc, "at_bc" + kind, [64, QSB], F32)
        ph = Phase("attn" + kind)
        for mi in range(nm):
            dma(ph, "sync", sp, mask_sb[:, mi * QSB:(mi + 1) * QSB],
                maskh.ap()[mi * P:(mi + 1) * P, :])
        mask_rdy = sp.n
        ph.wait("sync", sp, mask_rdy)
        with nc.psum_tensor("ps_s0" + kind, [P, 1024], F32) as s0, \
             nc.psum_tensor("ps_s1" + kind, [P, 1024], F32) as s1, \
             nc.psum_tensor("ps_c0" + kind, [P, 512], F32) as c0, \
             nc.psum_tensor("ps_c1" + kind, [P, 512], F32) as c1, \
             nc.psum_tensor("ps_b" + kind, [P, 512], F32) as pb:
            ps_s, ps_c = [s0, s1], [c0, c1]
            ld_tgt, hp_last_use = {}, {}

            def plan_load(hp):
                w = [(pe, hp_last_use[hp - 2])] if hp >= 2 else []
                dma(ph, "sync", qkvsem[hp % 2], qb[hp % 2][:], qth.ap()[hp], waits=w)
                dma(ph, "sync", qkvsem[hp % 2], kb[hp % 2][:], kth.ap()[hp])
                ld_tgt[hp] = dma(ph, "sync", qkvsem[hp % 2], vb[hp % 2][:],
                                 vh.ap()[hp])
            plan_load(0)
            plan_load(1)
            ji, cseq = 0, 0
            bank_free, mtmp_free, ctx_free = {}, {}, {}
            LAG = 2
            pending_norm = []
            norm_state = {"bt": {}, "g": 0}

            def plan_norm(cps, hp, hrow, qsb, cmt):
                def go():
                    g = norm_state["g"]; norm_state["g"] = g + 1
                    rb = recip2[g % 2]
                    wl1 = [(pe, cmt)]
                    if g - 2 in norm_state["bt"]:
                        wl1.append((pe, norm_state["bt"][g - 2]))
                    l1 = ph.op("scalar", lambda s, cps=cps, rb=rb: s.activation(
                        rb[:], cps[64:65, 0:512], AF.Ln), waits=wl1, inc=ac)
                    l2 = ph.op("scalar", lambda s, rb=rb: s.activation(
                        rb[:], rb[:], AF.Exp, scale=self.cst[0:1, 1:2]),
                        waits=[(ac, l1)], inc=ac)
                    bt = ph.op("tensor", lambda t, rb=rb: t.matmul(
                        pb[0:64, 0:512], self.sb_ones32[0:1, 0:64], rb[:],
                        start=True, stop=True),
                        waits=[(ac, l2), (dv, norm_state.get("ct_prev", 0))],
                        inc=pe)
                    norm_state["bt"][g] = bt
                    ct = ph.op("vector", lambda v: v.tensor_copy(
                        bcast[:], pb[0:64, 0:512]),
                        waits=[(pe, bt), (dv, dv.n)], inc=dv)
                    norm_state["ct_prev"] = ct
                    ev = ph.op("vector", lambda v, cps=cps, hp=hp, hrow=hrow,
                               qsb=qsb: v.tensor_mul(
                        self.ctxT[hrow:hrow + 64,
                                  T_ * hp + QSB * qsb:T_ * hp + QSB * (qsb + 1)],
                        cps[0:64, 0:512], bcast[:]),
                        waits=[(dv, ct)], inc=dv)
                    return ev
                return go

            for hp in range(NHP):
                for qsb in range(NQ):
                    lst = kbl[qsb]
                    nk = len(lst)
                    cA, cB = ps_c[0], ps_c[1]
                    pws = {}

                    def plan_S(j):
                        nonlocal ji
                        kbi = lst[j]
                        spair = ps_s[ji % 2]
                        wl = [(qkvsem[hp % 2], ld_tgt[hp])]
                        if ji % 2 in bank_free:
                            wl.append(bank_free[ji % 2])
                        for h2 in range(2):
                            hrow = 64 * h2
                            ph.op("tensor", lambda t, hrow=hrow, kbi=kbi,
                                  spair=spair, h2=h2, hp=hp, qsb=qsb: t.matmul(
                                spair[:, 512 * h2:512 * (h2 + 1)],
                                kb[hp % 2][hrow:hrow + 64, P * kbi:P * (kbi + 1)],
                                qb[hp % 2][hrow:hrow + 64,
                                           QSB * qsb:QSB * (qsb + 1)],
                                start=True, stop=True),
                                waits=(wl if h2 == 0 else []),
                                inc=(pe if h2 == 1 else None))
                        smt = pe.n
                        c = cls[(qsb, kbi)]
                        if c == 'clean':
                            et = ph.op("scalar", lambda s, spair=spair, ji=ji:
                                       s.activation(probs[ji % 3][:],
                                                    spair[:, 0:1024], AF.Exp,
                                                    scale=SCALE),
                                       waits=[(pe, smt)], inc=ac)
                            bank_free[ji % 2] = (ac, et)
                            pws[j] = (ac, et)
                        else:
                            mi = c[1]
                            wv = [(pe, smt), (dv, dv.n)]
                            if ji % 2 in mtmp_free:
                                wv.append(mtmp_free[ji % 2])
                            for h2 in range(2):
                                ph.op("vector", lambda v, spair=spair, mi=mi,
                                      ji=ji, h2=h2: v.scalar_tensor_tensor(
                                    mtmp[ji % 2][:, 512 * h2:512 * (h2 + 1)],
                                    spair[:, 512 * h2:512 * (h2 + 1)], SCALE,
                                    mask_sb[:, mi * QSB:(mi + 1) * QSB],
                                    ALU.mult, ALU.add),
                                    waits=(wv if h2 == 0 else []), inc=dv)
                            vt = dv.n
                            bank_free[ji % 2] = (dv, vt)
                            et = ph.op("scalar", lambda s, ji=ji: s.activation(
                                probs[ji % 3][:], mtmp[ji % 2][:, 0:1024],
                                AF.Exp), waits=[(dv, vt)], inc=ac)
                            mtmp_free[ji % 2] = (ac, et)
                            pws[j] = (ac, et)
                        pws[j, "pi"] = ji
                        ji += 1

                    def plan_C(j):
                        kbi = lst[j]
                        pi = pws[j, "pi"]
                        wl2 = [pws[j]]
                        if j == 0 and 0 in ctx_free:
                            wl2.append(ctx_free[0])
                            wl2.append(ctx_free[1])
                        for h2 in range(2):
                            vcol = 65 * NT * h2 + 65 * kbi
                            cps = cA if h2 == 0 else cB
                            ph.op("tensor", lambda t, vcol=vcol, cps=cps,
                                  pi=pi, j=j, h2=h2, hp=hp,
                                  last=(j == nk - 1): t.matmul(
                                cps[0:65, 0:512],
                                vb[hp % 2][:, vcol:vcol + 65],
                                probs[pi % 3][:, 512 * h2:512 * (h2 + 1)],
                                start=(j == 0), stop=last),
                                waits=(wl2 if h2 == 0 else []),
                                inc=(pe if (h2 == 1 and j == nk - 1) else None))

                    for j in range(nk):
                        plan_S(j)
                        if j == LAG and pending_norm:
                            ctx_free[0] = (dv, pending_norm.pop(0)())
                            ctx_free[1] = (dv, pending_norm.pop(0)())
                        if j >= LAG:
                            plan_C(j - LAG)
                    if nk <= LAG and pending_norm:
                        ctx_free[0] = (dv, pending_norm.pop(0)())
                        ctx_free[1] = (dv, pending_norm.pop(0)())
                    for j in range(max(0, nk - LAG), nk):
                        plan_C(j)
                    cmt = pe.n
                    pending_norm.append(plan_norm(cA, hp, 0, qsb, cmt))
                    pending_norm.append(plan_norm(cB, hp, 64, qsb, cmt))
                    cseq += 1
                hp_last_use[hp] = pe.n
                if hp + 2 < NHP:
                    plan_load(hp + 2)
            while pending_norm:
                pending_norm.pop(0)()
            for j2 in range(2):
                ph.wait("sync", qkvsem[j2], qkvsem[j2].n)
            self.emit_preconvert(ph, 20)
            ph.realize(nc)
        self.attn_done = dv.n


    def phase_woA(self, sc):
        nc = self.nc
        sp, gp, pe, ac, dv = (self.sem[k] for k in ("sp", "gp", "pe", "ac", "dv"))
        NT, T_ = self.NT, self.T
        wt = self.salloc(sc, "wt_aWo", [P, DC * D], BF16)
        stage = [self.salloc(sc, f"stgo{i}", [P, D], BF16) for i in range(2)]
        stgsem = [self.S(f"stgo{i}") for i in range(2)]
        xb = [self.salloc(sc, f"wx{i}", [P, D], F32) for i in range(2)]
        xsem = [self.S(f"wx{i}") for i in range(2)]
        ao = [self.salloc(sc, f"ao{i}", [P, D], F32) for i in range(2)]
        lnat = self.salloc(sc, "lnat", [P, D], F32)
        x1b = [self.salloc(sc, f"x1b{i}", [P, D], F32) for i in range(2)]
        ln1b = [self.salloc(sc, f"ln1b{i}", [P, D], BF16) for i in range(2)]
        sq = self.salloc(sc, "wsq", [P, D], F32)
        stat = [self.salloc(sc, f"wst{i}", [P, 8], F32) for i in range(4)]
        x1sem = [self.S(f"x1st{i}") for i in range(2)]
        ln1sem = [self.S(f"ln1st{i}") for i in range(2)]
        ph = Phase("woA")
        with nc.psum_tensor("pwt0", [P, P], BF16) as p0, \
             nc.psum_tensor("pwt1", [P, P], BF16) as p1, \
             nc.psum_tensor("po0", [P, 512], F32) as o0, \
             nc.psum_tensor("po1", [P, 512], F32) as o1, \
             nc.psum_tensor("po2", [P, 512], F32) as o2, \
             nc.psum_tensor("po3", [P, 512], F32) as o3:
            self._emit_wT(ph, sc, "aWo", wt, [p0, p1], stage, stgsem)
            wT_done = ac.n
            pos = [[o0, o1], [o2, o3]]
            ev_tgt, x1_tgt, ln1_tgt = {}, {}, {}
            for tb in range(NT):
                wx = [(dv, ev_tgt[tb - 2])] if tb >= 2 else []
                xt = dma(ph, "sync", xsem[tb % 2], xb[tb % 2][:],
                         self.x.ap()[tb * P:(tb + 1) * P, :], waits=wx)
                for dh in range(2):
                    for hp in range(NHP):
                        wl = []
                        if tb == 0 and dh == 0 and hp == 0:
                            wl.append((ac, wT_done))
                        if hp == 0 and tb >= 2:
                            wl.append((ac, ev_tgt[tb - 2, dh] if isinstance(
                                ev_tgt.get((tb - 2, dh)), int) else
                                ev_tgt[(tb - 2, dh)]))
                        ph.op("tensor", lambda t, tb=tb, dh=dh, hp=hp: t.matmul(
                            pos[tb % 2][dh][:, 0:512],
                            self.ctxT[:, T_ * hp + P * tb:T_ * hp + P * (tb + 1)],
                            wt[:, D * hp + 512 * dh:D * hp + 512 * (dh + 1)],
                            start=(hp == 0), stop=False), waits=wl, inc=None)
                    ph.op("tensor", lambda t, tb=tb, dh=dh: t.matmul(
                        pos[tb % 2][dh][:, 0:512], self.sb_ones[:],
                        self.r_bo[0:1, 512 * dh:512 * (dh + 1)],
                        start=False, stop=True), inc=pe)
                    mmt = pe.n
                    ev_tgt[(tb, dh)] = ph.op(
                        "scalar", lambda s, tb=tb, dh=dh: s.copy(
                            ao[tb % 2][:, 512 * dh:512 * (dh + 1)],
                            pos[tb % 2][dh][:, 0:512]),
                        waits=[(pe, mmt)], inc=ac)
                self._ln_tile(ph, src=ao[tb % 2][:],
                              pre_wait=[(ac, ev_tgt[(tb, 1)])],
                              sq=sq, stat=stat[tb % 2], out=lnat[:],
                              out_guard=(dv, dv.n))
                x1d = ph.op("vector", lambda v, tb=tb: v.tensor_add(
                    x1b[tb % 2][:], xb[tb % 2][:], lnat[:]),
                    waits=[(dv, dv.n), (xsem[tb % 2], xt),
                           (x1sem[tb % 2], x1_tgt.get(tb - 2, 0))], inc=dv)
                x1_tgt[tb] = dma(ph, "sync", x1sem[tb % 2],
                                 self.x1f32.ap()[tb * P:(tb + 1) * P, :],
                                 x1b[tb % 2][:], waits=[(dv, x1d)])
                self._ln_tile(ph, src=x1b[tb % 2][:], pre_wait=[(dv, x1d)],
                              sq=sq, stat=stat[2 + tb % 2], out=ln1b[tb % 2][:],
                              out_guard=(ln1sem[tb % 2], ln1_tgt.get(tb - 2, 0)))
                ev_tgt[tb] = dv.n
                ln1_tgt[tb] = dma(ph, "sync", ln1sem[tb % 2],
                                  self.ln1bf.ap()[tb * P:(tb + 1) * P, :],
                                  ln1b[tb % 2][:], waits=[(dv, dv.n)])
            for j in range(2):
                ph.wait("sync", x1sem[j], x1sem[j].n)
                ph.wait("sync", ln1sem[j], ln1sem[j].n)
            self.emit_preconvert(ph, 15)
            ph.realize(nc)

    def phase_dkv(self, sc):
        nc = self.nc
        sp, gp, pe, ac, dv = (self.sem[k] for k in ("sp", "gp", "pe", "ac", "dv"))
        NT, T_ = self.NT, self.T
        ln1T = self.salloc(sc, "ln1T", [P, DC * T_], BF16)
        wdkv = self.salloc(sc, "wdkv", [P, DC * L], BF16)
        wdq = self.salloc(sc, "wdq", [P, DC * L], BF16)
        ph = Phase("dkv")
        for c in range(DC):
            dma(ph, "sync", sp, ln1T[:, c * T_:(c + 1) * T_],
                self.ln1bf.ap()[:, c * P:(c + 1) * P], transpose=True)
        dma(ph, "gpsimd", gp, wdkv[:].rearrange("p (ic l) -> p ic l", ic=DC),
            self.ext["fWdkv"].ap().rearrange("(ic p) l -> p ic l", p=P))
        dma(ph, "gpsimd", gp, wdq[:].rearrange("p (ic l) -> p ic l", ic=DC),
            self.ext["fWdq"].ap().rearrange("(ic p) l -> p ic l", p=P))
        ph.wait("sync", sp, sp.n)
        ph.wait("gpsimd", gp, gp.n)
        ld_rdy_sp, ld_rdy_gp = sp.n, gp.n
        with nc.psum_tensor("pd0", [P, 512], F32) as d0, \
             nc.psum_tensor("pd1", [P, 512], F32) as d1:
            psd = [d0, d1]
            gi = 0
            ev_tgt = {}
            for (wsb, bias, out) in [(wdkv, self.b_bdkvT, self.ckvT),
                                     (wdq, self.b_bdqT, self.cqT)]:
                for lc in range(LC):
                    for nb in range(T_ // 512):
                        wl = []
                        if gi == 0:
                            wl = [(sp, ld_rdy_sp), (gp, ld_rdy_gp)]
                        if gi >= 2:
                            wl.append((ac, ev_tgt[gi - 2]))
                        for ic in range(DC):
                            ph.op("tensor", lambda t, wsb=wsb, lc=lc, nb=nb, ic=ic,
                                  gi=gi: t.matmul(
                                psd[gi % 2][:, 0:512],
                                wsb[:, L * ic + P * lc:L * ic + P * (lc + 1)],
                                ln1T[:, T_ * ic + 512 * nb:T_ * ic + 512 * (nb + 1)],
                                start=(ic == 0), stop=(ic == DC - 1)),
                                waits=(wl if ic == 0 else []),
                                inc=(pe if ic == DC - 1 else None))
                        ev_tgt[gi] = ph.op(
                            "scalar", lambda s, out=out, bias=bias, lc=lc, nb=nb,
                            gi=gi: s.activation(
                                out[:, T_ * lc + 512 * nb:T_ * lc + 512 * (nb + 1)],
                                psd[gi % 2][:, 0:512], AF.Identity,
                                bias=bias[:, lc:lc + 1]),
                            waits=[(pe, pe.n)], inc=ac)
                        gi += 1
            self.emit_preconvert(ph, 10)
            ph.realize(nc)

    def phase_uproj(self, sc):
        nc = self.nc
        sp, gp, pe, ac, dv = (self.sem[k] for k in ("sp", "gp", "pe", "ac", "dv"))
        NT, T_ = self.NT, self.T
        for wname, kind, bias, outh, src in [
            ("fWuk", "feat", self.b_bukT, self.kt2_hbm, "ckvT"),
            ("fWuq", "feat", self.b_buqT, self.qt2_hbm, "cqT"),
            ("fWuv", "tok", None, self.v2_hbm, "ckvT"),
        ]:
            from contextlib import ExitStack
            with ExitStack() as sc2:
                wu = self.salloc(sc2, "wu_" + wname, [P, LC * D], BF16)
                ob = [self.salloc(sc2, f"ub{i}_" + wname,
                                  [P, T_ if kind == "feat" else 2 * 65 * NT], BF16)
                      for i in range(2)]
                obsem = [self.S(f"ub{i}_{wname}") for i in range(2)]
                srcT = getattr(self, src)
                ph = Phase("uproj_" + wname)
                dma(ph, "gpsimd", gp, wu[:].rearrange("p (lc d) -> p lc d", lc=LC),
                    self.ext[wname].ap().rearrange("(lc p) d -> p lc d", p=P))
                ph.wait("gpsimd", gp, gp.n)
                w_rdy = gp.n
                with nc.psum_tensor("pu0" + wname, [P, 512], F32) as u0, \
                     nc.psum_tensor("pu1" + wname, [P, 512], F32) as u1:
                    psu = [u0, u1]
                    ev_tgt, st_tgt = {}, {}
                    for hp in range(NHP):
                        if kind == "feat":
                            for nb in range(T_ // 512):
                                gi = hp * (T_ // 512) + nb
                                wl = [(gp, w_rdy)] if gi == 0 else []
                                if gi >= 2:
                                    wl = wl + [(ac, ev_tgt[gi - 2])]
                                for lc in range(LC):
                                    ph.op("tensor", lambda t, hp=hp, nb=nb, lc=lc,
                                          gi=gi: t.matmul(
                                        psu[gi % 2][:, 0:512],
                                        wu[:, D * lc + P * hp:D * lc + P * (hp + 1)],
                                        srcT[:, T_ * lc + 512 * nb:
                                             T_ * lc + 512 * (nb + 1)],
                                        start=(lc == 0), stop=(lc == LC - 1)),
                                        waits=(wl if lc == 0 else []),
                                        inc=(pe if lc == LC - 1 else None))
                                mmt = pe.n
                                wl2 = [(pe, mmt)]
                                if hp >= 2 and nb == 0:
                                    wl2.append((obsem[hp % 2], st_tgt[hp - 2]))
                                ev_tgt[gi] = ph.op("scalar", lambda s, hp=hp, nb=nb,
                                                   gi=gi: s.activation(
                                    ob[hp % 2][:, 512 * nb:512 * (nb + 1)],
                                    psu[gi % 2][:, 0:512], AF.Identity,
                                    bias=bias[:, hp:hp + 1]),
                                    waits=wl2, inc=ac)
                            st_tgt[hp] = dma(ph, "sync", obsem[hp % 2],
                                             outh.ap()[hp], ob[hp % 2][:],
                                             waits=[(ac, ev_tgt[hp * (T_ // 512)
                                                                + T_ // 512 - 1])])
                        else:
                            for tb in range(NT):
                                gi = hp * NT + tb
                                wl = [(gp, w_rdy)] if gi == 0 else []
                                if gi >= 2:
                                    wl = wl + [(ac, ev_tgt[gi - 2])]
                                for lc in range(LC):
                                    ph.op("tensor", lambda t, hp=hp, tb=tb, lc=lc,
                                          gi=gi: t.matmul(
                                        psu[gi % 2][:, 0:P],
                                        srcT[:, T_ * lc + P * tb:T_ * lc + P * (tb + 1)],
                                        wu[:, D * lc + P * hp:D * lc + P * (hp + 1)],
                                        start=(lc == 0), stop=False),
                                        waits=(wl if lc == 0 else []), inc=None)
                                ph.op("tensor", lambda t, hp=hp, gi=gi: t.matmul(
                                    psu[gi % 2][:, 0:P], self.sb_ones[:],
                                    self.r_buv[0:1, P * hp:P * (hp + 1)],
                                    start=False, stop=True), inc=pe)
                                mmt = pe.n
                                wl2 = [(pe, mmt)]
                                if hp >= 2 and tb == 0:
                                    wl2.append((obsem[hp % 2], st_tgt[hp - 2]))
                                for h2 in range(2):
                                    col = 65 * NT * h2 + 65 * tb
                                    ev_tgt[gi] = ph.op(
                                        "scalar", lambda s, col=col, h2=h2, gi=gi,
                                        hp=hp: s.copy(
                                            ob[hp % 2][:, col:col + 64],
                                            psu[gi % 2][:, 64 * h2:64 * (h2 + 1)]),
                                        waits=(wl2 if h2 == 0 else []), inc=ac)
                            onescol = ph.op("vector", lambda v, hp=hp: v.memset(
                                ob[hp % 2][:, 0:2 * 65 * NT].rearrange(
                                    "p (t c) -> p t c", c=65)[:, :, 64:65], 1.0),
                                waits=[(ac, ev_tgt[hp * NT + NT - 1]), (dv, dv.n)],
                                inc=dv)
                            st_tgt[hp] = dma(ph, "sync", obsem[hp % 2],
                                             outh.ap()[hp], ob[hp % 2][:],
                                             waits=[(dv, onescol)])
                    for j in range(2):
                        ph.wait("sync", obsem[j], obsem[j].n)
                    self.emit_preconvert(ph, 8)
                    ph.realize(nc)

    def phase_woB(self, sc):
        nc = self.nc
        sp, gp, pe, ac, dv = (self.sem[k] for k in ("sp", "gp", "pe", "ac", "dv"))
        NT, T_ = self.NT, self.T
        wo2 = self.salloc(sc, "wo2", [P, DC * D], BF16)
        g1bc = self.salloc(sc, "g1bcl", [P, D], BF16)
        be1bc = self.salloc(sc, "be1bcl", [P, D], BF16)
        ln1b = [self.salloc(sc, f"bl{i}", [P, D], BF16) for i in range(2)]
        lnsem = [self.S(f"bln{i}") for i in range(2)]
        ai = [self.salloc(sc, f"ai{i}", [P, D], F32) for i in range(2)]
        ssum = self.salloc(sc, "ssum", [P, D], F32)
        x1fb = [self.salloc(sc, f"x1fb{i}", [P, D], F32) for i in range(2)]
        sq = self.salloc(sc, "bsq", [P, D], F32)
        stat = [self.salloc(sc, f"bst{i}", [P, 8], F32) for i in range(2)]
        x1fsem = [self.S(f"x1fst{i}") for i in range(2)]
        x1fbsem = [self.S(f"x1fbst{i}") for i in range(2)]
        ph = Phase("woB")
        dma(ph, "sync", sp, g1bc[:], self.gb_hbm.ap()[0:P, :])
        dma(ph, "sync", sp, be1bc[:], self.gb_hbm.ap()[P:2 * P, :])
        ph.wait("sync", sp, sp.n)
        dma(ph, "gpsimd", gp, wo2[:].rearrange("p (jc d) -> p jc d", jc=DC),
            self.ext["fWo"].ap().rearrange("(jc p) d -> p jc d", p=P))
        ph.wait("gpsimd", gp, gp.n)
        w_rdy = gp.n
        with nc.psum_tensor("bo0", [P, 512], F32) as o0, \
             nc.psum_tensor("bo1", [P, 512], F32) as o1, \
             nc.psum_tensor("bo2", [P, 512], F32) as o2, \
             nc.psum_tensor("bo3", [P, 512], F32) as o3:
            pos = [[o0, o1], [o2, o3]]
            ev_tgt, st1, st2 = {}, {}, {}
            for tb in range(NT):
                wlx = [(dv, ev_tgt[tb - 2])] if tb >= 2 else []
                lt = dma(ph, "sync", lnsem[tb % 2], ln1b[tb % 2][:],
                         self.ln1bf.ap()[tb * P:(tb + 1) * P, :], waits=wlx)
                for dh in range(2):
                    wl = [(gp, w_rdy)] if tb == 0 and dh == 0 else []
                    if tb >= 2:
                        wl.append((ac, ev_tgt[(tb - 2, dh)]))
                    for jc in range(DC):
                        ph.op("tensor", lambda t, tb=tb, dh=dh, jc=jc: t.matmul(
                            pos[tb % 2][dh][:, 0:512],
                            self.ctxT[:, T_ * jc + P * tb:T_ * jc + P * (tb + 1)],
                            wo2[:, D * jc + 512 * dh:D * jc + 512 * (dh + 1)],
                            start=(jc == 0), stop=False),
                            waits=(wl if jc == 0 else []), inc=None)
                    ph.op("tensor", lambda t, tb=tb, dh=dh: t.matmul(
                        pos[tb % 2][dh][:, 0:512], self.sb_ones[:],
                        self.r_fbo[0:1, 512 * dh:512 * (dh + 1)],
                        start=False, stop=True), inc=pe)
                    mmt = pe.n
                    ev_tgt[(tb, dh)] = ph.op(
                        "scalar", lambda s, tb=tb, dh=dh: s.copy(
                            ai[tb % 2][:, 512 * dh:512 * (dh + 1)],
                            pos[tb % 2][dh][:, 0:512]),
                        waits=[(pe, mmt)], inc=ac)
                sd = ph.op("vector", lambda v, tb=tb: v.tensor_add(
                    ssum[:], ln1b[tb % 2][:], ai[tb % 2][:]),
                    waits=[(ac, ev_tgt[(tb, 1)]), (lnsem[tb % 2], lt),
                           (dv, dv.n)], inc=dv)
                self._ln_tile(ph, src=ssum[:], pre_wait=[(dv, sd)],
                              sq=sq, stat=stat[tb % 2], out=ssum[:],
                              out_guard=(dv, dv.n))
                g1d = ph.op("vector", lambda v, tb=tb: v.tensor_mul(
                    ssum[:], ssum[:], g1bc[:]), waits=[(dv, dv.n)], inc=dv)
                b1d = ph.op("vector", lambda v, tb=tb: v.tensor_add(
                    x1fb[tb % 2][:], ssum[:], be1bc[:]),
                    waits=[(dv, g1d),
                           (x1fsem[tb % 2], st1.get(tb - 2, 0)),
                           (x1fbsem[tb % 2], st2.get(tb - 2, 0))], inc=dv)
                st1[tb] = dma(ph, "sync", x1fsem[tb % 2],
                              self.x1ff32.ap()[tb * P:(tb + 1) * P, :],
                              x1fb[tb % 2][:], waits=[(dv, b1d)])
                st2[tb] = dma(ph, "gpsimd", x1fbsem[tb % 2],
                              self.x1fbf.ap()[tb * P:(tb + 1) * P, :],
                              x1fb[tb % 2][:], waits=[(dv, b1d)])
                ev_tgt[tb] = dv.n
            for j in range(2):
                ph.wait("sync", x1fsem[j], x1fsem[j].n)
                ph.wait("gpsimd", x1fbsem[j], x1fbsem[j].n)
                ph.wait("sync", lnsem[j], lnsem[j].n)
            self.emit_preconvert(ph, 15)
            ph.realize(nc)

    def phase_x1fT(self, sc):
        nc = self.nc
        sp, gp, pe, ac, dv = (self.sem[k] for k in ("sp", "gp", "pe", "ac", "dv"))
        T_ = self.T
        stg = [self.salloc(sc, f"xft{i}", [P, T_], BF16) for i in range(2)]
        tsem = [self.S(f"xftt{i}") for i in range(2)]
        ssem = [self.S(f"xfts{i}") for i in range(2)]
        ph = Phase("x1fT")
        st_tgt = {}
        for c in range(DC):
            w = [(ssem[c % 2], st_tgt[c - 2])] if c >= 2 else []
            tt = dma(ph, "sync", tsem[c % 2], stg[c % 2][:],
                     self.x1fbf.ap()[:, c * P:(c + 1) * P], transpose=True,
                     waits=w)
            st_tgt[c] = dma(ph, "sync", ssem[c % 2], self.x1fT_hbm.ap()[c],
                            stg[c % 2][:], waits=[(tsem[c % 2], tt)])
        for j in range(2):
            ph.wait("sync", ssem[j], ssem[j].n)
        self.emit_preconvert(ph, 30)
        ph.realize(nc)

    def phase_moe(self, sc):
        nc = self.nc
        sp, gp, pe, ac, dv = (self.sem[k] for k in ("sp", "gp", "pe", "ac", "dv"))
        NT, T_ = self.NT, self.T
        NTC = T_ // 512
        w1sb = self.salloc(sc, "w1sb", [P, DC * FF], BF16)
        w2sb = self.salloc(sc, "w2sb", [P, FFC * D], BF16)
        h_sb = self.salloc(sc, "h_sb", [P, FFC * 512], BF16)
        moe = self.salloc(sc, "moe", [P, 4 * D], F32)
        xtc = self.salloc(sc, "xtc", [P, DC * 512], BF16)
        wg_sb = self.salloc(sc, "wg_sb", [P, DC * E], BF16)
        wn_sb = self.salloc(sc, "wn_sb", [P, DC * E], BF16)
        nz = self.salloc(sc, "nz", [P, 4 * E], F32)
        gt1 = self.salloc(sc, "gt1", [P, E], F32)
        gt2 = self.salloc(sc, "gt2", [P, E], F32)
        hx = self.salloc(sc, "hx", [P, E], F32)
        m1 = self.salloc(sc, "m1", [P, 1], F32)
        negm1 = self.salloc(sc, "negm1", [P, 1], F32)
        m1pad = self.salloc(sc, "m1pad", [P, E], F32)
        hrep = self.salloc(sc, "hrep", [P, E], F32)
        m2 = self.salloc(sc, "m2", [P, 1], F32)
        e1t = self.salloc(sc, "e1t", [P, E], F32)
        ge = self.salloc(sc, "ge", [P, E], F32)
        zt = self.salloc(sc, "zt", [P, 1], F32)
        w1sem, w2sem = self.S("w1s"), self.S("w2s")
        xtcsem, nzsem = self.S("xtc"), self.S("nz")
        self._moe_st = {}
        ph = Phase("moe")
        dma(ph, "gpsimd", gp, wg_sb[:].rearrange("p (ic e) -> p ic e", ic=DC),
            self.ext["Wg"].ap().rearrange("(ic p) e -> p ic e", p=P))
        dma(ph, "gpsimd", gp, wn_sb[:].rearrange("p (ic e) -> p ic e", ic=DC),
            self.ext["Wn"].ap().rearrange("(ic p) e -> p ic e", p=P))
        ph.wait("gpsimd", gp, gp.n)
        g_rdy = gp.n
        with nc.psum_tensor("mh0", [P, 512], F32) as h0, \
             nc.psum_tensor("mh1", [P, 512], F32) as h1, \
             nc.psum_tensor("mo0", [P, 512], F32) as po0, \
             nc.psum_tensor("mo1", [P, 512], F32) as po1, \
             nc.psum_tensor("mgG", [P, 512], F32) as pgG, \
             nc.psum_tensor("mgN", [P, 512], F32) as pgN, \
             nc.psum_tensor("mgT", [P, P], F32) as pgT:
            psH, psO = [h0, h1], [po0, po1]
            w1_use, w2_use = 0, 0
            xtc_use = 0
            ystore = {}
            for tc in range(NTC):
                xl = 0
                for c in range(DC):
                    xl = dma(ph, "sync", xtcsem, xtc[:, c * 512:(c + 1) * 512],
                             self.x1fT_hbm.ap()[c, :, tc * 512:(tc + 1) * 512],
                             waits=([(pe, xtc_use)] if c == 0 else []))
                nl = dma(ph, "sync", nzsem,
                         nz[:].rearrange("p (b e) -> p b e", e=E),
                         self.noise.ap()[tc * 512:(tc + 1) * 512, :]
                         .rearrange("(b p) e -> p b e", p=P),
                         waits=[(dv, dv.n)])
                w1l = dma(ph, "sync", w1sem,
                          w1sb[:].rearrange("p (ic f) -> p ic f", ic=DC),
                          self.w1bf.ap()[0].rearrange("(ic p) f -> p ic f", p=P),
                          waits=[(pe, w1_use), (gp, self._pc_done_tgt())])
                w2l = dma(ph, "sync", w2sem,
                          w2sb[:].rearrange("p (f d) -> p f d", f=FFC),
                          self.w2bf.ap()[0].rearrange("(f p) d -> p f d", p=P),
                          waits=[(pe, w2_use)])
                gmt = {}
                for tbi in range(4):
                    wlg = [(xtcsem, xl)]
                    if tc == 0 and tbi == 0:
                        wlg.append((gp, g_rdy))
                    if tbi >= 1:
                        wlg.append((dv, gmt[tbi - 1]))
                    for ic in range(DC):
                        ph.op("tensor", lambda t, tbi=tbi, ic=ic: t.matmul(
                            pgG[:, 0:E],
                            xtc[:, 512 * ic + P * tbi:512 * ic + P * (tbi + 1)],
                            wg_sb[:, E * ic:E * (ic + 1)],
                            start=(ic == 0), stop=False),
                            waits=(wlg if ic == 0 else []), inc=None)
                    ph.op("tensor", lambda t: t.matmul(
                        pgG[:, 0:E], self.sb_ones[:], self.r_bg[:],
                        start=False, stop=True), inc=pe)
                    for ic in range(DC):
                        ph.op("tensor", lambda t, tbi=tbi, ic=ic: t.matmul(
                            pgN[:, 0:E],
                            xtc[:, 512 * ic + P * tbi:512 * ic + P * (tbi + 1)],
                            wn_sb[:, E * ic:E * (ic + 1)],
                            start=(ic == 0), stop=False), inc=None)
                    ph.op("tensor", lambda t: t.matmul(
                        pgN[:, 0:E], self.sb_ones[:], self.r_bn[:],
                        start=False, stop=True), inc=pe)
                    gmm = pe.n
                    a1 = ph.op("scalar", lambda s: s.activation(
                        gt2[:], pgN[:, 0:E], AF.Abs), waits=[(pe, gmm)], inc=ac)
                    a2 = ph.op("scalar", lambda s: s.activation(
                        gt2[:], gt2[:], AF.Exp, scale=self.cst[:, 1:2]),
                        waits=[(ac, a1)], inc=ac)
                    a3 = ph.op("scalar", lambda s: s.activation(
                        gt2[:], gt2[:], AF.Ln, bias=1.0), waits=[(ac, a2)], inc=ac)
                    r1 = ph.op("vector", lambda v: v.tensor_scalar_max(
                        gt1[:], pgN[:, 0:E], 0.0),
                        waits=[(pe, gmm), (dv, dv.n)], inc=dv)
                    s1_ = ph.op("vector", lambda v: v.tensor_add(
                        gt1[:], gt1[:], gt2[:]), waits=[(dv, r1), (ac, a3)], inc=dv)
                    n1 = ph.op("vector", lambda v, tbi=tbi: v.tensor_mul(
                        gt1[:], gt1[:], nz[:, E * tbi:E * (tbi + 1)]),
                        waits=[(dv, s1_), (nzsem, nl)], inc=dv)
                    hx1 = ph.op("vector", lambda v: v.tensor_add(
                        hx[:], pgG[:, 0:E], gt1[:]), waits=[(dv, n1)], inc=dv)
                    mm1 = ph.op("vector", lambda v: v.reduce_max(
                        m1[:], hx[:], axis=AX.X), waits=[(dv, hx1)], inc=dv)
                    nm1 = ph.op("vector", lambda v: v.reduce_max(
                        negm1[:], hx[:], axis=AX.X, negate=True),
                        waits=[(dv, hx1)], inc=dv)
                    mp = ph.op("vector", lambda v: v.memset(m1pad[:], -1e38),
                               waits=[(dv, dv.n)], inc=dv)
                    mp2 = ph.op("vector", lambda v: v.tensor_copy(
                        m1pad[:, 0:1], m1[:]), waits=[(dv, mm1), (dv, mp)], inc=dv)
                    mr = ph.op("vector", lambda v: v.match_replace(
                        hrep[:], m1pad[:], hx[:], -1e30),
                        waits=[(dv, mp2)], inc=dv)
                    mm2 = ph.op("vector", lambda v: v.reduce_max(
                        m2[:], hrep[:], axis=AX.X), waits=[(dv, mr)], inc=dv)
                    ex1 = ph.op("scalar", lambda s: s.activation(
                        e1t[:], hx[:], AF.Exp, bias=negm1[:]),
                        waits=[(dv, nm1), (dv, hx1)], inc=ac)
                    g1_ = ph.op("vector", lambda v: v.tensor_scalar(
                        ge[:], hx[:], m2[:, 0:1], None, ALU.is_ge),
                        waits=[(dv, mm2)], inc=dv)
                    e1d = ph.op("vector", lambda v, tbi=tbi: v.tensor_mul(
                        self.e1m_sb[:, E * tbi:E * (tbi + 1)], e1t[:], ge[:]),
                        waits=[(dv, g1_), (ac, ex1)], inc=dv)
                    zd = ph.op("vector", lambda v, tbi=tbi: v.reduce_sum(
                        zt[:], self.e1m_sb[:, E * tbi:E * (tbi + 1)], axis=AX.X),
                        waits=[(dv, e1d)], inc=dv)
                    rz = ph.op("vector", lambda v, tb=tc * 4 + tbi: v.reciprocal(
                        self.recipZ[:, tb:tb + 1], zt[:]),
                        waits=[(dv, zd)], inc=dv)
                    et1 = ph.op("tensor", lambda t, tbi=tbi: t.transpose(
                        pgT[0:E, 0:P], self.e1m_sb[:, E * tbi:E * (tbi + 1)],
                        self.sb_ident32[:]), waits=[(dv, e1d), (ac, ac.n)], inc=pe)
                    ph.op("scalar", lambda s, tbi=tbi: s.copy(
                        self.e1mT_sb[0:E, P * tbi:P * (tbi + 1)], pgT[0:E, 0:P]),
                        waits=[(pe, et1)], inc=ac)
                    gmt[tbi] = dv.n
                gate_done_ac = ac.n
                b2t = {}
                for tbi in range(4):
                    for dh in range(2):
                        wbm = [(ac, gate_done_ac), (dv, dv.n)]
                        if tbi > 0:
                            wbm.append((ac, b2t[(tbi - 1, dh)]))
                        bt = ph.op("tensor", lambda t, tbi=tbi, dh=dh: t.matmul(
                            psO[dh][:, 0:512],
                            self.e1mT_sb[0:E, P * tbi:P * (tbi + 1)],
                            self.b2_sb[0:E, 512 * dh:512 * (dh + 1)],
                            start=True, stop=True),
                            waits=wbm, inc=pe)
                        wb2 = [(pe, bt)]
                        if tc > 0:
                            wb2.append((self.S(f"moest{tbi}"), self._moe_st[tbi]))
                        b2t[(tbi, dh)] = ph.op(
                            "scalar", lambda s, tbi=tbi, dh=dh: s.copy(
                                moe[:, D * tbi + 512 * dh:D * tbi + 512 * (dh + 1)],
                                psO[dh][:, 0:512]), waits=wb2, inc=ac)
                moe_init_ac = ac.n
                hev = {}
                for e in range(E):
                    if e > 0:
                        w1l = dma(ph, "sync", w1sem,
                                  w1sb[:].rearrange("p (ic f) -> p ic f", ic=DC),
                                  self.w1bf.ap()[e].rearrange(
                                      "(ic p) f -> p ic f", p=P),
                                  waits=[(pe, w1_use)])
                        w2l = dma(ph, "sync", w2sem,
                                  w2sb[:].rearrange("p (f d) -> p f d", f=FFC),
                                  self.w2bf.ap()[e].rearrange(
                                      "(f p) d -> p f d", p=P),
                                  waits=[(pe, w2_use)])
                    w2_prev_done = pe.n if e > 0 else 0
                    for f in range(FFC):
                        wl = [(w1sem, w1l)] if f == 0 else []
                        if f == 0:
                            wl.append((xtcsem, xl))
                        if f >= 2:
                            wl.append((ac, hev[f - 2]))
                        for ic in range(DC):
                            ph.op("tensor", lambda t, e=e, f=f, ic=ic: t.matmul(
                                psH[f % 2][:, 0:512],
                                w1sb[:, FF * ic + P * f:FF * ic + P * (f + 1)],
                                xtc[:, 512 * ic:512 * (ic + 1)],
                                start=(ic == 0), stop=(ic == DC - 1)),
                                waits=(wl if ic == 0 else []),
                                inc=(pe if ic == DC - 1 else None))
                        mmt = pe.n
                        wl2 = [(pe, mmt)]
                        if f == 0 and e > 0:
                            wl2.append((pe, w2_prev_done))
                        hev[f] = ph.op("scalar", lambda s, e=e, f=f: s.activation(
                            h_sb[:, 512 * f:512 * (f + 1)], psH[f % 2][:, 0:512],
                            AF.Relu, bias=self.b_b1T[:, FFC * e + f:FFC * e + f + 1]),
                            waits=wl2, inc=ac)
                    w1_use = pe.n
                    h_done = hev[FFC - 1]
                    sttv = {}
                    for tbi in range(4):
                        for dh in range(2):
                            gsel = (tbi * 2 + dh) % 2
                            wl = [(ac, h_done), (w2sem, w2l)] \
                                if tbi == 0 and dh == 0 else []
                            if e == 0 and tbi * 2 + dh < 2:
                                wl.append((ac, moe_init_ac))
                            if (tbi, dh) in (( 1, 0),) and False:
                                pass
                            prev = (tbi * 2 + dh) - 2
                            if prev >= 0:
                                wl.append((dv, sttv.get(prev, dv.n)))
                            elif e > 0 or tc > 0:
                                wl.append((dv, dv.n))
                            for f in range(FFC):
                                ph.op("tensor", lambda t, tbi=tbi, dh=dh, f=f,
                                      gsel=gsel: t.matmul(
                                    psO[gsel][:, 0:512],
                                    h_sb[:, 512 * f + P * tbi:512 * f + P * (tbi + 1)],
                                    w2sb[:, D * f + 512 * dh:D * f + 512 * (dh + 1)],
                                    start=(f == 0), stop=(f == FFC - 1)),
                                    waits=(wl if f == 0 else []),
                                    inc=(pe if f == FFC - 1 else None))
                            mmt = pe.n
                            wme = [(pe, mmt), (dv, dv.n)]
                            if e == 0:
                                wme.append((ac, b2t[(tbi, dh)]))
                            sttv[tbi * 2 + dh] = ph.op(
                                "vector", lambda v, tbi=tbi, dh=dh, e=e:
                                v.scalar_tensor_tensor(
                                    moe[:, D * tbi + 512 * dh:
                                        D * tbi + 512 * (dh + 1)],
                                    psO[gsel][:, 0:512] if False else
                                    psO[(tbi * 2 + dh) % 2][:, 0:512],
                                    self.e1m_sb[:, E * tbi + e:E * tbi + e + 1],
                                    moe[:, D * tbi + 512 * dh:
                                        D * tbi + 512 * (dh + 1)],
                                    ALU.mult, ALU.add),
                                waits=wme, inc=dv)
                    w2_use = pe.n
                xtc_use = pe.n
                for tbi in range(4):
                    tb = tc * 4 + tbi
                    self._moe_st[tbi] = dma(
                        ph, "sync", self.S(f"moest{tbi}"),
                        self.moe_hbm.ap()[tb * P:(tb + 1) * P, :],
                        moe[:, D * tbi:D * (tbi + 1)],
                        waits=[(dv, sttv[tbi * 2 + 1])])
            for tbi in range(4):
                sm = self.S(f"moest{tbi}")
                ph.wait("sync", sm, sm.n)
            ph.realize(nc)

    def phase_gate(self, sc):
        nc = self.nc
        sp, gp, pe, ac, dv = (self.sem[k] for k in ("sp", "gp", "pe", "ac", "dv"))
        NT, T_, NCHK = self.NT, self.T, self.NCHK
        xT = self.salloc(sc, "gxT", [P, DC * T_], BF16)
        wg_sb = self.salloc(sc, "gwg", [P, DC * E], BF16)
        wn_sb = self.salloc(sc, "gwn", [P, DC * E], BF16)
        selb = self.salloc(sc, "selb", [P, NT * E], BF16)
        i12 = self.salloc(sc, "i12", [P, 2 * NT], F32)
        nzb = [self.salloc(sc, f"gnz{i}", [P, E], F32) for i in range(2)]
        nzsem = [self.S(f"gnz{i}") for i in range(2)]
        gt1 = self.salloc(sc, "ggt1", [P, E], F32)
        gt2 = self.salloc(sc, "ggt2", [P, E], F32)
        hx = self.salloc(sc, "ghx", [P, E], F32)
        m1 = self.salloc(sc, "gm1", [P, 1], F32)
        negm1 = self.salloc(sc, "gnegm1", [P, 1], F32)
        m1pad = self.salloc(sc, "gm1pad", [P, E], F32)
        hrep = self.salloc(sc, "ghrep", [P, E], F32)
        m2 = self.salloc(sc, "gm2", [P, 1], F32)
        e1t = self.salloc(sc, "ge1t", [P, E], F32)
        m2pad = self.salloc(sc, "gm2pad", [P, E], F32)
        idx8 = self.salloc(sc, "gidx8", [P, E], mybir.dt.uint32)
        idx1 = self.salloc(sc, "gidx1", [P, 1], F32)
        ge = self.salloc(sc, "gge", [P, E], F32)
        e1m = self.salloc(sc, "ge1m", [P, E], F32)
        zt = self.salloc(sc, "gzt", [P, 1], F32)
        aux = [self.salloc(sc, f"gaux{i}", [P, 2], F32) for i in range(2)]
        auxsem = [self.S(f"gaux{i}") for i in range(2)]
        lt_sb = self.salloc(sc, "glt", [P, P], BF16)
        onescol = self.salloc(sc, "gonescol", [P, 1], BF16)
        iota_sb = self.salloc(sc, "giota", [P, 1], F32)
        iota8_sb = self.salloc(sc, "giota8", [P, E], F32)
        iotar = self.salloc(sc, "giotar", [P, NCHK], F32)
        cval_sb = self.salloc(sc, "gcval", [1, NCHK], F32)
        big_sb = self.salloc(sc, "gbig", [P, NCHK * 4], F32)
        big_sb2 = self.salloc(sc, "gbig2", [P, NCHK * 4], F32)
        totals = self.salloc(sc, "gtot", [1, NT * E], F32)
        excl = self.salloc(sc, "gexcl", [1, NT * E], F32)
        run = self.salloc(sc, "grun", [1, E], F32)
        nrow = self.salloc(sc, "gnrow", [1, E], F32)
        pad = self.salloc(sc, "gpad", [1, E], F32)
        base = self.salloc(sc, "gbase", [1, E], F32)
        endr = self.salloc(sc, "gend", [1, E], F32)
        brep = self.salloc(sc, "gbrep", [1, NT * E], F32)
        segrow = self.salloc(sc, "gsegrow", [1, NT * E], F32)
        echk = self.salloc(sc, "gechk", [1, NCHK], F32)
        tmpc = self.salloc(sc, "gtmpc", [1, NCHK], F32)
        tmpc2 = self.salloc(sc, "gtmpc2", [1, NCHK], F32)
        padi = self.salloc(sc, "gpadi", [1, E], I32)
        Rl = self.salloc(sc, "gRl", [P, NT * E], F32)
        pall = self.salloc(sc, "gpall", [P, NT * E], F32)
        ohb = self.salloc(sc, "gohb", [P, E], F32)
        posf2 = [self.salloc(sc, f"gposf{i}", [P, 1], F32) for i in range(2)]
        posi2 = [self.salloc(sc, f"gposi{i}", [P, 1], I32) for i in range(2)]
        tv2 = [self.salloc(sc, f"gtv{i}", [P, 1], F32) for i in range(2)]
        ts2 = [self.salloc(sc, f"gts{i}", [P, 1], F32) for i in range(2)]
        ph = Phase("gate")
        for c in range(DC):
            dma(ph, "sync", sp, xT[:, c * T_:(c + 1) * T_], self.x1fT_hbm.ap()[c])
        dma(ph, "gpsimd", gp, wg_sb[:].rearrange("p (ic e) -> p ic e", ic=DC),
            self.ext["Wg"].ap().rearrange("(ic p) e -> p ic e", p=P))
        dma(ph, "gpsimd", gp, wn_sb[:].rearrange("p (ic e) -> p ic e", ic=DC),
            self.ext["Wn"].ap().rearrange("(ic p) e -> p ic e", p=P))
        for dst, src in [(lt_sb[:], self.ltu_bf.ap()), (iota_sb[:], self.iota_p.ap()),
                         (iota8_sb[:], self.iota8.ap()), (cval_sb[:], self.cvals.ap()),
                         (big_sb[:], self.padg_c.ap()),
                         (big_sb2[:], self.pads_c.ap()),
                         (iotar[:], self.iota_pr.ap())]:
            dma(ph, "sync", sp, dst, src)
        dma(ph, "sync", sp,
            self.idxg_hbm.ap().rearrange("(p b) -> p b", p=P), big_sb[:],
            waits=[(sp, sp.n)])
        dma(ph, "sync", sp,
            self.idxs_hbm.ap().rearrange("(p b) -> p b", p=P), big_sb2[:])
        ph.op("vector", lambda v: v.memset(onescol[:], 1.0), inc=dv)
        ld_rdy = sp.n
        ph.wait("sync", sp, sp.n)
        ph.wait("gpsimd", gp, gp.n)
        g_rdy = gp.n
        with nc.psum_tensor("gpsG", [P, 512], F32) as pgG, \
             nc.psum_tensor("gpsN", [P, 512], F32) as pgN, \
             nc.psum_tensor("gps1", [P, 512], F32) as ps1, \
             nc.psum_tensor("gps2", [P, 512], F32) as ps2:
            st_aux = {}
            for tb in range(NT):
                nl = dma(ph, "sync", nzsem[tb % 2], nzb[tb % 2][:],
                         self.noise.ap()[tb * P:(tb + 1) * P, :],
                         waits=[(dv, dv.n)])
                wlg = [(sp, ld_rdy), (gp, g_rdy)] if tb == 0 else [(dv, dv.n), (ac, ac.n)]
                for ic in range(DC):
                    ph.op("tensor", lambda t, tb=tb, ic=ic: t.matmul(
                        pgG[:, 0:E],
                        xT[:, T_ * ic + P * tb:T_ * ic + P * (tb + 1)],
                        wg_sb[:, E * ic:E * (ic + 1)],
                        start=(ic == 0), stop=False),
                        waits=(wlg if ic == 0 else []), inc=None)
                ph.op("tensor", lambda t: t.matmul(
                    pgG[:, 0:E], self.sb_ones[:], self.r_bg[:],
                    start=False, stop=True), inc=pe)
                for ic in range(DC):
                    ph.op("tensor", lambda t, tb=tb, ic=ic: t.matmul(
                        pgN[:, 0:E],
                        xT[:, T_ * ic + P * tb:T_ * ic + P * (tb + 1)],
                        wn_sb[:, E * ic:E * (ic + 1)],
                        start=(ic == 0), stop=False), inc=None)
                ph.op("tensor", lambda t: t.matmul(
                    pgN[:, 0:E], self.sb_ones[:], self.r_bn[:],
                    start=False, stop=True), inc=pe)
                gmm = pe.n
                a1 = ph.op("scalar", lambda s: s.activation(
                    gt2[:], pgN[:, 0:E], AF.Abs), waits=[(pe, gmm), (dv, dv.n)],
                    inc=ac)
                a2 = ph.op("scalar", lambda s: s.activation(
                    gt2[:], gt2[:], AF.Exp, scale=self.cst[:, 1:2]),
                    waits=[(ac, a1)], inc=ac)
                a3 = ph.op("scalar", lambda s: s.activation(
                    gt2[:], gt2[:], AF.Ln, bias=1.0), waits=[(ac, a2)], inc=ac)
                r1 = ph.op("vector", lambda v: v.tensor_scalar_max(
                    gt1[:], pgN[:, 0:E], 0.0), waits=[(pe, gmm), (dv, dv.n)], inc=dv)
                s1_ = ph.op("vector", lambda v: v.tensor_add(
                    gt1[:], gt1[:], gt2[:]), waits=[(dv, r1), (ac, a3)], inc=dv)
                n1 = ph.op("vector", lambda v, tb=tb: v.tensor_mul(
                    gt1[:], gt1[:], nzb[tb % 2][:]),
                    waits=[(dv, s1_), (nzsem[tb % 2], nl)], inc=dv)
                hx1 = ph.op("vector", lambda v: v.tensor_add(
                    hx[:], pgG[:, 0:E], gt1[:]), waits=[(dv, n1)], inc=dv)
                mm1 = ph.op("vector", lambda v: v.reduce_max(
                    m1[:], hx[:], axis=AX.X), waits=[(dv, hx1)], inc=dv)
                nm1 = ph.op("vector", lambda v: v.reduce_max(
                    negm1[:], hx[:], axis=AX.X, negate=True), waits=[(dv, hx1)],
                    inc=dv)
                mp = ph.op("vector", lambda v: v.memset(m1pad[:], -1e38),
                           waits=[(dv, dv.n)], inc=dv)
                mp2 = ph.op("vector", lambda v: v.tensor_copy(
                    m1pad[:, 0:1], m1[:]), waits=[(dv, mm1), (dv, mp)], inc=dv)
                mr = ph.op("vector", lambda v: v.match_replace(
                    hrep[:], m1pad[:], hx[:], -1e30), waits=[(dv, mp2)], inc=dv)
                mm2 = ph.op("vector", lambda v: v.reduce_max(
                    m2[:], hrep[:], axis=AX.X), waits=[(dv, mr)], inc=dv)
                ii1a = ph.op("vector", lambda v: v.max_index(
                    idx8[:], m1pad[:], hx[:]), waits=[(dv, mp2), (dv, dv.n)],
                    inc=dv)
                ii1 = ph.op("vector", lambda v, tb=tb: v.tensor_copy(
                    i12[:, tb:tb + 1], idx8[:, 0:1]), waits=[(dv, ii1a)], inc=dv)
                mq = ph.op("vector", lambda v: v.memset(m2pad[:], -1e38),
                           waits=[(dv, dv.n)], inc=dv)
                mq2 = ph.op("vector", lambda v: v.tensor_copy(
                    m2pad[:, 0:1], m2[:]), waits=[(dv, mm2), (dv, mq)], inc=dv)
                ii2a = ph.op("vector", lambda v: v.max_index(
                    idx8[:], m2pad[:], hrep[:]), waits=[(dv, mq2)], inc=dv)
                ii2 = ph.op("vector", lambda v, tb=tb: v.tensor_copy(
                    i12[:, NT + tb:NT + tb + 1], idx8[:, 0:1]),
                    waits=[(dv, ii2a)], inc=dv)
                ex1 = ph.op("scalar", lambda s: s.activation(
                    e1t[:], hx[:], AF.Exp, bias=negm1[:]),
                    waits=[(dv, nm1), (dv, hx1)], inc=ac)
                g1_ = ph.op("vector", lambda v: v.tensor_scalar(
                    ge[:], hx[:], m2[:, 0:1], None, ALU.is_ge),
                    waits=[(dv, mm2)], inc=dv)
                sb1 = ph.op("vector", lambda v, tb=tb: v.tensor_copy(
                    selb[:, E * tb:E * (tb + 1)], ge[:]), waits=[(dv, g1_)], inc=dv)
                e1d = ph.op("vector", lambda v: v.tensor_mul(
                    e1m[:], e1t[:], ge[:]), waits=[(dv, g1_), (ac, ex1)], inc=dv)
                zd = ph.op("vector", lambda v: v.reduce_sum(
                    zt[:], e1m[:], axis=AX.X), waits=[(dv, e1d)], inc=dv)
                rz = ph.op("vector", lambda v, tb=tb: v.reciprocal(
                    self.recipZ[:, tb:tb + 1], zt[:]), waits=[(dv, zd)], inc=dv)
                wl_aux = [(dv, zd)]
                if tb >= 2:
                    wl_aux.append((auxsem[tb % 2], st_aux[tb - 2]))
                w1d = ph.op("vector", lambda v, tb=tb: v.reduce_max(
                    aux[tb % 2][:, 0:1], e1m[:], axis=AX.X), waits=wl_aux, inc=dv)
                w2d = ph.op("vector", lambda v, tb=tb: v.tensor_sub(
                    aux[tb % 2][:, 1:2], zt[:], aux[tb % 2][:, 0:1]),
                    waits=[(dv, w1d)], inc=dv)
                st_aux[tb] = dma(ph, "sync", auxsem[tb % 2],
                                 self.aux2_hbm.ap()[tb * P:(tb + 1) * P, :],
                                 aux[tb % 2][:], waits=[(dv, w2d)])
            t1 = ph.op("tensor", lambda t: t.matmul(
                ps1[0:1, 0:NT * E], onescol[:], selb[:], start=True, stop=True),
                waits=[(dv, dv.n)], inc=pe)
            t2 = ph.op("tensor", lambda t: t.matmul(
                ps2[:, 0:NT * E], lt_sb[:], selb[:], start=True, stop=True),
                inc=pe)
            e1_ = ph.op("vector", lambda v: v.tensor_copy(
                totals[:], ps1[0:1, 0:NT * E]), waits=[(pe, t1)], inc=dv)
            e2_ = ph.op("vector", lambda v: v.tensor_copy(
                Rl[:], ps2[:, 0:NT * E]), waits=[(pe, t2)], inc=dv)
            ph.op("vector", lambda v: v.memset(run[:], 0.0),
                  waits=[(dv, dv.n)], inc=dv)
            for tb in range(NT):
                ph.op("vector", lambda v, tb=tb: v.tensor_copy(
                    excl[:, E * tb:E * (tb + 1)], run[:]),
                    waits=[(dv, dv.n)], inc=dv)
                ph.op("vector", lambda v, tb=tb: v.tensor_add(
                    run[:], run[:], totals[:, E * tb:E * (tb + 1)]),
                    waits=[(dv, dv.n)], inc=dv)
            ph.op("vector", lambda v: v.tensor_copy(nrow[:], run[:]),
                  waits=[(dv, dv.n)], inc=dv)
            ph.op("vector", lambda v: v.tensor_scalar_add(pad[:], nrow[:], 511.0),
                  waits=[(dv, dv.n)], inc=dv)
            ph.op("vector", lambda v: v.tensor_copy(padi[:], pad[:]),
                  waits=[(dv, dv.n)], inc=dv)
            ph.op("vector", lambda v: v.tensor_scalar(
                padi[:], padi[:], 9, None, ALU.arith_shift_right),
                waits=[(dv, dv.n)], inc=dv)
            ph.op("vector", lambda v: v.tensor_scalar(
                padi[:], padi[:], 9, None, ALU.logical_shift_left),
                waits=[(dv, dv.n)], inc=dv)
            ph.op("vector", lambda v: v.tensor_copy(pad[:], padi[:]),
                  waits=[(dv, dv.n)], inc=dv)
            ph.op("vector", lambda v: v.memset(run[:], 0.0),
                  waits=[(dv, dv.n)], inc=dv)
            for e_ in range(E):
                ph.op("vector", lambda v, e_=e_: v.tensor_copy(
                    base[:, e_:e_ + 1], run[:, e_:e_ + 1] if False
                    else run[:, 0:1]), waits=[(dv, dv.n)], inc=dv)
                ph.op("vector", lambda v, e_=e_: v.tensor_add(
                    run[:, 0:1], run[:, 0:1], pad[:, e_:e_ + 1]),
                    waits=[(dv, dv.n)], inc=dv)
            ph.op("vector", lambda v: v.tensor_add(endr[:], base[:], pad[:]),
                  waits=[(dv, dv.n)], inc=dv)
            ph.op("vector", lambda v: v.memset(brep[:], 0.0),
                  waits=[(dv, dv.n)], inc=dv)
            for e_ in range(E):
                ph.op("vector", lambda v, e_=e_: v.tensor_scalar_add(
                    brep[0:1, :].rearrange("o (t e) -> o t e", e=E)[:, :, e_:e_ + 1],
                    brep[0:1, :].rearrange("o (t e) -> o t e", e=E)[:, :, e_:e_ + 1],
                    base[:, e_:e_ + 1]), waits=[(dv, dv.n)], inc=dv)
            ph.op("vector", lambda v: v.tensor_add(segrow[:], excl[:], brep[:]),
                  waits=[(dv, dv.n)], inc=dv)
            sgd = ph.op("vector", lambda v: v.memset(echk[:], 8.0),
                        waits=[(dv, dv.n)], inc=dv)
            for e_ in range(E):
                ph.op("vector", lambda v, e_=e_: v.tensor_scalar(
                    tmpc[:], cval_sb[:], base[:, e_:e_ + 1], None, ALU.is_ge),
                    waits=[(dv, dv.n)], inc=dv)
                ph.op("vector", lambda v, e_=e_: v.tensor_scalar(
                    tmpc2[:], cval_sb[:], endr[:, e_:e_ + 1], None, ALU.is_lt),
                    waits=[(dv, dv.n)], inc=dv)
                ph.op("vector", lambda v, e_=e_: v.tensor_mul(
                    tmpc[:], tmpc[:], tmpc2[:]), waits=[(dv, dv.n)], inc=dv)
                ph.op("vector", lambda v, e_=e_: v.scalar_tensor_tensor(
                    echk[:], tmpc[:], float(e_ - 8), echk[:], ALU.mult, ALU.add),
                    waits=[(dv, dv.n)], inc=dv)
            ph.op("vector", lambda v: v.tensor_scalar_min(
                echk[:], echk[:], 7.0), waits=[(dv, dv.n)], inc=dv)
            ecd = ph.op("vector", lambda v: v.tensor_copy(self.idxE_i[:], echk[:]),
                        waits=[(dv, dv.n)], inc=dv)
            b1_ = ph.op("tensor", lambda t: t.matmul(
                ps1[:, 0:NT * E], self.sb_ones32[:], segrow[:],
                start=True, stop=True), waits=[(dv, dv.n), (dv, e1_)], inc=pe)
            pa = ph.op("vector", lambda v: v.tensor_add(
                pall[:], ps1[:, 0:NT * E], Rl[:]),
                waits=[(pe, b1_), (dv, e2_), (dv, dv.n)], inc=dv)
            b2_ = ph.op("tensor", lambda t: t.matmul(
                ps2[:, 0:NCHK], self.sb_ones32[:], echk[:],
                start=True, stop=True), waits=[(dv, dv.n)], inc=pe)
            ph.op("vector", lambda v: v.tensor_scalar(
                self._idxWf_sb[:], ps2[:, 0:NCHK], 128.0, None, ALU.mult),
                waits=[(pe, b2_), (dv, dv.n)], inc=dv)
            ph.op("vector", lambda v: v.tensor_add(
                self._idxWf_sb[:], self._idxWf_sb[:], iotar[:]),
                waits=[(dv, dv.n)], inc=dv)
            iwd = ph.op("vector", lambda v: v.tensor_copy(
                self.idxW_i[:], self._idxWf_sb[:]), waits=[(dv, dv.n)], inc=dv)
            sc_done = {}
            idxg_v = self.idxg_hbm.ap().rearrange("(n o) -> n o", o=1)
            idxs_v = self.idxs_hbm.ap().rearrange("(n o) -> n o", o=1)
            for tb in range(NT):
                for sl in range(2):
                    it = tb * 2 + sl
                    wl0 = [(dv, dv.n)]
                    if it >= 2:
                        wl0.append((gp, gp.n))
                    pr = it % 2
                    oh = ph.op("vector", lambda v, tb=tb, sl=sl: v.tensor_scalar(
                        ohb[:], iota8_sb[:], i12[:, NT * sl + tb:NT * sl + tb + 1],
                        None, ALU.is_equal), waits=wl0, inc=dv)
                    ph.op("vector", lambda v, tb=tb: v.tensor_mul(
                        ohb[:], ohb[:], pall[:, E * tb:E * (tb + 1)]),
                        waits=[(dv, oh), (dv, pa)], inc=dv)
                    ph.op("vector", lambda v, pr=pr: v.reduce_sum(
                        posf2[pr][:], ohb[:], axis=AX.X), waits=[(dv, dv.n)], inc=dv)
                    ph.op("vector", lambda v, pr=pr: v.tensor_copy(
                        posi2[pr][:], posf2[pr][:]), waits=[(dv, dv.n)], inc=dv)
                    pid = ph.op("vector", lambda v, tb=tb, sl=sl, pr=pr:
                                v.tensor_scalar_add(
                        tv2[pr][:], iota_sb[:], float(128 * tb)),
                        waits=[(dv, dv.n)], inc=dv)
                    tsd = ph.op("vector", lambda v, sl=sl, pr=pr:
                                v.tensor_scalar_add(
                        ts2[pr][:], tv2[pr][:], float(self.T * sl)),
                        waits=[(dv, pid)], inc=dv)
                    dma_i = ph.op("gpsimd", lambda g, pr=pr: g.indirect_dma_start(
                        idxg_v, IndirectOffsetOnAxis(posi2[pr][:], 0),
                        tv2[pr][:], None),
                        waits=[(dv, tsd)], inc=gp, inc_n=16)
                    sc_done[it] = ph.op("gpsimd", lambda g, pr=pr:
                                        g.indirect_dma_start(
                        idxs_v, IndirectOffsetOnAxis(posi2[pr][:], 0),
                        ts2[pr][:], None),
                        waits=[], inc=gp, inc_n=16)
            ph.wait("gpsimd", gp, gp.n)
            ph.wait("sync", sp, sp.n)
            for j in range(2):
                ph.wait("sync", auxsem[j], auxsem[j].n)
                ph.wait("sync", nzsem[j], nzsem[j].n)
            ph.realize(nc)

    def phase_moe_sparse(self, sc):
        nc = self.nc
        sp, gp, pe, ac, dv = (self.sem[k] for k in ("sp", "gp", "pe", "ac", "dv"))
        NT, T_, NCHK = self.NT, self.T, self.NCHK
        w1sb = self.salloc(sc, "sw1", [P, DC * FF], BF16)
        w2sb = self.salloc(sc, "sw2", [P, FFC * D], BF16)
        h_sb = self.salloc(sc, "sh", [P, FFC * 512], BF16)
        xg = [self.salloc(sc, f"sxg{i}", [P, D], BF16) for i in range(4)]
        xTs = self.salloc(sc, "sxT", [P, DC * 512], BF16)
        idg = [self.salloc(sc, f"sidg{i}", [P, 4], F32) for i in range(2)]
        ids = [self.salloc(sc, f"sids{i}", [P, 4], F32) for i in range(2)]
        idgi = [self.salloc(sc, f"sidgi{i}", [P, 1], I32) for i in range(4)]
        idsi = [[self.salloc(sc, f"sidsi{j}_{i}", [P, 1], I32) for i in range(4)]
                for j in range(2)]
        idai = [self.salloc(sc, f"sidai{i}", [P, 1], I32) for i in range(4)]
        tmpa = self.salloc(sc, "stmpa", [P, 1], F32)
        tmpb = self.salloc(sc, "stmpb", [P, 1], F32)
        b1g = [self.salloc(sc, f"sb1g{i}", [P, FFC], F32) for i in range(2)]
        idxB = self.salloc(sc, "sidxB", [2, 1], I32)
        idxBf = self.salloc(sc, "sidxBf", [2, 1], F32)
        iota2 = self.salloc(sc, "siota2", [P, 1], F32)
        b2rb = [self.salloc(sc, f"sb2rb{i}", [2, D], BF16) for i in range(2)]
        auxw = [self.salloc(sc, f"sauxw{i}", [P, 4], F32) for i in range(2)]
        osc = [self.salloc(sc, f"sosc{i}", [P, D], BF16) for i in range(4)]
        idsem = [self.S(f"sid{i}") for i in range(2)]
        gq = self.S("gq_w2")
        gx = self.S("gx_rows")
        wgsem = [self.S(f"swg{i}") for i in range(2)]
        xgsem = [self.S(f"sxg{i}") for i in range(2)]
        axsem = [self.S(f"sax{i}") for i in range(2)]
        scsem = [self.S(f"ssc{i}") for i in range(4)]
        w1v = self.w1bf.ap().rearrange("e p f -> (e p) f")
        w2v = self.w2bf.ap().rearrange("e p f -> (e p) f")
        b1v = self.ext["b1p"].ap().rearrange("e p f -> (e p) f")
        aux_v = self.aux2_hbm.ap().rearrange("t a -> (t a) 1" if False
                                             else "t a -> (t a)").rearrange(
                                                 "(n o) -> n o", o=1)
        ph = Phase("moeS")
        dma(ph, "sync", sp, iota2[:], self.iota_p.ap())
        iota_rdy = sp.n
        ph.wait("sync", sp, sp.n)

        with nc.psum_tensor("spst0", [P, P], BF16) as pt0, \
             nc.psum_tensor("spst1", [P, P], BF16) as pt1, \
             nc.psum_tensor("spsh0", [P, 512], F32) as h0, \
             nc.psum_tensor("spsh1", [P, 512], F32) as h1, \
             nc.psum_tensor("spso0", [P, 512], F32) as o0, \
             nc.psum_tensor("spso1", [P, 512], F32) as o1:
            pst, psH, psO = [pt0, pt1], [h0, h1], [o0, o1]
            w1_use = w2_use = 0
            idx_use = {}
            sc_st = {}
            self._gx_tgt, self._ga_tgt, self._sc_tgt = {}, {}, {}
            self._pst_free = {}
            self._idg_used = {}
            self._tp_done = {}
            self._pend_sc = []
            for c in range(NCHK):
                cp = c % 2
                wl = [(dv, self._idg_used.get(c - 2, 0))]
                for tbi in range(4):
                    dma(ph, "sync", idsem[cp], idg[cp][:, tbi:tbi + 1],
                        self.idxg_hbm.ap()[512 * c + 128 * tbi:
                                           512 * c + 128 * (tbi + 1)]
                        .rearrange("(p o) -> p o", o=1),
                        waits=(wl if tbi == 0 else []))
                for tbi in range(4):
                    dma(ph, "sync", idsem[cp], ids[cp][:, tbi:tbi + 1],
                        self.idxs_hbm.ap()[512 * c + 128 * tbi:
                                           512 * c + 128 * (tbi + 1)]
                        .rearrange("(p o) -> p o", o=1))
                id_tgt = idsem[cp].n
                tp_ev = {}
                for tbi in range(4):
                    c0 = ph.op("vector", lambda v, cp=cp, tbi=tbi: v.tensor_copy(
                        idgi[tbi][:], idg[cp][:, tbi:tbi + 1]),
                        waits=[(idsem[cp], id_tgt), (dv, dv.n),
                               (gx, gx.n)], inc=dv)
                    c1 = ph.op("vector", lambda v, cp=cp, tbi=tbi: v.tensor_copy(
                        idsi[cp][tbi][:], ids[cp][:, tbi:tbi + 1]),
                        waits=[(dv, dv.n),
                               (scsem[tbi], self._sc_tgt.get((c - 2, tbi), 0))],
                        inc=dv)
                    self._c1t = getattr(self, "_c1t", {})
                    self._c1t[(c, tbi)] = c1
                    d1 = ph.op("vector", lambda v, cp=cp, tbi=tbi: v.tensor_sub(
                        tmpa[:], ids[cp][:, tbi:tbi + 1], idg[cp][:, tbi:tbi + 1]),
                        waits=[(dv, dv.n)], inc=dv)
                    d2 = ph.op("vector", lambda v, tbi=tbi: v.tensor_scalar(
                        tmpb[:], tmpa[:], 1.0 / self.T, None, ALU.mult),
                        waits=[(dv, d1)], inc=dv)
                    d3 = ph.op("vector", lambda v, cp=cp, tbi=tbi:
                               v.scalar_tensor_tensor(
                        tmpa[:], idg[cp][:, tbi:tbi + 1], 2.0, tmpb[:],
                        ALU.mult, ALU.add), waits=[(dv, d2)], inc=dv)
                    d4 = ph.op("vector", lambda v, tbi=tbi: v.tensor_copy(
                        idai[tbi][:], tmpa[:]),
                        waits=[(dv, d3), (gx, gx.n)], inc=dv)
                    gxt = ph.op("gpsimd", lambda g, cp=cp, tbi=tbi:
                               g.indirect_dma_start(
                        xg[tbi][:], None, self.x1fbf.ap(),
                        IndirectOffsetOnAxis(idgi[tbi][:], 0)),
                        waits=[(dv, c0),
                               (pe, self._tp_done.get((c - 1, tbi), 0))],
                        inc=gx, inc_n=16)
                    self._gx_tgt[(c, tbi)] = gxt
                    wla = [(dv, d4)]
                    if tbi == 0:
                        wla.append((ac, sc_st.get((c - 2, "aux"), 0)))
                    ga = ph.op("gpsimd", lambda g, cp=cp, tbi=tbi:
                               g.indirect_dma_start(
                        auxw[cp][:, tbi:tbi + 1], None, aux_v,
                        IndirectOffsetOnAxis(idai[tbi][:], 0)),
                        waits=wla, inc=gx, inc_n=16)
                    self._ga_tgt[(c, tbi)] = ga
                    gx_t = gx.n
                    for ic in range(DC):
                        wlt = [(gx, gx_t)] if ic == 0 else []
                        if (ic % 2) in self._pst_free:
                            wlt = wlt + [self._pst_free[ic % 2]]
                        ph.op("tensor", lambda t, tbi=tbi, ic=ic: t.transpose(
                            pst[ic % 2][:], xg[tbi][:, P * ic:P * (ic + 1)],
                            self.sb_ident[:]), waits=wlt, inc=pe)
                        ev = ph.op("scalar", lambda s2, tbi=tbi, ic=ic: s2.copy(
                            xTs[:, 512 * ic + P * tbi:512 * ic + P * (tbi + 1)],
                            pst[ic % 2][:]), waits=[(pe, pe.n)], inc=ac)
                        self._pst_free[ic % 2] = (ac, ev)
                    tp_ev[(c, tbi)] = pe.n
                    self._tp_done[(c, tbi)] = pe.n
                wg1 = ph.op("gpsimd", lambda g, c=c: g.indirect_dma_start(
                    w1sb[:], None, w1v, IndirectOffsetOnAxis(
                        self.idxW_i[:, c:c + 1], 0)),
                    waits=[(pe, w1_use)], inc=gp, inc_n=16)
                bg1 = ph.op("gpsimd", lambda g, c=c, cp=cp: g.indirect_dma_start(
                    b1g[cp][:], None, b1v, IndirectOffsetOnAxis(
                        self.idxW_i[:, c:c + 1], 0)),
                    waits=[(ac, sc_st.get((c - 2, "b1"), 0))], inc=gp, inc_n=16)
                bi1 = ph.op("vector", lambda v, c=c: v.tensor_sub(
                    idxBf[:], self._idxWf_sb[0:2, c:c + 1], iota2[0:2, :]),
                    waits=[(dv, dv.n), (sp, iota_rdy), (gp, gp.n)], inc=dv)
                bi2 = ph.op("vector", lambda v: v.tensor_scalar(
                    idxBf[:], idxBf[:], 1.0 / 128.0, None, ALU.mult),
                    waits=[(dv, bi1)], inc=dv)
                bi3 = ph.op("vector", lambda v: v.tensor_copy(
                    idxB[:], idxBf[:]), waits=[(dv, bi2)], inc=dv)
                bg2 = ph.op("gpsimd", lambda g, c=c, cp=cp: g.indirect_dma_start(
                    b2rb[cp][:], None, self.ext["b2"].ap(), IndirectOffsetOnAxis(
                        idxB[:], 0)),
                    waits=[(dv, bi3), (pe, sc_st.get((c - 2, "b2use"), 0))],
                    inc=gp, inc_n=16)
                for fgo in self._pend_sc:
                    fgo()
                self._pend_sc = []
                wg2 = ph.op("gpsimd", lambda g, c=c: g.indirect_dma_start(
                    w2sb[:], None, w2v, IndirectOffsetOnAxis(
                        self.idxW_i[:, c:c + 1], 0)),
                    waits=[(pe, w2_use)], inc=gq, inc_n=16)
                gp_w = gp.n
                self._idg_used[c] = dv.n
                gp_now = gp.n
                gx_now = gx.n
                xT_rdy = ac.n
                hev = {}
                for f in range(FFC):
                    wlf = []
                    if f == 0:
                        wlf = [(gp, gp_now), (ac, xT_rdy)]
                    if f >= 2:
                        wlf.append((ac, hev[f - 2]))
                    for ic in range(DC):
                        ph.op("tensor", lambda t, f=f, ic=ic: t.matmul(
                            psH[f % 2][:, 0:512],
                            w1sb[:, FF * ic + P * f:FF * ic + P * (f + 1)],
                            xTs[:, 512 * ic:512 * (ic + 1)],
                            start=(ic == 0), stop=(ic == DC - 1)),
                            waits=(wlf if ic == 0 else []),
                            inc=(pe if ic == DC - 1 else None))
                    mmt = pe.n
                    wl2 = [(pe, mmt)]
                    if f == 0:
                        wl2.append((gp, gp_now))
                    hev[f] = ph.op("scalar", lambda s2, f=f, cp=cp: s2.activation(
                        h_sb[:, 512 * f:512 * (f + 1)], psH[f % 2][:, 0:512],
                        AF.Relu, bias=b1g[cp][:, f:f + 1]),
                        waits=wl2, inc=ac)
                w1_use = pe.n
                h_done = hev[FFC - 1]
                sc_st[(c, "b1")] = h_done
                sttv = {}
                for tbi in range(4):
                    for dh in range(2):
                        gi = tbi * 2 + dh
                        wl3 = []
                        if gi == 0:
                            wl3 = [(ac, h_done), (gq, gq.n)]
                        if gi >= 2:
                            wl3.append((ac, sttv[gi - 2]))
                        for f in range(FFC):
                            ph.op("tensor", lambda t, tbi=tbi, dh=dh, f=f, gi=gi:
                                  t.matmul(
                                psO[gi % 2][:, 0:512],
                                h_sb[:, 512 * f + P * tbi:512 * f + P * (tbi + 1)],
                                w2sb[:, D * f + 512 * dh:D * f + 512 * (dh + 1)],
                                start=(f == 0), stop=False),
                                waits=(wl3 if f == 0 else []), inc=None)
                        ph.op("tensor", lambda t, dh=dh, gi=gi, cp=cp: t.matmul(
                            psO[gi % 2][:, 0:512], self.sb_ones[:],
                            b2rb[cp][0:1, 512 * dh:512 * (dh + 1)],
                            start=False, stop=True),
                            waits=[(gp, gp_w)] if gi == 0 else [],
                            inc=pe)
                        mmt = pe.n
                        wl4 = [(pe, mmt), (gx, gx_now)]
                        if dh == 0:
                            wl4.append((scsem[tbi],
                                        self._sc_tgt.get((c - 1, tbi), 0)))
                        sttv[gi] = ph.op(
                            "scalar", lambda s2, tbi=tbi, dh=dh, gi=gi, cp=cp:
                            s2.activation(
                                osc[tbi][:, 512 * dh:512 * (dh + 1)],
                                psO[gi % 2][:, 0:512], AF.Identity,
                                scale=auxw[cp][:, tbi:tbi + 1]),
                            waits=wl4, inc=ac)
                    def mk_scatter(c=c, cp=cp, tbi=tbi, ev=sttv[tbi * 2 + 1],
                                   c1e=self._c1t[(c, tbi)]):
                        def go():
                            self._sc_tgt[(c, tbi)] = ph.op(
                                "gpsimd", lambda g: g.indirect_dma_start(
                                    self.moe2_hbm.ap(),
                                    IndirectOffsetOnAxis(idsi[cp][tbi][:], 0),
                                    osc[tbi][:], None),
                                waits=[(ac, ev), (dv, c1e)],
                                inc=scsem[tbi], inc_n=16)
                        return go
                    self._pend_sc.append(mk_scatter())
                w2_use = pe.n
                sc_st[(c, "aux")] = sttv[7]
                sc_st[(c, "b2use")] = pe.n
                idx_use[c] = pe.n
            for fgo in self._pend_sc:
                fgo()
            self._pend_sc = []
            ph.wait("gpsimd", gp, gp.n)
            ph.wait("gpsimd", gq, gq.n)
            ph.wait("gpsimd", gx, gx.n)
            for j in range(4):
                ph.wait("gpsimd", scsem[j], scsem[j].n)
            for j in range(2):
                ph.wait("sync", idsem[j], idsem[j].n)
            ph.realize(nc)

    def phase_final(self, sc):
        nc = self.nc
        sp, gp, pe, ac, dv = (self.sem[k] for k in ("sp", "gp", "pe", "ac", "dv"))
        NT = self.NT
        mo = [self.salloc(sc, f"fmo{i}", [P, D], BF16) for i in range(2)]
        mo2 = [self.salloc(sc, f"fmo2{i}", [P, D], BF16) for i in range(2)]
        fbm = self.salloc(sc, "fbm", [P, D], F32)
        x1fe = [self.salloc(sc, f"x1fe{i}", [P, D], F32) for i in range(2)]
        x1e = [self.salloc(sc, f"x1e{i}", [P, D], F32) for i in range(2)]
        fb = self.salloc(sc, "fb", [P, D], F32)
        yb = [self.salloc(sc, f"yb{i}", [P, D], F32) for i in range(2)]
        sqm = self.salloc(sc, "sqm", [P, D], F32)
        statm = [self.salloc(sc, f"mst{i}", [P, 8], F32) for i in range(4)]
        g2bc = self.salloc(sc, "g2bcl", [P, D], BF16)
        be2bc = self.salloc(sc, "be2bcl", [P, D], BF16)
        mosem = [self.S(f"fmo{i}") for i in range(2)]
        x1fesem = [self.S(f"x1fe{i}") for i in range(2)]
        x1esem = [self.S(f"x1e{i}") for i in range(2)]
        ysem = [self.S(f"yst{i}") for i in range(2)]
        ph = Phase("final")
        dma(ph, "sync", sp, g2bc[:], self.gb_hbm.ap()[2 * P:3 * P, :])
        dma(ph, "sync", sp, be2bc[:], self.gb_hbm.ap()[3 * P:4 * P, :])
        ph.wait("sync", sp, sp.n)
        ystore, chend = {}, {}
        for tb in range(NT):
            w = [(dv, chend[tb - 2])] if tb >= 2 else []
            mot = dma(ph, "sync", mosem[tb % 2], mo[tb % 2][:],
                      self.moe2_hbm.ap()[tb * P:(tb + 1) * P, :], waits=w)
            mot = dma(ph, "scalar", mosem[tb % 2], mo2[tb % 2][:],
                      self.moe2_hbm.ap()[self.T + tb * P:self.T + (tb + 1) * P, :],
                      waits=w)
            xft = dma(ph, "gpsimd", x1fesem[tb % 2], x1fe[tb % 2][:],
                      self.x1ff32.ap()[tb * P:(tb + 1) * P, :], waits=w)
            x1t = dma(ph, "scalar", x1esem[tb % 2], x1e[tb % 2][:],
                      self.x1f32.ap()[tb * P:(tb + 1) * P, :])
            ad = ph.op("vector", lambda v, tb=tb: v.tensor_add(
                fbm[:], mo[tb % 2][:], mo2[tb % 2][:]),
                waits=[(dv, dv.n), (mosem[tb % 2], mot)], inc=dv)
            sd = ph.op("vector", lambda v, tb=tb: v.scalar_tensor_tensor(
                fb[:], fbm[:], self.recipZ[:, tb:tb + 1], x1fe[tb % 2][:],
                ALU.mult, ALU.add),
                waits=[(dv, ad), (x1fesem[tb % 2], xft)], inc=dv)
            self._ln_tile(ph, src=fb[:], pre_wait=[(dv, sd)],
                          sq=sqm, stat=statm[tb % 2], out=fb[:],
                          out_guard=(dv, dv.n))
            g2d = ph.op("vector", lambda v: v.tensor_mul(
                fb[:], fb[:], g2bc[:]), waits=[(dv, dv.n)], inc=dv)
            b2d = ph.op("vector", lambda v: v.tensor_add(
                fb[:], fb[:], be2bc[:]), waits=[(dv, g2d)], inc=dv)
            self._ln_tile(ph, src=fb[:], pre_wait=[(dv, b2d)],
                          sq=sqm, stat=statm[2 + tb % 2], out=fb[:],
                          out_guard=(dv, dv.n))
            yd = ph.op("vector", lambda v, tb=tb: v.tensor_add(
                yb[tb % 2][:], x1e[tb % 2][:], fb[:]),
                waits=[(dv, dv.n), (x1esem[tb % 2], x1t),
                       (ysem[tb % 2], ystore.get(tb - 2, 0))], inc=dv)
            chend[tb] = dv.n
            ystore[tb] = dma(ph, "sync", ysem[tb % 2],
                             self.y.ap()[tb * P:(tb + 1) * P, :],
                             yb[tb % 2][:], waits=[(dv, yd)])
        for j in range(2):
            ph.wait("sync", ysem[j], ysem[j].n)
        ph.realize(nc)

    def _pc_done_tgt(self):
        return self.sem["gp"].n


def kernel(**inputs):
    T = 4096
    b = Builder(T)
    nc = b.build()
    names = {"x": None}
    base = {
        "aWq": inputs["L0_attn_Wq"], "abq": inputs["L0_attn_bq"],
        "aWk": inputs["L0_attn_Wk"], "abk": inputs["L0_attn_bk"],
        "aWv": inputs["L0_attn_Wv"], "abv": inputs["L0_attn_bv"],
        "aWo": inputs["L0_attn_Wo"], "abo": inputs["L0_attn_bo"],
        "fWdkv": inputs["L0_ffn_Wdkv"], "fbdkv": inputs["L0_ffn_bdkv"],
        "fWuk": inputs["L0_ffn_Wuk"], "fbuk": inputs["L0_ffn_buk"],
        "fWuv": inputs["L0_ffn_Wuv"], "fbuv": inputs["L0_ffn_buv"],
        "fWdq": inputs["L0_ffn_Wdq"], "fbdq": inputs["L0_ffn_bdq"],
        "fWuq": inputs["L0_ffn_Wuq"], "fbuq": inputs["L0_ffn_buq"],
        "fWo": inputs["L0_ffn_Wo"], "fbo": inputs["L0_ffn_bo"],
        "g1": inputs["L0_ffn_gamma1"], "be1": inputs["L0_ffn_beta1"],
        "g2": inputs["L0_ffn_gamma2"], "be2": inputs["L0_ffn_beta2"],
        "Wg": inputs["L0_ffn_Wg"], "bg": inputs["L0_ffn_bg"],
        "Wn": inputs["L0_ffn_Wn"], "bn": inputs["L0_ffn_bn"],
        "W1t": np.asarray(inputs["L0_ffn_W1"], dtype=np.float32).reshape(
            E, DC, P, FF).transpose(0, 2, 1, 3).reshape(E, P, DC * FF),
        "b1p": np.asarray(inputs["L0_ffn_b1"], dtype=np.float32).reshape(
            E, FFC, P).transpose(0, 2, 1),
        "W2t": np.asarray(inputs["L0_ffn_W2"], dtype=np.float32).reshape(
            E, FFC, P, D).transpose(0, 2, 1, 3).reshape(E, P, FFC * D),
        "b2": inputs["L0_ffn_b2"],
    }
    base = {k: np.ascontiguousarray(np.asarray(v, dtype=np.float32))
            for k, v in base.items()}
    import jax
    import jax.numpy as jnp
    with jax.default_device(jax.devices("cpu")[0]):
        key = jax.random.wrap_key_data(jnp.array([0, 42], dtype=jnp.uint32),
                                       impl="threefry2x32")
        noise = np.asarray(jax.random.normal(key, (8, T, E), dtype=np.float32))
    xs = np.ascontiguousarray(np.asarray(inputs["L0_attn_hidden_states"],
                                         dtype=np.float32))
    in_maps = []
    for c in range(8):
        m = dict(base)
        m["x"] = xs[c]
        m["noise"] = np.ascontiguousarray(noise[c])
        in_maps.append(m)
    res = run_bass_kernel_spmd(nc, in_maps, core_ids=list(range(8)))
    return np.stack([np.asarray(r["y"]) for r in res.results], axis=0)
